# version 02
# baseline (speedup 1.0000x reference)
import numpy as np
import ml_dtypes
from contextlib import ExitStack

import concourse.bass as bass
import concourse.mybir as mybir
from concourse.bass_utils import run_bass_kernel_spmd

F32 = mybir.dt.float32
BF16 = mybir.dt.bfloat16
AF = mybir.ActivationFunctionType
ALU = mybir.AluOpType

NCORE = 8
BPC = 2
S = 2048
D = 1024
NCH = D // 128
NTT = S // 128
H = 16
DH = 64
L0_SHIFT = 16.0
LN_EPS = 1e-5

NOW = None

_CC = {}


def host_constants():
    if _CC:
        return _CC
    identf = np.eye(128, dtype=np.float32)
    identb = np.eye(128).astype(ml_dtypes.bfloat16)
    inv_freq = 1.0 / (10000.0 ** (np.arange(0, DH, 2, dtype=np.float64) / DH))
    ang = np.arange(S, dtype=np.float64)[:, None] * inv_freq[None, :]
    cos64 = np.concatenate([np.cos(ang), np.cos(ang)], axis=1)
    sin64 = np.concatenate([-np.sin(ang), np.sin(ang)], axis=1)
    _CC.update(identf=identf, identb=identb,
               cos64=np.ascontiguousarray(cos64.astype(np.float32)),
               sin64=np.ascontiguousarray(sin64.astype(np.float32)))
    return _CC


class Sem:
    __slots__ = ("h", "n", "name")

    def __init__(self, ctx, nc, name):
        self.h = ctx.enter_context(nc.semaphore(name))
        self.n = 0
        self.name = name


FREE0 = ()


class Prog:
    ENGINES = ("sync", "tensor", "vector", "scalar", "gpsimd")

    def __init__(self, nc):
        self.nc = nc
        self.q = {e: [] for e in self.ENGINES}

    def on(self, eng, fn, wait=(), inc=None, inc_n=1, drain=False,
           fence=False):
        w = []
        for rec in wait:
            if not rec:
                continue
            s, v = rec
            if v is NOW:
                v = s.n
            if v > 0:
                w.append((s, v))
        if fence and inc is not None:
            fb = self.fence
            self.q[eng].append((tuple(w), fn, None, 0))
            self.q[eng].append(((), lambda e: e.matmul(
                fb[0], fb[1], fb[2], start=True, stop=True), None, 0))
            self.q[eng].append(((), lambda e: e.matmul(
                fb[0], fb[1], fb[2], start=True, stop=True), inc, inc_n))
        elif drain and inc is not None:
            self.q[eng].append((tuple(w), fn, None, 0))
            self.q[eng].append(((), lambda e: e.drain(), inc, inc_n))
        else:
            self.q[eng].append((tuple(w), fn, inc, inc_n))
        if inc is not None:
            inc.n += inc_n
            return (inc, inc.n)
        return None

    def dma(self, fn, wait=(), inc=None):
        return self.on("sync", fn, wait=wait, inc=inc, inc_n=16)

    def block(self, name):
        qs = self.q
        self.q = {e: [] for e in self.ENGINES}
        with self.nc.Block(name) as blk:
            for eng_name in self.ENGINES:
                entries = qs[eng_name]
                if not entries:
                    continue

                def body(eng, entries=entries):
                    for waits, fn, inc, inc_n in entries:
                        for sem, val in waits:
                            eng.wait_ge(sem.h, val)
                        instr = fn(eng)
                        if inc is not None:
                            instr.then_inc(inc.h, inc_n)

                getattr(blk, eng_name)(body)


def fmc(c, t0=0, n=S):
    return slice(c * S + t0, c * S + t0 + n)


class NCProxy:

    def __init__(self, nc, B):
        self._nc = nc
        self._B = B

    def __getattr__(self, k):
        return getattr(self._nc, k)

    def dram_tensor(self, name, shape, dtype, kind="Internal"):
        dm = self._B.dram_map
        if name not in dm:
            dm[name] = self._nc.dram_tensor(name, shape, dtype, kind=kind)
        return dm[name]

    def sbuf_tensor(self, name, shape, dtype):
        return self._nc.sbuf_tensor(self._B.prefix + name, shape, dtype)

    def psum_tensor(self, name, shape, dtype):
        return self._nc.psum_tensor(self._B.prefix + name, shape, dtype)

    def Block(self, name=None):
        return self._nc.Block(self._B.prefix + name if name else name)


class Builder:
    def __init__(self):
        self._stack = ExitStack()
        self.nc = bass.Bass("TRN2", target_bir_lowering=False)
        self._nc_real = self.nc
        self.ctx = self._stack
        self.alloc_ctx = self._stack
        self.P = Prog(self.nc)
        self.sems = {}
        self.state = {}
        self.prefix = ""
        self.dram_map = {}
        self._phase_stack = None

    def begin_phase(self, prefix):
        self.prefix = prefix
        self._phase_stack = ExitStack()
        self.alloc_ctx = self._phase_stack
        self.nc = NCProxy(self._nc_real, self)
        self.P.nc = self.nc

    def end_phase(self):
        self._phase_stack.close()
        self._phase_stack = None
        self.alloc_ctx = self._stack
        self.prefix = ""
        self.nc = self._nc_real
        self.P.nc = self.nc

    def sem(self, name):
        if name not in self.sems:
            self.sems[name] = Sem(self.ctx, self._nc_real, name)
        return self.sems[name]

    def sb(self, name, shape, dtype):
        return self.alloc_ctx.enter_context(
            self._nc_real.sbuf_tensor(self.prefix + name, shape, dtype))

    def ps(self, name, shape, dtype):
        return self.alloc_ctx.enter_context(
            self._nc_real.psum_tensor(self.prefix + name, shape, dtype))

    def make_fence(self, pF, ones_col, identf):
        self.P.fence = (pF[0:1, 0:128], ones_col[0:1, 0:1], identf[0:1, 0:128])

    def make_bias_consts(self, values):
        self.bias_tile = self.sb("biasconsts", [128, max(1, len(values))], F32)
        self.bias_map = {}
        for i, v in enumerate(values):
            self.bias_map[float(v)] = i
            self.P.on("vector", lambda e, i=i, v=v: e.memset(
                self.bias_tile[:, i:i + 1], float(v)))

    def bias(self, v):
        return self.bias_tile[:, self.bias_map[float(v)]:self.bias_map[float(v)] + 1]

    def finish(self):
        self._stack.close()
        return self.nc



def emit_load_x_fm(B, x_dram_seq, xfm, xstage, pA, pB, identf):
    P = B.P
    s_ld, s_tp, s_ev = B.sem("ld"), B.sem("tp"), B.sem("ev")
    st_free = [FREE0, FREE0]
    pp_free = [FREE0, FREE0]
    for tt in range(NTT):
        st = xstage[tt % 2]
        ld_f = P.dma(lambda e, st=st, tt=tt: e.dma_start(
            out=st[:, :], in_=x_dram_seq[tt * 128:(tt + 1) * 128, :]),
            wait=[st_free[tt % 2]], inc=s_ld)
        pp = pA if tt % 2 == 0 else pB
        tp_f = None
        for c in range(NCH):
            r = P.on("tensor", lambda e, st=st, pp=pp, c=c:
                     e.transpose(pp[:, c * 128:(c + 1) * 128],
                                 st[:, c * 128:(c + 1) * 128], identf[:, :]),
                     wait=[ld_f, pp_free[tt % 2]] if c == 0 else [],
                     inc=s_tp if c == NCH - 1 else None,
                     fence=(c == NCH - 1))
            tp_f = r or tp_f
        st_free[tt % 2] = tp_f
        pp_free[tt % 2] = P.on("scalar", lambda e, pp=pp, tt=tt: e.activation(
            out=xfm[:, :].rearrange("p (c s) -> p c s", c=NCH)[
                :, :, tt * 128:tt * 128 + 128],
            in_=pp[:, :].rearrange("p (c j) -> p c j", c=NCH),
            func=AF.Copy),
            wait=[tp_f], inc=s_ev)


def emit_store_fm_to_tm(B, src_fm, out_dram_seq, ostage, pA, pB, identf,
                        width=S):
    P = B.P
    s_tp, s_ev, s_st = B.sem("tp"), B.sem("ev"), B.sem("st")
    st_free = [FREE0, FREE0]
    pp_free = [FREE0, FREE0]
    for tt in range(width // 128):
        pp = pA if tt % 2 == 0 else pB
        tp_f = None
        for c in range(NCH):
            r = P.on("tensor", lambda e, pp=pp, c=c, tt=tt:
                     e.transpose(pp[:, c * 128:(c + 1) * 128],
                                 src_fm[:, c * width + tt * 128:
                                        c * width + tt * 128 + 128],
                                 identf[:, :]),
                     wait=[pp_free[tt % 2]] if c == 0 else [],
                     inc=s_tp if c == NCH - 1 else None,
                     fence=(c == NCH - 1))
            tp_f = r or tp_f
        st = ostage[tt % 2]
        ev_f = P.on("scalar", lambda e, pp=pp, st=st: e.activation(
            out=st[:, :], in_=pp[:, :], func=AF.Copy),
            wait=[tp_f, st_free[tt % 2]], inc=s_ev, drain=True)
        pp_free[tt % 2] = ev_f
        st_free[tt % 2] = P.dma(lambda e, st=st, tt=tt: e.dma_start(
            out=out_dram_seq[tt * 128:(tt + 1) * 128, :], in_=st[:, :]),
            wait=[ev_f], inc=s_st)


def emit_weight_cast(B, w_dram, wsb, wstage, eng="vector"):
    P = B.P
    s_ld, s_wc = B.sem("ld"), B.sem("wc")
    nkc = w_dram.shape[0] // 128
    M = w_dram.shape[1]
    key = ("stage", id(wstage[0]))
    st_free = B.state.setdefault(key, [FREE0, FREE0])
    for kc in range(nkc):
        st = wstage[kc % 2]
        ld_f = P.dma(lambda e, st=st, kc=kc: e.dma_start(
            out=st[:, 0:M], in_=w_dram[kc * 128:(kc + 1) * 128, :]),
            wait=[st_free[kc % 2]], inc=s_ld)
        st_free[kc % 2] = P.on(eng, lambda e, st=st, kc=kc: e.tensor_copy(
            out=wsb[:, kc * M:(kc + 1) * M], in_=st[:, 0:M]),
            wait=[ld_f], inc=s_wc)


def emit_postln_inplace(B, xfm, pA, pC, sqb, tmp, rows, ones_col, ones_row,
                        width=S, xfm_ready=FREE0, final_drain=False):
    P = B.P
    def col(c, t0, n):
        return slice(c * width + t0, c * width + t0 + n)
    s_mm, s_rw, s_ap = B.sem("mm"), B.sem("rw"), B.sem("ap")
    pc_free = FREE0
    pa_free = FREE0
    sq_free = [FREE0, FREE0]
    N = min(512, width)
    for tg in range(max(1, width // 512)):
        t0 = tg * 512
        for c in range(NCH):
            P.on("tensor", lambda e, c=c, t0=t0: e.matmul(
                pC[0:1, 0:N], ones_col[:, :], xfm[:, col(c, t0, N)],
                start=(c == 0), stop=(c == NCH - 1)),
                wait=[pc_free, xfm_ready] if c == 0 else [])
        mm_f = None
        for c in range(NCH):
            sq = sqb[c % 2]
            rw_f = P.on("scalar", lambda e, c=c, t0=t0, sq=sq: e.activation(
                out=sq[:, 0:N], in_=xfm[:, col(c, t0, N)], func=AF.Square),
                wait=[sq_free[c % 2], xfm_ready if c == 0 else FREE0],
                inc=s_rw, drain=True)
            mm_f = P.on("tensor", lambda e, c=c, sq=sq: e.matmul(
                pC[0:1, 512:512 + N], ones_col[:, :], sq[:, 0:N],
                start=(c == 0), stop=(c == NCH - 1)),
                wait=[rw_f], inc=s_mm, fence=(c == NCH - 1))
            sq_free[c % 2] = mm_f
        P.on("scalar", lambda e: e.activation(
            out=rows[:, 0:N], in_=pC[0:1, 0:N], func=AF.Copy, scale=1.0 / D),
            wait=[mm_f])
        pc_free = P.on("scalar", lambda e: e.activation(
            out=rows[:, 512:512 + N], in_=pC[0:1, 512:512 + N], func=AF.Copy,
            scale=1.0 / D), inc=s_rw, drain=True)
        P.on("vector", lambda e: e.tensor_tensor(
            out=rows[:, 1024:1024 + N], in0=rows[:, 0:N], in1=rows[:, 0:N],
            op=ALU.mult), wait=[pc_free])
        var_f = P.on("vector", lambda e: e.tensor_tensor(
            out=rows[:, 512:512 + N], in0=rows[:, 512:512 + N],
            in1=rows[:, 1024:1024 + N],
            op=ALU.subtract), inc=s_rw, drain=True)
        P.on("scalar", lambda e: e.activation(
            out=rows[:, 512:512 + N], in_=rows[:, 512:512 + N], func=AF.Ln,
            bias=B.bias(LN_EPS)[0:1, :]), wait=[var_f])
        rstd_f = P.on("scalar", lambda e: e.activation(
            out=rows[:, 512:512 + N], in_=rows[:, 512:512 + N], func=AF.Exp,
            scale=-0.5), inc=s_rw, drain=True)
        bm_f = P.on("vector", lambda e: e.tensor_tensor(
            out=rows[:, 0:N], in0=rows[:, 0:N], in1=rows[:, 512:512 + N],
            op=ALU.mult), wait=[rstd_f], inc=s_rw, drain=True)
        P.on("tensor", lambda e: e.matmul(
            pA[:, 0:N], ones_row[:, :], rows[:, 512:512 + N],
            start=True, stop=True),
            wait=[bm_f, pa_free])
        bc_f = P.on("tensor", lambda e: e.matmul(
            pA[:, 512:512 + N], ones_row[:, :], rows[:, 0:N],
            start=True, stop=True), inc=s_mm, fence=True)
        ap_f = None
        for c in range(NCH):
            P.on("vector", lambda e, c=c, t0=t0: e.tensor_tensor(
                out=tmp[:, 0:N], in0=xfm[:, col(c, t0, N)], in1=pA[:, 0:N],
                op=ALU.mult),
                wait=[bc_f] if c == 0 else [])
            ap_f = P.on("vector", lambda e, c=c, t0=t0: e.tensor_tensor(
                out=xfm[:, col(c, t0, N)], in0=tmp[:, 0:N],
                in1=pA[:, 512:512 + N],
                op=ALU.subtract), inc=s_ap,
                drain=(final_drain and c == NCH - 1
                       and tg == max(1, width // 512) - 1))
        pa_free = ap_f


def emit_xtiles(B, x_dram_rows, ntiles, dst_fn, xstage, pA, pB, identf,
                evac_eng="scalar", dst_free=FREE0):
    P = B.P
    s_ld, s_tp, s_ev = B.sem("ld"), B.sem("tp"), B.sem("ev")
    key = ("stage", id(xstage[0]))
    st_free = B.state.setdefault(key, [FREE0, FREE0])
    keyp = ("xpsum", id(pA))
    pp_free = B.state.setdefault(keyp, [FREE0, FREE0])
    ev_f = None
    for tt in range(ntiles):
        st = xstage[tt % 2]
        ld_f = P.dma(lambda e, st=st, tt=tt: e.dma_start(
            out=st[:, 0:D], in_=x_dram_rows[tt * 128:(tt + 1) * 128, :]),
            wait=[st_free[tt % 2]], inc=s_ld)
        pp = pA if tt % 2 == 0 else pB
        tp_f = None
        for c in range(NCH):
            r = P.on("tensor", lambda e, st=st, pp=pp, c=c:
                     e.transpose(pp[:, c * 128:(c + 1) * 128],
                                 st[:, c * 128:(c + 1) * 128], identf[:, :]),
                     wait=[ld_f, pp_free[tt % 2]] if c == 0 else [],
                     inc=s_tp if c == NCH - 1 else None,
                     fence=(c == NCH - 1))
            tp_f = r or tp_f
        st_free[tt % 2] = tp_f
        if evac_eng == "scalar":
            ev_f = P.on("scalar", lambda e, pp=pp, tt=tt: e.activation(
                out=dst_fn(tt), in_=pp[:, :].rearrange("p (c j) -> p c j", c=NCH),
                func=AF.Copy),
                wait=[tp_f, dst_free], inc=s_ev, drain=True)
        else:
            ev_f = P.on("vector", lambda e, pp=pp, tt=tt: e.tensor_copy(
                out=dst_fn(tt), in_=pp[:, :].rearrange("p (c j) -> p c j", c=NCH)),
                wait=[tp_f, dst_free], inc=s_ev, drain=True)
        pp_free[tt % 2] = ev_f
    return ev_f


def emit_wstream(B, w_dram_2d, dst_sb, dst_off, nkc, cols, wstage, dst_free):
    P = B.P
    s_wl, s_wcp = B.sem("wl"), B.sem("wcp")
    key = ("stage", id(wstage[0]))
    st_free = B.state.setdefault(key, [FREE0, FREE0])
    cp_f = None
    for kc in range(nkc):
        st = wstage[kc % 2]
        ld_f = P.dma(lambda e, st=st, kc=kc: e.dma_start(
            out=st[:, 0:cols], in_=w_dram_2d[kc * 128:(kc + 1) * 128, :]),
            wait=[st_free[kc % 2]], inc=s_wl)
        cp_f = P.on("gpsimd", lambda e, st=st, kc=kc: e.tensor_copy(
            out=dst_sb[:, dst_off + kc * cols:dst_off + (kc + 1) * cols],
            in_=st[:, 0:cols]),
            wait=[ld_f] + ([dst_free] if kc == 0 else []),
            inc=s_wcp, drain=(kc == nkc - 1))
        st_free[kc % 2] = cp_f
    return cp_f



DEBUG_L0A = False
B2_BARRIERS = False


def build_l0a(B=None):
    standalone = B is None
    if standalone:
        B = Builder()
    nc, P = B.nc, B.P
    xs = nc.dram_tensor("xs", [BPC, S, D], F32, kind="ExternalInput").ap()
    wq_d = nc.dram_tensor("wq", [D, D], F32, kind="ExternalInput").ap()
    wkv_d = nc.dram_tensor("wkv", [D, D], F32, kind="ExternalInput").ap()
    wout_d = nc.dram_tensor("wout", [D, D], F32, kind="ExternalInput").ap()
    boutT = nc.dram_tensor("boutT", [128, NCH], F32, kind="ExternalInput").ap()
    identf_d = nc.dram_tensor("identf", [128, 128], F32, kind="ExternalInput").ap()
    identb_d = nc.dram_tensor("identb", [128, 128], BF16, kind="ExternalInput").ap()
    cos_d = nc.dram_tensor("cos64", [S, 64], F32, kind="ExternalInput").ap()
    sin_d = nc.dram_tensor("sin64", [S, 64], F32, kind="ExternalInput").ap()
    x1_d = nc.dram_tensor("x1", [BPC, S, D], F32, kind="ExternalOutput").ap()
    ctx_d = nc.dram_tensor("ctxd", [NCH, 128, S], BF16,
                           kind="ExternalOutput" if DEBUG_L0A else "Internal").ap()
    if DEBUG_L0A:
        B.dbg_qtm = nc.dram_tensor("dbg_qtm", [128, NTT * D], BF16,
                                   kind="ExternalOutput").ap()
        B.dbg_kvtm = nc.dram_tensor("dbg_kvtm", [128, NTT * D], BF16,
                                    kind="ExternalOutput").ap()
        B.dbg_pre = nc.dram_tensor("dbg_pre", [128, NCH * S], F32,
                                   kind="ExternalOutput").ap()
        B.dbg_qraw = nc.dram_tensor("dbg_qraw", [2, 128, D], F32,
                                    kind="ExternalOutput").ap()
        B.dbg_xfm = nc.dram_tensor("dbg_xfm", [128, NCH * S], F32,
                                   kind="ExternalOutput").ap()
        B.dbg_w = nc.dram_tensor("dbg_w", [2, 128, NCH * D], BF16,
                                 kind="ExternalOutput").ap()

    s_ld = B.sem("ld")

    xfm = B.sb("xfm", [128, NCH * S], F32)
    identf = B.sb("sidf", [128, 128], F32)
    identb = B.sb("sidb", [128, 128], BF16)
    ones_col = B.sb("ones_col", [128, 1], F32)
    ones_row = B.sb("ones_row", [1, 128], F32)
    boutsb = B.sb("boutsb", [128, NCH], F32)
    qtm = B.sb("qtm", [128, NTT * D], BF16)
    kvtm = B.sb("kvtm", [128, NTT * D], BF16)
    cossb = B.sb("cossb", [128, NTT * 64], F32)
    sinsb = B.sb("sinsb", [128, NTT * 64], F32)
    wq = B.sb("wqsb", [128, NCH * D], BF16)
    wkv = B.sb("wkvsb", [128, NCH * D], BF16)
    wout = B.sb("wosb", [128, NCH * D], BF16)

    pA = B.ps("pA", [128, 1024], F32)
    pB = B.ps("pB", [128, 1024], F32)
    pC = B.ps("pC", [128, 1024], F32)
    pT0 = B.ps("pT0", [128, 1024], BF16)
    pF = B.ps("pF", [128, 512], F32)
    pT1 = pT0
    B.make_fence(pF, ones_col, identf)

    P.dma(lambda e: e.dma_start(out=identf[:, :], in_=identf_d), inc=s_ld)
    P.dma(lambda e: e.dma_start(out=identb[:, :], in_=identb_d), inc=s_ld)
    P.dma(lambda e: e.dma_start(out=boutsb[:, :], in_=boutT), inc=s_ld)
    P.dma(lambda e: e.dma_start(
        out=cossb[:, :].rearrange("p (t d) -> p t d", t=NTT),
        in_=cos_d.rearrange("(t p) d -> p t d", p=128)), inc=s_ld)
    P.dma(lambda e: e.dma_start(
        out=sinsb[:, :].rearrange("p (t d) -> p t d", t=NTT),
        in_=sin_d.rearrange("(t p) d -> p t d", p=128)), inc=s_ld)
    P.on("vector", lambda e: e.memset(ones_col[:, :], 1.0))
    P.on("vector", lambda e: e.memset(ones_row[:, :], 1.0))
    B.make_bias_consts([-L0_SHIFT, LN_EPS])
    P.on("sync", lambda e, v=s_ld.n: e.wait_ge(s_ld.h, v))
    P.block("consts")

    with ExitStack() as cw:
        wstage = [cw.enter_context(nc.sbuf_tensor(f"wst{i}", [128, D], F32))
                  for i in range(2)]
        emit_weight_cast(B, wq_d, wq, wstage)
        emit_weight_cast(B, wkv_d, wkv, wstage)
        emit_weight_cast(B, wout_d, wout, wstage)
        P.block("wcast")
        if DEBUG_L0A:
            s_dbg4 = B.sem("dbg4")
            P.dma(lambda e: e.dma_start(out=B.dbg_w[0], in_=wq[:, :]), inc=s_dbg4)
            P.dma(lambda e: e.dma_start(out=B.dbg_w[1], in_=wkv[:, :]), inc=s_dbg4)
            P.on("sync", lambda e, v=s_dbg4.n: e.wait_ge(s_dbg4.h, v))
            P.block("dbgdump4")

    for seq in range(BPC):
        _l0a_seq(B, seq, xs, x1_d, ctx_d, xfm, qtm, kvtm, cossb, sinsb,
                 identf, identb, ones_col, ones_row, boutsb, wq, wkv, wout,
                 pA, pB, pC, pT0, pT1)

    return B.finish() if standalone else None


def _l0a_seq(B, seq, xs, x1_d, ctx_d, xfm, qtm, kvtm, cossb, sinsb,
             identf, identb, ones_col, ones_row, boutsb, wq, wkv, wout,
             pA, pB, pC, pT0, pT1):
    nc, P = B.nc, B.P
    s_xb, s_qk, s_pj, s_rot = (B.sem(n) for n in ("xb", "qk", "pj", "rot"))

    with ExitStack() as c1:
        xstage = [c1.enter_context(nc.sbuf_tensor(f"xst{seq}_{i}", [128, D], F32))
                  for i in range(2)]
        emit_load_x_fm(B, xs[seq], xfm, xstage, pA, pB, identf)
        P.block(f"s{seq}_loadx")
        if DEBUG_L0A and seq == 0:
            s_dbg3 = B.sem("dbg3")
            P.dma(lambda e: e.dma_start(out=B.dbg_xfm, in_=xfm[:, :]), inc=s_dbg3)
            P.on("sync", lambda e, v=s_dbg3.n: e.wait_ge(s_dbg3.h, v))
            P.block("dbgdump3")

    with ExitStack() as c2:
        en2 = c2.enter_context
        xbt = en2(nc.sbuf_tensor(f"xbt{seq}", [128, NCH * 128], BF16))
        qraw = en2(nc.sbuf_tensor(f"qraw{seq}", [128, D], F32))
        t1 = en2(nc.sbuf_tensor(f"t1_{seq}", [128, D], F32))
        t2 = en2(nc.sbuf_tensor(f"t2_{seq}", [128, D], F32))

        xbt_free = FREE0
        pp_free = [FREE0, FREE0]
        qraw_free = FREE0
        for tt in range(NTT):
            xb_f = P.on("vector", lambda e, tt=tt: e.tensor_copy(
                out=xbt[:, :].rearrange("p (c j) -> p c j", c=NCH),
                in_=xfm[:, :].rearrange("p (c s) -> p c s", c=NCH)[
                    :, :, tt * 128:tt * 128 + 128]),
                wait=[xbt_free], inc=s_xb, drain=True)
            for wi, (wt, dst, scale) in enumerate(
                    ((wq, qtm, 0.125), (wkv, kvtm, 1.0))):
                pp = pB if wi == 0 else pA
                qk_f = None
                for half in range(2):
                    for kc in range(NCH):
                        r = P.on("tensor",
                                 lambda e, pp=pp, kc=kc, half=half, wt=wt:
                                 e.matmul(pp[:, half * 512:half * 512 + 512],
                                          xbt[:, kc * 128:(kc + 1) * 128],
                                          wt[:, kc * D + half * 512:
                                             kc * D + half * 512 + 512],
                                          start=(kc == 0), stop=(kc == NCH - 1)),
                                 wait=([xb_f, pp_free[wi]]
                                       if kc == 0 and half == 0 else []),
                                 inc=s_qk if (kc == NCH - 1 and half == 1) else None,
                                 fence=(kc == NCH - 1 and half == 1))
                        qk_f = r or qk_f
                xbt_free = qk_f
                pj_f = P.on("vector", lambda e, pp=pp, scale=scale:
                    e.tensor_scalar_mul(qraw[:, :], pp[:, :], scale),
                    wait=[qk_f, qraw_free], inc=s_pj)
                pp_free[wi] = pj_f
                if DEBUG_L0A and seq == 0 and tt == 0:
                    s_dbg2 = B.sem("dbg2")
                    db_f = P.dma(lambda e, wi=wi: e.dma_start(
                        out=B.dbg_qraw[wi], in_=qraw[:, :]),
                        wait=[pj_f], inc=s_dbg2)
                    qraw_free = db_f
                cs = cossb[:, tt * 64:(tt + 1) * 64]
                sn = sinsb[:, tt * 64:(tt + 1) * 64]
                qv = qraw[:, :].rearrange("p (h sj) -> p h sj", sj=64)
                t1v = t1[:, :].rearrange("p (h sj) -> p h sj", sj=64)
                t2v = t2[:, :].rearrange("p (h sj) -> p h sj", sj=64)
                rot_f = None
                for hh in range(H):
                    P.on("vector", lambda e, hh=hh, cs=cs, qv=qv, t1v=t1v:
                         e.tensor_tensor(out=t1v[:, hh, :], in0=qv[:, hh, :],
                                         in1=cs, op=ALU.mult),
                         wait=[pj_f] if hh == 0 else [])
                    P.on("vector", lambda e, hh=hh, sn=sn, qv=qv, t2v=t2v:
                         e.tensor_tensor(out=t2v[:, hh, 0:32],
                                         in0=qv[:, hh, 32:64],
                                         in1=sn[:, 0:32], op=ALU.mult))
                    r = P.on("vector", lambda e, hh=hh, sn=sn, qv=qv, t2v=t2v:
                             e.tensor_tensor(out=t2v[:, hh, 32:64],
                                             in0=qv[:, hh, 0:32],
                                             in1=sn[:, 32:64], op=ALU.mult),
                             inc=s_rot if hh == H - 1 else None)
                    rot_f = r or rot_f
                qraw_free = rot_f
                P.on("vector", lambda e, dst=dst, tt=tt: e.tensor_tensor(
                    out=dst[:, tt * D:(tt + 1) * D], in0=t1[:, :], in1=t2[:, :],
                    op=ALU.add))
            if B2_BARRIERS:
                P.block(f"s{seq}_proj_t{tt}")
        P.block(f"s{seq}_proj")
        if DEBUG_L0A and seq == 0:
            s_dbg = B.sem("dbg")
            P.dma(lambda e: e.dma_start(out=B.dbg_qtm, in_=qtm[:, :]), inc=s_dbg)
            P.dma(lambda e: e.dma_start(out=B.dbg_kvtm, in_=kvtm[:, :]), inc=s_dbg)
            P.on("sync", lambda e, v=s_dbg.n: e.wait_ge(s_dbg.h, v))
            P.block("dbgdump")

    with ExitStack() as c3:
        en3 = c3.enter_context
        qfh = en3(nc.sbuf_tensor(f"qfh{seq}", [64, S], BF16))
        kfh = en3(nc.sbuf_tensor(f"kfh{seq}", [64, S], BF16))
        ktm65 = en3(nc.sbuf_tensor(f"ktm65{seq}", [128, NTT * 65], BF16))
        ptsb = [en3(nc.sbuf_tensor(f"pt{seq}_{i}", [128, 1024], BF16))
                for i in range(2)]
        ctxs = en3(nc.sbuf_tensor(f"ctxs{seq}", [64, 1024], BF16))
        bcsb = en3(nc.sbuf_tensor(f"bcsb{seq}", [65, 1024], F32))
        drow = bcsb[64:65, :]
        hs = {
            "pt_free": [FREE0, FREE0],
            "st_free": [FREE0, FREE0],
            "ctx_free": FREE0,
            "fm_free": FREE0,
            "k65_free": FREE0,
            "ptbank_free": [FREE0, FREE0],
            "ctxs_free": FREE0,
            "bc_free": FREE0,
        }
        for h in range(H):
            _l0a_head(B, seq, h, hs, qtm, kvtm, qfh, kfh, ktm65, ptsb, ctxs,
                      bcsb, drow, ctx_d, identb, ones_row, pA, pB, pC, pT0, pT1)
        P.block(f"s{seq}_attn")

    with ExitStack() as c4:
        en4 = c4.enter_context
        ctxsb = [en4(nc.sbuf_tensor(f"cxsb{seq}_0", [128, NCH * 512], BF16))]
        sqb = [en4(nc.sbuf_tensor(f"sq{seq}_{i}", [128, 512], F32))
               for i in range(2)]
        tmp = en4(nc.sbuf_tensor(f"tmp{seq}", [128, 512], F32))
        rows = en4(nc.sbuf_tensor(f"rows{seq}", [1, 1536], F32))
        s_ld = B.sem("ld")

        slots = [(pA, 0), (pA, 512), (pB, 0), (pB, 512)]
        slot_free = [FREE0] * 4
        cx_free = [FREE0]
        s_cd = B.sem("cd")
        idx = 0
        for tg in range(4):
            cx = ctxsb[0]
            ld_f = P.dma(lambda e, cx=cx, tg=tg: e.dma_start(
                out=cx[:, :].rearrange("p (c s) -> p c s", c=NCH),
                in_=ctx_d[:, :, tg * 512:(tg + 1) * 512].rearrange(
                    "c p s -> p c s")),
                wait=[cx_free[0], (s_cd, s_cd.n)], inc=s_ld)
            for mc in range(NCH):
                pp, off = slots[idx % 4]
                qk_f = None
                for kc in range(NCH):
                    r = P.on("tensor",
                             lambda e, pp=pp, off=off, kc=kc, mc=mc, cx=cx:
                             e.matmul(pp[:, off:off + 512],
                                      wout[:, kc * D + mc * 128:
                                           kc * D + mc * 128 + 128],
                                      cx[:, kc * 512:(kc + 1) * 512],
                                      start=(kc == 0), stop=(kc == NCH - 1)),
                             wait=[ld_f, slot_free[idx % 4]] if kc == 0 else [],
                             inc=s_qk if kc == NCH - 1 else None,
                             fence=(kc == NCH - 1))
                    qk_f = r or qk_f
                if mc == NCH - 1:
                    cx_free[0] = qk_f
                P.on("vector", lambda e, pp=pp, off=off, mc=mc:
                     e.tensor_scalar(out=tmp[:, :], in0=pp[:, off:off + 512],
                                     scalar1=boutsb[:, mc:mc + 1], scalar2=None,
                                     op0=ALU.add),
                     wait=[qk_f])
                slot_free[idx % 4] = P.on(
                    "vector", lambda e, mc=mc, tg=tg:
                    e.tensor_tensor(out=xfm[:, fmc(mc, tg * 512, 512)],
                                    in0=tmp[:, :],
                                    in1=xfm[:, fmc(mc, tg * 512, 512)],
                                    op=ALU.add),
                    inc=s_pj)
                idx += 1
        P.block(f"s{seq}_wout")
        if DEBUG_L0A and seq == 0:
            s_dbg = B.sem("dbg")
            P.dma(lambda e: e.dma_start(out=B.dbg_pre, in_=xfm[:, :]), inc=s_dbg)
            P.on("sync", lambda e, v=s_dbg.n: e.wait_ge(s_dbg.h, v))
            P.block("dbgdump2")

        emit_postln_inplace(B, xfm, pA, pC, sqb, tmp, rows, ones_col, ones_row)
        P.block(f"s{seq}_ln")

    with ExitStack() as c5:
        ostage = [c5.enter_context(nc.sbuf_tensor(f"ost{seq}_{i}", [128, D], F32))
                  for i in range(2)]
        emit_store_fm_to_tm(B, xfm, x1_d[seq], ostage, pA, pB, identf)
        s_st = B.sem("st")
        P.on("sync", lambda e, v=s_st.n: e.wait_ge(s_st.h, v))
        P.block(f"s{seq}_store")


def _l0a_head(B, seq, h, hs, qtm, kvtm, qfh, kfh, ktm65, ptsb, ctxs, bcsb,
              drow, ctx_d, identb, ones_row, pA, pB, pC, pT0, pT1):
    nc, P = B.nc, B.P
    s_tp, s_ev = B.sem("tp"), B.sem("ev")
    s_qk, s_ex, s_av, s_ce = (B.sem(n) for n in ("qk2", "ex", "av", "ce"))
    s_cd = B.sem("cd")

    for src, dst in ((qtm, qfh), (kvtm, kfh)):
        for half in range(2):
            rb = 64 * half
            tp_f = None
            for i in range(8):
                tt = half * 8 + i
                r = P.on("tensor", lambda e, rb=rb, i=i, tt=tt, src=src:
                         e.transpose(pT0[rb:rb + 64, i * 128:(i + 1) * 128],
                                     src[:, tt * D + h * 64:tt * D + h * 64 + 64],
                                     identb[:, :]),
                         wait=[hs["ptbank_free"][half]] if i == 0 else [],
                         inc=s_tp if i == 7 else None)
                tp_f = r or tp_f
            hs["ptbank_free"][half] = P.on(
                "vector", lambda e, rb=rb, half=half, dst=dst:
                e.tensor_copy(out=dst[:, half * 1024:(half + 1) * 1024],
                              in_=pT0[rb:rb + 64, :]),
                wait=[tp_f, hs["fm_free"]], inc=s_ev, drain=True)
    P.on("vector", lambda e: e.memset(ktm65[:, :], 1.0),
         wait=[hs["k65_free"]])
    k65_f = P.on("vector", lambda e: e.tensor_copy(
        out=ktm65[:, :].rearrange("p (t c) -> p t c", c=65)[:, :, 0:64],
        in_=kvtm[:, :].rearrange("p (t d) -> p t d", d=D)[
            :, :, h * 64:h * 64 + 64]),
        inc=s_ev, drain=True)

    for qcp in range(2):
        q0 = qcp * 1024

        def emit_qk(kt):
            stp = pA if kt % 2 == 0 else pB
            qk_f = None
            for qc in range(2):
                r = P.on("tensor", lambda e, stp=stp, kt=kt, qc=qc, q0=q0:
                         e.matmul(stp[:, qc * 512:qc * 512 + 512],
                                  kfh[:, kt * 128:(kt + 1) * 128],
                                  qfh[:, q0 + qc * 512:q0 + qc * 512 + 512],
                                  start=True, stop=True),
                         wait=([k65_f, hs["st_free"][kt % 2]]
                               if qc == 0 else []),
                         inc=s_qk if qc == 1 else None)
                qk_f = r or qk_f
            return qk_f

        def emit_exp(kt, qk_f):
            stp = pA if kt % 2 == 0 else pB
            pt = ptsb[kt % 2]
            ex_f = P.on("scalar", lambda e, stp=stp, pt=pt: e.activation(
                out=pt[:, :], in_=stp[:, :], func=AF.Exp, bias=B.bias(-L0_SHIFT)),
                wait=[qk_f, hs["pt_free"][kt % 2]], inc=s_ex, drain=True)
            hs["st_free"][kt % 2] = ex_f
            return ex_f

        def emit_av(kt, ex_f):
            av_f = None
            pt = ptsb[kt % 2]
            for qc in range(2):
                r = P.on("tensor", lambda e, pt=pt, kt=kt, qc=qc:
                         e.matmul(pC[0:65, qc * 512:qc * 512 + 512],
                                  ktm65[:, kt * 65:kt * 65 + 65],
                                  pt[:, qc * 512:qc * 512 + 512],
                                  start=(kt == 0), stop=(kt == NTT - 1)),
                         wait=[ex_f, hs["ctx_free"]] if qc == 0 else [],
                         inc=s_av if qc == 1 else None,
                         fence=(qc == 1 and kt == NTT - 1))
                av_f = r or av_f
            hs["pt_free"][kt % 2] = av_f
            return av_f

        exv = {}
        qk0 = emit_qk(0)
        exv[0] = emit_exp(0, qk0)
        for kt in range(1, NTT):
            qkk = emit_qk(kt)
            emit_av(kt - 1, exv[kt - 1])
            exv[kt] = emit_exp(kt, qkk)
        av_last = emit_av(NTT - 1, exv[NTT - 1])
        hs["k65_free"] = av_last

        P.on("vector", lambda e: e.tensor_copy(out=bcsb[0:1, :], in_=pC[64:65, :]),
             wait=[av_last, hs["bc_free"]])
        rc_f = P.on("vector", lambda e: e.reciprocal(
            out=bcsb[0:1, :], in_=bcsb[0:1, :]), inc=s_ce, drain=True)
        bc_f = None
        for qc in range(2):
            r = P.on("tensor", lambda e, qc=qc: e.matmul(
                pA[0:64, qc * 512:qc * 512 + 512], ones_row[0:1, 0:64],
                bcsb[0:1, qc * 512:qc * 512 + 512], start=True, stop=True),
                wait=[rc_f, hs["st_free"][0]] if qc == 0 else [],
                inc=s_qk if qc == 1 else None, fence=(qc == 1))
            bc_f = r or bc_f
        hs["bc_free"] = (B.sem("qk2"), B.sem("qk2").n)
        cp_f = P.on("vector", lambda e: e.tensor_copy(
            out=bcsb[0:64, :], in_=pA[0:64, :]),
            wait=[bc_f], inc=s_ce)
        hs["st_free"][0] = cp_f
        sc_f = P.on("vector", lambda e: e.tensor_tensor(
            out=ctxs[:, :], in0=pC[0:64, :], in1=bcsb[0:64, :], op=ALU.mult),
            wait=[hs["ctxs_free"]], inc=s_ce, drain=True)
        hs["ctx_free"] = sc_f
        c = h // 2
        pbase = 64 * (h % 2)
        hs["ctxs_free"] = P.dma(
            lambda e, c=c, pbase=pbase, q0=q0: e.dma_start(
                out=ctx_d[c, pbase:pbase + 64, q0:q0 + 1024], in_=ctxs[:, :]),
            wait=[sc_f], inc=s_cd)
    hs["fm_free"] = (s_qk, s_qk.n)



def _common_consts():
    cc = host_constants()
    return dict(identf=cc["identf"], identb=cc["identb"],
                cos64=cc["cos64"], sin64=cc["sin64"])


def run_l0a(L0_attn_x, L0_attn_Wq, L0_attn_Wkv, L0_attn_Wout, L0_attn_bout):
    nc = build_l0a()
    x = np.ascontiguousarray(np.asarray(L0_attn_x, dtype=np.float32))
    shards = x.reshape(NCORE, BPC, S, D)
    boutT = np.ascontiguousarray(
        np.asarray(L0_attn_bout, np.float32).reshape(NCH, 128).T)
    base = dict(
        wq=np.ascontiguousarray(np.asarray(L0_attn_Wq, np.float32)),
        wkv=np.ascontiguousarray(np.asarray(L0_attn_Wkv, np.float32)),
        wout=np.ascontiguousarray(np.asarray(L0_attn_Wout, np.float32)),
        boutT=boutT, **_common_consts())
    in_maps = [dict(base, xs=np.ascontiguousarray(shards[i]))
               for i in range(NCORE)]
    res = run_bass_kernel_spmd(nc, in_maps, core_ids=list(range(NCORE)))
    out = np.concatenate([r["x1"] for r in res.results], axis=0)
    return out.reshape(16, S, D)


def build_mega():
    B = Builder()
    for nm in ("x1", "x2", "x3"):
        B.dram_map[nm] = B._nc_real.dram_tensor(nm, [BPC, S, D], F32,
                                                kind="Internal")
    B.begin_phase("A_")
    build_l0a(B)
    B.end_phase()
    B.begin_phase("F_")
    build_l0f(B)
    B.end_phase()
    B.begin_phase("C_")
    build_l1a(B)
    B.end_phase()
    B.begin_phase("G_")
    build_l1f(B)
    B.end_phase()
    return B.finish()


def kernel(**inputs):
    cc = host_l1_consts()
    nc = build_mega()
    x = np.ascontiguousarray(
        np.asarray(inputs["L0_attn_x"], np.float32).reshape(NCORE, BPC, S, D))
    scal = np.zeros((1, 4), np.float32)
    scal[0, 0] = np.float32(np.asarray(inputs["L1_ffn_unc_b2"]).reshape(-1)[0])
    scal[0, 1] = np.float32(inputs["L1_ffn_gate_w1"])
    scal[0, 2] = np.float32(inputs["L1_ffn_gate_bias"])
    f32c = lambda a: np.ascontiguousarray(np.asarray(a, np.float32))
    base = dict(
        wq=f32c(inputs["L0_attn_Wq"]), wkv=f32c(inputs["L0_attn_Wkv"]),
        wout=f32c(inputs["L0_attn_Wout"]),
        boutT=f32c(np.asarray(inputs["L0_attn_bout"],
                              np.float32).reshape(NCH, 128).T),
        identf=cc["identf"], identb=cc["identb"],
        cos64=cc["cos64"], sin64=cc["sin64"],
        wred=f32c(inputs["L0_ffn_w_red"]),
        wgT=f32c(np.asarray(inputs["L0_ffn_wg"], np.float32).T),
        W1=f32c(inputs["L0_ffn_W1"]),
        b1T=f32c(np.asarray(inputs["L0_ffn_b1"], np.float32)
                 .reshape(NEXP, FFH // 128, 128).transpose(0, 2, 1)),
        W2=f32c(inputs["L0_ffn_W2"]), b2=f32c(inputs["L0_ffn_b2"]),
        wqT=f32c(np.asarray(inputs["L1_attn_Wq"], np.float32).T),
        wkT=f32c(np.asarray(inputs["L1_attn_Wk"], np.float32).T),
        wvT=f32c(np.asarray(inputs["L1_attn_Wv"], np.float32).T),
        woT=f32c(np.asarray(inputs["L1_attn_Wo"], np.float32).T),
        boT=f32c(np.asarray(inputs["L1_attn_bo"],
                            np.float32).reshape(NCH, 128).T),
        stripb=cc["stripb"], cinf=cc["cinf"],
        qpT=f32c(np.asarray(inputs["L1_ffn_q_proj_w"], np.float32).T),
        rwT=f32c(np.asarray(inputs["L1_ffn_router_w"], np.float32).T),
        akT=f32c(np.asarray(inputs["L1_ffn_aux_keys"], np.float32).T),
        av=f32c(inputs["L1_ffn_aux_values"]),
        rmask=f32c(np.asarray(inputs["L1_ffn_reliability_mask"],
                              np.float32).reshape(1, AUXN)),
        uw1T=f32c(np.asarray(inputs["L1_ffn_unc_w1"], np.float32).T),
        b1uT=f32c(np.asarray(inputs["L1_ffn_unc_b1"],
                             np.float32).reshape(2, 128).T),
        uw2T=f32c(np.asarray(inputs["L1_ffn_unc_w2"],
                             np.float32).reshape(2, 128).T),
        scal=scal)
    in_maps = [dict(base, xs=np.ascontiguousarray(x[i])) for i in range(NCORE)]
    res = run_bass_kernel_spmd(nc, in_maps, core_ids=list(range(NCORE)))
    out = np.concatenate([r["out"] for r in res.results], axis=0)
    return np.ascontiguousarray(out.reshape(16, S, D).astype(np.float32))



FFH = 2048
NEXP = 8
GRP = 1024
DEBUG_L0F = False


def build_l0f(B=None):
    standalone = B is None
    if standalone:
        B = Builder()
    nc, P = B.nc, B.P
    x1 = nc.dram_tensor("x1", [BPC, S, D], F32, kind="ExternalInput").ap()
    wred_d = nc.dram_tensor("wred", [D, 16], F32, kind="ExternalInput").ap()
    wgT_d = nc.dram_tensor("wgT", [16, NEXP], F32, kind="ExternalInput").ap()
    W1_d = nc.dram_tensor("W1", [NEXP, D, FFH], F32, kind="ExternalInput").ap()
    b1T_d = nc.dram_tensor("b1T", [NEXP, 128, FFH // 128], F32, kind="ExternalInput").ap()
    W2_d = nc.dram_tensor("W2", [NEXP, FFH, D], F32, kind="ExternalInput").ap()
    b2_d = nc.dram_tensor("b2", [NEXP, D], F32, kind="ExternalInput").ap()
    identf_d = nc.dram_tensor("identf", [128, 128], F32, kind="ExternalInput").ap()
    x2_d = nc.dram_tensor("x2", [BPC, S, D], F32, kind="ExternalOutput").ap()
    if DEBUG_L0F:
        B.dbg_xb = nc.dram_tensor("dbg_xb", [128, NCH * GRP], BF16,
                                  kind="ExternalOutput").ap()
        B.dbg_gat = nc.dram_tensor("dbg_gat", [9, GRP], F32,
                                   kind="ExternalOutput").ap()
        B.dbg_oacc = nc.dram_tensor("dbg_oacc", [128, NCH * GRP], F32,
                                    kind="ExternalOutput").ap()
        B.dbg_hb = nc.dram_tensor("dbg_hb", [128, 16 * 512], BF16,
                                  kind="ExternalOutput").ap()
        B.dbg_comb = nc.dram_tensor("dbg_comb", [128, NCH * GRP], F32,
                                    kind="ExternalOutput").ap()
        B.dbg_gs = nc.dram_tensor("dbg_gs", [128, 80], F32,
                                  kind="ExternalOutput").ap()

    s_ld = B.sem("ld")
    s_mm2 = B.sem("mm2")
    s_ge = B.sem("ge")
    s_mk = B.sem("mk")
    s_oa = B.sem("oa")
    s_gt = B.sem("gt")

    identf = B.sb("sidf", [128, 128], F32)
    ones_col = B.sb("ones_col", [128, 1], F32)
    ones_row = B.sb("ones_row", [1, 128], F32)
    xb = B.sb("xb", [128, NCH * GRP], BF16)
    o_acc = B.sb("o_acc", [128, NCH * GRP], F32)
    w1b = [B.sb(f"w1b{i}", [128, NCH * FFH], BF16) for i in range(2)]
    w2b = B.sb("w2b", [128, (FFH // 128) * D], BF16)
    hb = B.sb("hb", [128, 16 * 512], BF16)
    stage = [B.sb(f"stg{i}", [128, FFH], F32) for i in range(2)]
    wrsb = B.sb("wrsb", [128, NCH * 16], BF16)
    wgT = B.sb("wgTsb", [16, NEXP], F32)
    wgnb = B.sb("wgnb", [16, NEXP], BF16)
    b2sb = B.sb("b2sb", [NEXP, D], BF16)
    b1sb = B.sb("b1sb", [128, FFH // 128], F32)
    xrTb = B.sb("xrTb", [16, GRP], BF16)
    mx = B.sb("mx", [128, 8], F32)
    mxn = B.sb("mxn", [128, 8], F32)
    gt = B.sb("gtb", [128, 64], F32)
    gs = B.sb("gs", [128, 8], F32)
    oh_tm = B.sb("oh_tm", [128, 9 * 8], F32)
    gat = B.sb("gat", [9, GRP], F32)
    ohb = B.sb("ohb", [NEXP, GRP], BF16)
    ssr = B.sb("ssr", [1, GRP], F32)
    maske = B.sb("maske", [128, GRP], BF16)
    B.orow = B.sb("orow", [1, GRP], F32)
    x1t = B.sb("x1t", [128, NCH * 128], F32)
    sqb = [stage[0][:, 0:512], stage[1][:, 0:512]]
    tmp = stage[0][:, 512:1024]
    tmp2 = B.sb("tmp2", [128, 128], F32)
    rows = stage[1][0:1, 512:2048]

    pA = B.ps("pA", [128, 1024], F32)
    pB = B.ps("pB", [128, 1024], F32)
    pC = B.ps("pC", [128, 1024], F32)
    pF = B.ps("pF", [128, 512], F32)
    B.make_fence(pF, ones_col, identf)

    P.dma(lambda e: e.dma_start(out=identf[:, :], in_=identf_d), inc=s_ld)
    P.dma(lambda e: e.dma_start(out=wgT[:, :], in_=wgT_d), inc=s_ld)
    P.dma(lambda e: e.dma_start(out=stage[0][0:NEXP, 0:D], in_=b2_d), inc=s_ld)
    P.on("vector", lambda e: e.memset(ones_col[:, :], 1.0))
    P.on("vector", lambda e: e.memset(ones_row[:, :], 1.0))
    B.make_bias_consts([LN_EPS, 1e-8])
    ld0 = (s_ld, s_ld.n)
    P.on("vector", lambda e: e.tensor_copy(out=b2sb[:, :],
                                           in_=stage[0][0:NEXP, 0:D]),
         wait=[ld0])
    emit_wstream(B, wred_d, wrsb, 0, NCH, 16, stage, FREE0)
    P.on("scalar", lambda e: e.activation(out=gt[0:16, 0:8], in_=wgT[:, :],
                                          func=AF.Square), wait=[ld0],
         inc=s_gt, drain=True)
    g1 = (B.sems["gt"], B.sems["gt"].n)
    P.on("tensor", lambda e: e.matmul(pC[0:1, 0:8], ones_col[0:16, :],
                                      gt[0:16, 0:8], start=True, stop=True),
         wait=[g1], inc=s_mm2, fence=True)
    g2 = (s_mm2, s_mm2.n)
    P.on("vector", lambda e: e.tensor_scalar_max(rows[:, 0:8], pC[0:1, 0:8],
                                                 1e-8), wait=[g2],
         inc=s_gt, drain=True)
    g2b = (s_gt, s_gt.n)
    P.on("scalar", lambda e: e.activation(out=rows[:, 0:8], in_=rows[:, 0:8],
                                          func=AF.Ln), wait=[g2b])
    P.on("scalar", lambda e: e.activation(out=rows[:, 0:8], in_=rows[:, 0:8],
                                          func=AF.Exp, scale=-0.5),
         inc=s_gt, drain=True)
    g3 = (B.sems["gt"], B.sems["gt"].n)
    P.on("tensor", lambda e: e.matmul(pC[0:16, 8:16], ones_row[0:1, 0:16],
                                      rows[:, 0:8], start=True, stop=True),
         wait=[g3], inc=s_mm2, fence=True)
    g4 = (s_mm2, s_mm2.n)
    P.on("vector", lambda e: e.tensor_tensor(out=wgnb[:, :], in0=wgT[:, :],
                                             in1=pC[0:16, 8:16], op=ALU.mult),
         wait=[g4], inc=s_gt, drain=True)
    P.block("l0f_consts")

    for g in range(4):
        _l0f_group(B, g, x1, W1_d, b1T_d, W2_d, x2_d,
                   xb, o_acc, w1b, w2b, hb, stage, wrsb, wgnb, b2sb, b1sb,
                   xrTb, mx, mxn, gt, gs, oh_tm, gat, ohb, ssr, maske,
                   x1t, sqb, tmp, tmp2, rows, identf, ones_col, ones_row,
                   pA, pB, pC)
    return B.finish() if standalone else None


def _l0f_group(B, g, x1, W1_d, b1T_d, W2_d, x2_d,
               xb, o_acc, w1b, w2b, hb, stage, wrsb, wgnb, b2sb, b1sb,
               xrTb, mx, mxn, gt, gs, oh_tm, gat, ohb, ssr, maske,
               x1t, sqb, tmp, tmp2, rows, identf, ones_col, ones_row,
               pA, pB, pC):
    nc, P = B.nc, B.P
    seq, t0 = g // 2, (g % 2) * GRP
    s_mm2, s_ge, s_mk, s_oa, s_gt = (B.sems[k] for k in
                                     ("mm2", "ge", "mk", "oa", "gt"))
    s_wcp = B.sem("wcp")

    xbv = xb[:, :].rearrange("p (c s) -> p c s", s=GRP)
    emit_xtiles(B, x1[seq][t0:t0 + GRP, :], GRP // 128,
                lambda tt: xbv[:, :, tt * 128:tt * 128 + 128],
                stage, pA, pB, identf)
    P.block(f"g{g}_xb")
    if DEBUG_L0F and g == 0:
        sd = B.sem("dbgf")
        P.dma(lambda e: e.dma_start(out=B.dbg_xb, in_=xb[:, :]), inc=sd)
        P.on("sync", lambda e, v=sd.n: e.wait_ge(sd.h, v))
        P.block("dbgf_xb")

    for half in range(GRP // 512):
        for kc in range(NCH):
            P.on("tensor", lambda e, half=half, kc=kc: e.matmul(
                pC[0:16, half * 512:half * 512 + 512],
                wrsb[:, kc * 16:(kc + 1) * 16],
                xb[:, kc * GRP + half * 512:kc * GRP + half * 512 + 512],
                start=(kc == 0), stop=(kc == NCH - 1)),
                inc=s_mm2 if kc == NCH - 1 else None,
                fence=(kc == NCH - 1 and half == GRP // 512 - 1))
    xr_mm = (s_mm2, s_mm2.n)
    P.on("vector", lambda e: e.tensor_copy(out=xrTb[:, :], in_=pC[0:16, 0:GRP]),
         wait=[xr_mm], inc=s_gt, drain=True)
    xr_f = (s_gt, s_gt.n)
    for i in range(8):
        P.on("tensor", lambda e, i=i: e.matmul(
            pA[:, i * 8:(i + 1) * 8], xrTb[:, i * 128:(i + 1) * 128],
            wgnb[:, :], start=True, stop=True),
            wait=[xr_f] if i == 0 else [],
            inc=s_mm2 if i == 7 else None, fence=(i == 7))
    lg_f = (s_mm2, s_mm2.n)
    for i in range(8):
        P.on("vector", lambda e, i=i: e.tensor_reduce(
            mx[:, i:i + 1], pA[:, i * 8:(i + 1) * 8],
            axis=mybir.AxisListType.X, op=ALU.max),
            wait=[lg_f] if i == 0 else [])
    for i in range(8):
        P.on("scalar", lambda e, i=i: e.activation(
            out=gt[:, i * 8:(i + 1) * 8], in_=pA[:, i * 8:(i + 1) * 8],
            func=AF.Exp),
            wait=[lg_f] if i == 0 else [],
            inc=s_ge if i == 7 else None, drain=(i == 7))
    gs_f = (s_ge, s_ge.n)
    for i in range(8):
        P.on("vector", lambda e, i=i: e.tensor_reduce(
            gs[:, i:i + 1], gt[:, i * 8:(i + 1) * 8],
            axis=mybir.AxisListType.X, op=ALU.add),
            wait=[gs_f] if i == 0 else [])
    for i in range(8):
        P.on("vector", lambda e, i=i: e.tensor_reduce(
            mxn[:, i:i + 1], gt[:, i * 8:(i + 1) * 8],
            axis=mybir.AxisListType.X, op=ALU.max))
    for i in range(8):
        P.on("vector", lambda e, i=i: e.reciprocal(
            out=gs[:, i:i + 1], in_=gs[:, i:i + 1]))
    for i in range(8):
        P.on("vector", lambda e, i=i: e.tensor_scalar(
            out=oh_tm[:, i * 9:i * 9 + 8], in0=pA[:, i * 8:(i + 1) * 8],
            scalar1=mx[:, i:i + 1], scalar2=None, op0=ALU.is_equal))
        P.on("vector", lambda e, i=i: e.tensor_tensor(
            out=oh_tm[:, i * 9 + 8:i * 9 + 9], in0=mxn[:, i:i + 1],
            in1=gs[:, i:i + 1], op=ALU.mult),
            inc=s_gt if i == 7 else None, drain=(i == 7))
    oh_f = (s_gt, s_gt.n)
    for i in range(8):
        P.on("tensor", lambda e, i=i: e.transpose(
            pC[0:9, i * 128:(i + 1) * 128], oh_tm[:, i * 9:(i + 1) * 9],
            identf[:, :]),
            wait=[oh_f] if i == 0 else [],
            inc=s_mm2 if i == 7 else None, fence=(i == 7))
    tr_f = (s_mm2, s_mm2.n)
    P.on("vector", lambda e: e.tensor_copy(out=gat[:, :], in_=pC[0:9, 0:GRP]),
         wait=[tr_f])
    P.on("vector", lambda e: e.tensor_copy(out=ohb[:, :], in_=gat[0:8, :]),
         inc=s_gt, drain=True)
    gat_f = (s_gt, s_gt.n)
    ssr_f = P.dma(lambda e: e.dma_start(out=ssr[:, :], in_=gat[8:9, :]),
                  wait=[gat_f, B.state.get("ssr_free", FREE0)], inc=B.sem("ld"))
    B.state["ssr_f"] = ssr_f
    if DEBUG_L0F and g == 0:
        sd = B.sem("dbgf")
        P.dma(lambda e: e.dma_start(out=B.dbg_gat, in_=gat[:, :]),
              wait=[gat_f], inc=sd)
        P.dma(lambda e: e.dma_start(out=B.dbg_gs[:, 0:64], in_=gt[:, :]),
              wait=[gat_f], inc=sd)
        P.dma(lambda e: e.dma_start(out=B.dbg_gs[:, 64:72], in_=gs[:, :]),
              wait=[gat_f], inc=sd)
        P.dma(lambda e: e.dma_start(out=B.dbg_gs[:, 72:80], in_=mx[:, :]),
              wait=[gat_f], inc=sd)
        P.on("sync", lambda e, v=sd.n: e.wait_ge(sd.h, v))
    P.block(f"g{g}_gate")

    st_w1 = B.state.setdefault("w1_free", [FREE0, FREE0])
    st_w2 = B.state.setdefault("w2_free", [FREE0])
    st_b1 = B.state.setdefault("b1_free", [FREE0])
    st_hb = B.state.setdefault("hb_free", [FREE0])
    st_me = B.state.setdefault("maske_free", [FREE0])
    s_ld = B.sem("ld")

    w1cast = {}
    w1cast[0] = emit_wstream(B, W1_d[0], w1b[0], 0, NCH, FFH, stage, st_w1[0])
    for e_ in range(NEXP):
        wb = w1b[e_ % 2]
        b1_f = P.dma(lambda e, e_=e_: e.dma_start(out=b1sb[:, :], in_=b1T_d[e_]),
                     wait=[st_b1[0]], inc=s_ld)
        orow = B.orow
        or_f = P.dma(lambda e, e_=e_, orow=orow: e.dma_start(
            out=orow[:, :], in_=gat[e_:e_ + 1, :]),
            wait=[gat_f, B.state.get("orow_free", FREE0)], inc=s_ld)
        for half in range(GRP // 512):
            P.on("tensor", lambda e, e_=e_, half=half: e.matmul(
                pC[:, half * 512:half * 512 + 512], ones_row[0:1, :],
                orow[:, half * 512:half * 512 + 512],
                start=True, stop=True),
                wait=[or_f, st_me[0]] if half == 0 else [],
                inc=s_mm2 if half == 1 else None, fence=(half == 1))
        B.state["orow_free"] = (s_mm2, s_mm2.n)
        mk_mm = (s_mm2, s_mm2.n)
        mk_f = P.on("vector", lambda e: e.tensor_copy(out=maske[:, :],
                                                      in_=pC[:, 0:GRP]),
                    wait=[mk_mm], inc=s_mk, drain=True)
        w2cast = emit_wstream(B, W2_d[e_], w2b, 0, FFH // 128, D, stage,
                              st_w2[0])
        if e_ + 1 < NEXP:
            w1cast[e_ + 1] = emit_wstream(B, W1_d[e_ + 1], w1b[(e_ + 1) % 2],
                                          0, NCH, FFH, stage, st_w1[(e_ + 1) % 2])
        for nt in range(GRP // 512):
            n0 = nt * 512
            for hm in range(16):
                pp, off = (pA, 0) if hm % 2 == 0 else (pA, 512)
                for kc in range(NCH):
                    P.on("tensor", lambda e, wb=wb, pp=pp, off=off, hm=hm,
                         kc=kc, n0=n0: e.matmul(
                             pp[:, off:off + 512],
                             wb[:, kc * FFH + hm * 128:kc * FFH + hm * 128 + 128],
                             xb[:, kc * GRP + n0:kc * GRP + n0 + 512],
                             start=(kc == 0), stop=(kc == NCH - 1)),
                         wait=([w1cast[e_], (s_ge, B.state.get(("hps", hm % 2), 0))]
                               if kc == 0 else []),
                         inc=s_mm2 if kc == NCH - 1 else None,
                         fence=(kc == NCH - 1 and hm == 15))
                h_mm = (s_mm2, s_mm2.n)
                ge_f = P.on("scalar", lambda e, pp=pp, off=off, hm=hm, e_=e_:
                            e.activation(out=hb[:, hm * 512:(hm + 1) * 512],
                                         in_=pp[:, off:off + 512], func=AF.Gelu,
                                         bias=b1sb[:, hm:hm + 1]),
                            wait=[h_mm, b1_f, st_hb[0]], inc=s_ge,
                            drain=(hm == 15))
                B.state[("hps", hm % 2)] = s_ge.n
            ge_all = (s_ge, s_ge.n)
            for hm in range(16):
                mk2 = P.on("vector", lambda e, hm=hm, n0=n0: e.tensor_tensor(
                    out=hb[:, hm * 512:(hm + 1) * 512],
                    in0=hb[:, hm * 512:(hm + 1) * 512],
                    in1=maske[:, n0:n0 + 512], op=ALU.mult),
                    wait=[ge_all, mk_f] if hm == 0 else [],
                    inc=s_mk if hm == 15 else None, drain=(hm == 15))
            mk_all = (s_mk, s_mk.n)
            for om in range(NCH):
                pp, off = (pB, 0) if om % 2 == 0 else (pB, 512)
                for kh in range(16):
                    P.on("tensor", lambda e, pp=pp, off=off, om=om, kh=kh:
                         e.matmul(pp[:, off:off + 512],
                                  w2b[:, kh * D + om * 128:kh * D + om * 128 + 128],
                                  hb[:, kh * 512:(kh + 1) * 512],
                                  start=(kh == 0), stop=(kh == 15)),
                         wait=([mk_all, w2cast,
                                (s_oa, B.state.get(("ops", om % 2), 0))]
                               if kh == 0 else []),
                         inc=s_mm2 if kh == 15 else None,
                         fence=(kh == 15 and om == NCH - 1))
                o_mm = (s_mm2, s_mm2.n)
                if e_ == 0:
                    oa_f = P.on("vector", lambda e, pp=pp, off=off, om=om, n0=n0:
                                e.tensor_copy(
                                    out=o_acc[:, om * GRP + n0:om * GRP + n0 + 512],
                                    in_=pp[:, off:off + 512]),
                                wait=[o_mm], inc=s_oa)
                else:
                    oa_f = P.on("vector", lambda e, pp=pp, off=off, om=om, n0=n0:
                                e.tensor_tensor(
                                    out=o_acc[:, om * GRP + n0:om * GRP + n0 + 512],
                                    in0=pp[:, off:off + 512],
                                    in1=o_acc[:, om * GRP + n0:om * GRP + n0 + 512],
                                    op=ALU.add),
                                wait=[o_mm], inc=s_oa)
                B.state[("ops", om % 2)] = s_oa.n
            st_hb[0] = (s_mm2, s_mm2.n)
        st_w1[e_ % 2] = (s_mm2, s_mm2.n)
        st_w2[0] = (s_mm2, s_mm2.n)
        st_b1[0] = (s_ge, s_ge.n)
        st_me[0] = (s_mk, s_mk.n)
        if DEBUG_L0F and g == 0 and e_ == 0:
            sd = B.sem("dbgf")
            P.dma(lambda e: e.dma_start(out=B.dbg_hb, in_=hb[:, :]),
                  wait=[(B.sems["mk"], B.sems["mk"].n)], inc=sd)
            P.on("sync", lambda e, v=sd.n: e.wait_ge(sd.h, v))
    P.block(f"g{g}_exp")
    if DEBUG_L0F and g == 0:
        sd = B.sem("dbgf")
        P.dma(lambda e: e.dma_start(out=B.dbg_oacc, in_=o_acc[:, :]), inc=sd)
        P.on("sync", lambda e, v=sd.n: e.wait_ge(sd.h, v))
        P.block("dbgf_oacc")

    s_cb = B.sem("cb")
    x1tv = x1t[:, :].rearrange("p (c j) -> p c j", j=128)
    for tt in range(GRP // 128):
        x_f = emit_xtiles(B, x1[seq][t0 + tt * 128:t0 + (tt + 1) * 128, :], 1,
                          lambda _: x1tv, stage, pA, pB, identf,
                          dst_free=(s_cb, s_cb.n))
        for c in range(NCH):
            P.on("tensor", lambda e, c=c, tt=tt: e.matmul(
                pC[:, c * 128:(c + 1) * 128], b2sb[:, c * 128:(c + 1) * 128],
                ohb[:, tt * 128:(tt + 1) * 128], start=True, stop=True),
                wait=[(B.sems["cb"], B.state.get("pc_free", 0))] if c == 0 else [],
                inc=s_mm2 if c == NCH - 1 else None)
        P.on("tensor", lambda e, tt=tt: e.matmul(
            pB[:, 0:128], ones_row[0:1, :], ssr[:, tt * 128:(tt + 1) * 128],
            start=True, stop=True),
            wait=[B.state["ssr_f"]] if tt == 0 else [], inc=s_mm2, fence=True)
        cb_mm = (s_mm2, s_mm2.n)
        for c in range(NCH):
            sl = slice(c * GRP + tt * 128, c * GRP + tt * 128 + 128)
            P.on("vector", lambda e, sl=sl: e.tensor_tensor(
                out=tmp2[:, :], in0=o_acc[:, sl], in1=pB[:, 0:128],
                op=ALU.mult),
                wait=[cb_mm, x_f] if c == 0 else [])
            P.on("vector", lambda e, c=c: e.tensor_tensor(
                out=tmp2[:, :], in0=tmp2[:, :], in1=pC[:, c * 128:(c + 1) * 128],
                op=ALU.add))
            P.on("vector", lambda e, sl=sl, c=c: e.tensor_tensor(
                out=o_acc[:, sl], in0=tmp2[:, :], in1=x1tv[:, c, :],
                op=ALU.add),
                inc=s_cb if c == NCH - 1 else None)
        B.state["pc_free"] = B.sems["cb"].n
    P.block(f"g{g}_comb")
    if DEBUG_L0F and g == 0:
        sd = B.sem("dbgf")
        P.dma(lambda e: e.dma_start(out=B.dbg_comb, in_=o_acc[:, :]), inc=sd)
        P.on("sync", lambda e, v=sd.n: e.wait_ge(sd.h, v))
        P.block("dbgf_comb")

    emit_postln_inplace(B, o_acc, pA, pC, sqb, tmp, rows, ones_col, ones_row,
                        width=GRP)
    P.block(f"g{g}_ln")

    ostage = [stage[0][:, 0:D], stage[1][:, 0:D]]
    emit_store_fm_to_tm(B, o_acc, x2_d[seq][t0:t0 + GRP, :], ostage,
                        pA, pB, identf, width=GRP)
    s_st = B.sem("st")
    P.on("sync", lambda e, v=s_st.n: e.wait_ge(s_st.h, v))
    P.block(f"g{g}_store")


def run_l0f(x1, w_red, wg, W1, b1, W2, b2):
    cc = host_constants()
    nc = build_l0f()
    x1 = np.ascontiguousarray(np.asarray(x1, np.float32).reshape(NCORE, BPC, S, D))
    b1T = np.ascontiguousarray(
        np.asarray(b1, np.float32).reshape(NEXP, FFH // 128, 128).transpose(0, 2, 1))
    base = dict(
        wred=np.ascontiguousarray(np.asarray(w_red, np.float32)),
        wgT=np.ascontiguousarray(np.asarray(wg, np.float32).T),
        W1=np.ascontiguousarray(np.asarray(W1, np.float32)),
        b1T=b1T,
        W2=np.ascontiguousarray(np.asarray(W2, np.float32)),
        b2=np.ascontiguousarray(np.asarray(b2, np.float32)),
        identf=cc["identf"])
    in_maps = [dict(base, x1=x1[i]) for i in range(NCORE)]
    res = run_bass_kernel_spmd(nc, in_maps, core_ids=list(range(NCORE)))
    out = np.concatenate([r["x2"] for r in res.results], axis=0)
    return out.reshape(16, S, D)



STRIPW = 4608


def host_l1_consts():
    cc = host_constants()
    if "stripb" not in cc:
        p = np.arange(128, dtype=np.float64)[:, None]
        c = np.arange(STRIPW, dtype=np.float64)[None, :]
        cc["stripb"] = np.ascontiguousarray(
            (-np.abs(c - p - 2047.0)).astype(np.float32))
        cc["cinf"] = np.ascontiguousarray(
            np.where(c < p + 2047.0, -4.2e6, 0.0).astype(np.float32))
    return cc


DEBUG_L1A = False


def build_l1a(B=None):
    standalone = B is None
    if standalone:
        B = Builder()
    nc, P = B.nc, B.P
    x2 = nc.dram_tensor("x2", [BPC, S, D], F32, kind="ExternalInput").ap()
    wqT_d = nc.dram_tensor("wqT", [D, D], F32, kind="ExternalInput").ap()
    wkT_d = nc.dram_tensor("wkT", [D, D], F32, kind="ExternalInput").ap()
    wvT_d = nc.dram_tensor("wvT", [D, D], F32, kind="ExternalInput").ap()
    woT_d = nc.dram_tensor("woT", [D, D], F32, kind="ExternalInput").ap()
    boT = nc.dram_tensor("boT", [128, NCH], F32, kind="ExternalInput").ap()
    identf_d = nc.dram_tensor("identf", [128, 128], F32, kind="ExternalInput").ap()
    strip_d = nc.dram_tensor("stripb", [128, STRIPW], F32, kind="ExternalInput").ap()
    cinf_d = nc.dram_tensor("cinf", [128, STRIPW], F32, kind="ExternalInput").ap()
    x3_d = nc.dram_tensor("x3", [BPC, S, D], F32, kind="ExternalOutput").ap()
    ctx_d = nc.dram_tensor("ctxd", [NCH, 128, S], BF16,
                           kind="ExternalOutput" if DEBUG_L1A else "Internal").ap()
    if DEBUG_L1A:
        B.dbg_qf = nc.dram_tensor("dbg_qf", [128, NCH * S], BF16,
                                  kind="ExternalOutput").ap()
        B.dbg_kf = nc.dram_tensor("dbg_kf", [128, NCH * S], BF16,
                                  kind="ExternalOutput").ap()
        B.dbg_v65 = nc.dram_tensor("dbg_v65", [128, NTT * H * 65], BF16,
                                   kind="ExternalOutput").ap()
        B.dbg_z = nc.dram_tensor("dbg_z", [128, 1024], F32,
                                 kind="ExternalOutput").ap()
        B.dbg_p = nc.dram_tensor("dbg_p", [128, 1024], BF16,
                                 kind="ExternalOutput").ap()
        B.dbg_st = nc.dram_tensor("dbg_st", [128, 1024], F32,
                                  kind="ExternalOutput").ap()
        B.dbg_pre = nc.dram_tensor("dbg_pre", [128, NCH * S], F32,
                                   kind="ExternalOutput").ap()
        B.dbg_cx = nc.dram_tensor("dbg_cx", [128, NCH * 512], BF16,
                                  kind="ExternalOutput").ap()
        B.dbg_sh = nc.dram_tensor("dbg_sh", [128, 1024], F32,
                                  kind="ExternalOutput").ap()

    s_ld = B.sem("ld")
    identf = B.sb("sidf", [128, 128], F32)
    ones_col = B.sb("ones_col", [128, 1], F32)
    ones_row = B.sb("ones_row", [1, 128], F32)
    bosb = B.sb("bosb", [128, NCH], F32)

    qf = B.sb("qf", [128, NCH * S], BF16)
    kf = B.sb("kf", [128, NCH * S], BF16)
    v65 = B.sb("v65", [128, NTT * H * 65], BF16)

    pA = B.ps("pA", [128, 1024], F32)
    pB = B.ps("pB", [128, 1024], F32)
    pC = B.ps("pC", [128, 1024], F32)
    pF = B.ps("pF", [128, 512], F32)
    B.make_fence(pF, ones_col, identf)

    P.dma(lambda e: e.dma_start(out=identf[:, :], in_=identf_d), inc=s_ld)
    P.dma(lambda e: e.dma_start(out=bosb[:, :], in_=boT), inc=s_ld)

    P.on("vector", lambda e: e.memset(ones_col[:, :], 1.0))
    P.on("vector", lambda e: e.memset(ones_row[:, :], 1.0))
    B.make_bias_consts([LN_EPS, -30.0])
    P.on("sync", lambda e, v=s_ld.n: e.wait_ge(s_ld.h, v))
    P.block("l1_consts")

    for seq in range(BPC):
        _l1a_seq(B, seq, x2, x3_d, ctx_d, wqT_d, wkT_d, wvT_d, woT_d,
                 identf, ones_col, ones_row, bosb, strip_d, cinf_d, qf, kf, v65,
                 pA, pB, pC)
    return B.finish() if standalone else None


def _l1a_seq(B, seq, x2, x3_d, ctx_d, wqT_d, wkT_d, wvT_d, woT_d,
             identf, ones_col, ones_row, bosb, strip_d, cinf_d, qf, kf, v65,
             pA, pB, pC):
    nc, P = B.nc, B.P
    s_qk, s_pj, s_wc = B.sem("qk"), B.sem("pj"), B.sem("wc")
    s_ld = B.sem("ld")

    with ExitStack() as c2:
        en2 = c2.enter_context
        xb = en2(nc.sbuf_tensor(f"xb{seq}", [128, NCH * S], BF16))
        wsb = en2(nc.sbuf_tensor(f"wsb{seq}", [128, NCH * D], BF16))
        stage = [en2(nc.sbuf_tensor(f"stg{seq}_{i}", [128, D], F32))
                 for i in range(2)]

        xbv = xb[:, :].rearrange("p (c s) -> p c s", s=S)
        emit_xtiles(B, x2[seq], NTT, lambda tt: xbv[:, :, tt * 128:tt * 128 + 128],
                    stage, pA, pB, identf)
        P.block(f"q{seq}_xb")

        for wi, (wd, dst) in enumerate(((wqT_d, qf), (wkT_d, kf))):
            w_f = emit_wstream(B, wd, wsb, 0, NCH, D, stage,
                               B.state.get("l1w_free", FREE0))
            slot_free = [FREE0] * 2
            for tg in range(4):
                for mc in range(NCH):
                    pp, off = (pA, 0) if mc % 2 == 0 else (pA, 512)
                    qk_f = None
                    for kc in range(NCH):
                        r = P.on("tensor",
                                 lambda e, pp=pp, off=off, kc=kc, mc=mc, tg=tg:
                                 e.matmul(pp[:, off:off + 512],
                                          wsb[:, kc * D + mc * 128:
                                              kc * D + mc * 128 + 128],
                                          xb[:, kc * S + tg * 512:
                                             kc * S + tg * 512 + 512],
                                          start=(kc == 0), stop=(kc == NCH - 1)),
                                 wait=[w_f, slot_free[mc % 2]] if kc == 0 else [],
                                 inc=s_qk if kc == NCH - 1 else None,
                                 fence=(kc == NCH - 1))
                        qk_f = r or qk_f
                    slot_free[mc % 2] = P.on(
                        "scalar", lambda e, pp=pp, off=off, mc=mc, tg=tg, dst=dst:
                        e.activation(out=dst[:, mc * S + tg * 512:
                                             mc * S + tg * 512 + 512],
                                     in_=pp[:, off:off + 512], func=AF.Copy),
                        wait=[qk_f], inc=s_pj, drain=True)
            B.state["l1w_free"] = (s_qk, s_qk.n)
        w_f = emit_wstream(B, wvT_d, wsb, 0, NCH, D, stage,
                           B.state.get("l1w_free", FREE0))
        P.on("vector", lambda e: e.memset(v65[:, :], 1.0))
        v65v = v65[:, :].rearrange("p (t h c) -> p t h c", h=H, c=65)
        for tt in range(NTT):
            for half in range(2):
                pp = pB if half == 0 else pC
                qk_f = None
                for kc in range(NCH):
                    r = P.on("tensor", lambda e, pp=pp, kc=kc, half=half, tt=tt:
                             e.matmul(pp[:, 0:512],
                                      xb[:, kc * S + tt * 128:kc * S + tt * 128 + 128],
                                      wsb[:, kc * D + half * 512:
                                          kc * D + half * 512 + 512],
                                      start=(kc == 0), stop=(kc == NCH - 1))
                             if False else
                             e.matmul(pp[:, half * 0:512][:, 0:512] if False else pp[:, 0:512],
                                      xb[:, kc * S + tt * 128:kc * S + tt * 128 + 128],
                                      wsb[:, kc * D + half * 512:
                                          kc * D + half * 512 + 512],
                                      start=(kc == 0), stop=(kc == NCH - 1)),
                             wait=[w_f, B.state.get(("vps", half), FREE0)]
                             if kc == 0 else [],
                             inc=s_qk if kc == NCH - 1 else None,
                             fence=(kc == NCH - 1))
                    qk_f = r or qk_f
                ev = P.on("scalar", lambda e, pp=pp, tt=tt, half=half:
                          e.activation(
                              out=v65v[:, tt, half * 8:(half + 1) * 8, 0:64],
                              in_=pp[:, 0:512].rearrange("p (h d) -> p h d", d=64),
                              func=AF.Copy),
                          wait=[qk_f], inc=s_pj, drain=True)
                B.state[("vps", half)] = ev
        P.block(f"q{seq}_proj")
        if DEBUG_L1A and seq == 0:
            sd = B.sem("dbga")
            P.dma(lambda e: e.dma_start(out=B.dbg_qf, in_=qf[:, :]), inc=sd)
            P.dma(lambda e: e.dma_start(out=B.dbg_kf, in_=kf[:, :]), inc=sd)
            P.dma(lambda e: e.dma_start(out=B.dbg_v65, in_=v65[:, :]), inc=sd)
            P.on("sync", lambda e, v=sd.n: e.wait_ge(sd.h, v))
            P.block("dbga_proj")

    with ExitStack() as c3:
        en3 = c3.enter_context
        striph = en3(nc.sbuf_tensor(f"sh{seq}", [128, STRIPW], F32))
        stripb = en3(nc.sbuf_tensor(f"sb{seq}", [128, STRIPW], F32))
        cinf = en3(nc.sbuf_tensor(f"ci{seq}", [128, STRIPW], F32))
        s_ld2 = B.sem("ld")
        st_f = P.dma(lambda e: e.dma_start(out=stripb[:, :], in_=strip_d),
                     wait=[B.state.get("strip_ld_free", FREE0)], inc=s_ld2)
        ci_f = P.dma(lambda e: e.dma_start(out=cinf[:, :], in_=cinf_d),
                     inc=s_ld2)
        B.state["strips_ld"] = (s_ld2, s_ld2.n)
        zb = [en3(nc.sbuf_tensor(f"zb{seq}_{i}", [128, 1024], F32)) for i in range(2)]
        tb = [en3(nc.sbuf_tensor(f"tb{seq}_{i}", [128, 1024], F32)) for i in range(2)]
        ptsb = [en3(nc.sbuf_tensor(f"pt{seq}_{i}", [128, 1024], BF16))
                for i in range(2)]
        ctxs = en3(nc.sbuf_tensor(f"cs{seq}", [64, 1024], BF16))
        bcsb = en3(nc.sbuf_tensor(f"bc{seq}", [65, 1024], F32))
        hs = {
            "pt_free": [FREE0, FREE0],
            "st_free": [FREE0, FREE0],
            "zb_free": [FREE0, FREE0],
            "tb_free": [FREE0, FREE0],
            "ctx_free": FREE0,
            "ctxs_free": FREE0,
            "strip_free": FREE0,
            "bc_free": FREE0,
        }
        for h in range(H):
            _l1a_head(B, seq, h, hs, qf, kf, v65, striph, stripb, cinf, zb, tb,
                      ptsb, ctxs, bcsb, ctx_d, ones_row, pA, pB, pC)
        P.block(f"q{seq}_attn")

    with ExitStack() as c4:
        en4 = c4.enter_context
        xfm = en4(nc.sbuf_tensor(f"xf{seq}", [128, NCH * S], F32))
        wsb = en4(nc.sbuf_tensor(f"wo{seq}", [128, NCH * D], BF16))
        stage = [en4(nc.sbuf_tensor(f"st2{seq}_{i}", [128, D], F32))
                 for i in range(2)]
        ctxsb = en4(nc.sbuf_tensor(f"cx{seq}", [128, NCH * 512], BF16))
        tmp = en4(nc.sbuf_tensor(f"tmp{seq}", [128, 512], F32))
        sqb = [stage[0][:, 0:512], stage[1][:, 0:512]]
        rows = en4(nc.sbuf_tensor(f"rw{seq}", [1, 1536], F32))

        xfv = xfm[:, :].rearrange("p (c s) -> p c s", s=S)
        emit_xtiles(B, x2[seq], NTT, lambda tt: xfv[:, :, tt * 128:tt * 128 + 128],
                    stage, pA, pB, identf)
        w_f = emit_wstream(B, woT_d, wsb, 0, NCH, D, stage,
                           B.state.get("l1w_free", FREE0))
        B.state["l1w_free"] = FREE0
        P.block(f"q{seq}_epi0")

        s_cd = B.sem("cd")
        slots = [(pA, 0), (pA, 512), (pB, 0), (pB, 512)]
        slot_free = [FREE0] * 4
        cx_free = [FREE0]
        idx = 0
        for tg in range(4):
            cx = ctxsb
            ld_f = P.dma(lambda e, cx=cx, tg=tg: e.dma_start(
                out=cx[:, :].rearrange("p (c s) -> p c s", c=NCH),
                in_=ctx_d[:, :, tg * 512:(tg + 1) * 512].rearrange(
                    "c p s -> p c s")),
                wait=[cx_free[0], (s_cd, s_cd.n)], inc=s_ld)
            for mc in range(NCH):
                pp, off = slots[idx % 4]
                qk_f = None
                for kc in range(NCH):
                    r = P.on("tensor",
                             lambda e, pp=pp, off=off, kc=kc, mc=mc, cx=cx:
                             e.matmul(pp[:, off:off + 512],
                                      wsb[:, kc * D + mc * 128:
                                          kc * D + mc * 128 + 128],
                                      cx[:, kc * 512:(kc + 1) * 512],
                                      start=(kc == 0), stop=(kc == NCH - 1)),
                             wait=[ld_f, w_f, slot_free[idx % 4]]
                             if kc == 0 else [],
                             inc=s_qk if kc == NCH - 1 else None,
                             fence=(kc == NCH - 1))
                    qk_f = r or qk_f
                if mc == NCH - 1:
                    cx_free[0] = qk_f
                P.on("vector", lambda e, pp=pp, off=off, mc=mc:
                     e.tensor_scalar(out=tmp[:, :], in0=pp[:, off:off + 512],
                                     scalar1=bosb[:, mc:mc + 1], scalar2=None,
                                     op0=ALU.add),
                     wait=[qk_f])
                slot_free[idx % 4] = P.on(
                    "vector", lambda e, mc=mc, tg=tg:
                    e.tensor_tensor(out=xfm[:, fmc(mc, tg * 512, 512)],
                                    in0=tmp[:, :],
                                    in1=xfm[:, fmc(mc, tg * 512, 512)],
                                    op=ALU.add),
                    inc=s_pj)
                idx += 1
            if DEBUG_L1A and seq == 1 and tg == 3:
                sd = B.sem("dbga")
                P.dma(lambda e: e.dma_start(out=B.dbg_cx, in_=ctxsb[:, :]),
                      wait=[(B.sems["qk"], B.sems["qk"].n)], inc=sd)
                cx_free[0] = (sd, sd.n)
        P.block(f"q{seq}_wo")
        if DEBUG_L1A and seq == 1:
            sd = B.sem("dbga")
            P.dma(lambda e: e.dma_start(out=B.dbg_pre, in_=xfm[:, :]), inc=sd)
            P.on("sync", lambda e, v=sd.n: e.wait_ge(sd.h, v))
            P.block("dbga_pre")

        emit_postln_inplace(B, xfm, pA, pC, sqb, tmp, rows, ones_col, ones_row)
        P.block(f"q{seq}_ln")

        ostage = [stage[0][:, 0:D], stage[1][:, 0:D]]
        emit_store_fm_to_tm(B, xfm, x3_d[seq], ostage, pA, pB, identf)
        s_st = B.sem("st")
        P.on("sync", lambda e, v=s_st.n: e.wait_ge(s_st.h, v))
        P.block(f"q{seq}_store")


def _l1a_head(B, seq, h, hs, qf, kf, v65, striph, stripb, cinf, zb, tb, ptsb,
              ctxs, bcsb, ctx_d, ones_row, pA, pB, pC):
    nc, P = B.nc, B.P
    s_qk, s_ex, s_av, s_ce = (B.sem(n) for n in ("qk2", "ex", "av", "ce"))
    s_zz, s_aa, s_tt = (B.sem(n) for n in ("zz", "aa", "tt2"))
    s_cd = B.sem("cd")
    slope = 2.0 ** (-(h + 1) * 8.0 / H)

    c = h // 2
    pbase = 64 * (h % 2)
    qsl = lambda t0, n: qf[pbase:pbase + 64, c * S + t0:c * S + t0 + n]
    ksl = lambda t0, n: kf[pbase:pbase + 64, c * S + t0:c * S + t0 + n]

    P.on("vector", lambda e: e.tensor_scalar_mul(
        striph[:, :], stripb[:, :], slope / 240.0),
        wait=[hs["strip_free"], B.state["strips_ld"]])
    sp_f = P.on("vector", lambda e: e.tensor_tensor(
        out=striph[:, :], in0=striph[:, :], in1=cinf[:, :], op=ALU.add),
        inc=s_zz, drain=True)

    for qcp in range(2):
        qlo = 2 * qcp
        q0 = qlo * 512
        ktmax = 4 * (qlo + 1) + 3

        def emit_qk(kt):
            stp = pA if kt % 2 == 0 else pB
            qk_f = None
            for qc2 in range(2):
                r = P.on("tensor", lambda e, stp=stp, kt=kt, qc2=qc2, q0=q0:
                         e.matmul(stp[:, qc2 * 512:qc2 * 512 + 512],
                                  ksl(kt * 128, 128),
                                  qsl(q0 + qc2 * 512, 512),
                                  start=True, stop=True),
                         wait=[hs["st_free"][kt % 2]] if qc2 == 0 else [],
                         inc=s_qk if qc2 == 1 else None)
                qk_f = r or qk_f
            return qk_f

        def emit_z(kt, qk_f):
            stp = pA if kt % 2 == 0 else pB
            z = zb[kt % 2]
            toff = 2047 - kt * 128 + q0
            zz_f = P.on("vector", lambda e, stp=stp, z=z, toff=toff:
                        e.scalar_tensor_tensor(
                            out=z[:, :], in0=stp[:, :], scalar=1.0 / 240.0,
                            in1=striph[:, toff:toff + 1024],
                            op0=ALU.mult, op1=ALU.add),
                        wait=[qk_f, sp_f, hs["zb_free"][kt % 2]],
                        inc=s_zz, drain=True)
            hs["st_free"][kt % 2] = zz_f
            return zz_f

        def emit_tanh_exp(kt, aa_f):
            z = zb[kt % 2]
            t = tb[kt % 2]
            pt = ptsb[kt % 2]
            P.on("scalar", lambda e, z=z, t=t: e.activation(
                out=t[:, :], in_=z[:, :], func=AF.Tanh),
                wait=[aa_f, hs["tb_free"][kt % 2]])
            ex_f = P.on("scalar", lambda e, t=t, pt=pt: e.activation(
                out=pt[:, :], in_=t[:, :], func=AF.Exp, scale=30.0,
                bias=B.bias(-30.0)),
                wait=[hs["pt_free"][kt % 2]], inc=s_ex, drain=True)
            hs["zb_free"][kt % 2] = ex_f
            hs["tb_free"][kt % 2] = ex_f
            return ex_f

        def emit_av(kt, ex_f):
            pt = ptsb[kt % 2]
            av_f = None
            for qc2 in range(2):
                r = P.on("tensor",
                         lambda e, pt=pt, kt=kt, qc2=qc2, ktmax=ktmax:
                         e.matmul(pC[0:65, qc2 * 512:qc2 * 512 + 512],
                                  v65[:, (kt * H + h) * 65:(kt * H + h) * 65 + 65],
                                  pt[:, qc2 * 512:qc2 * 512 + 512],
                                  start=(kt == 0), stop=(kt == ktmax)),
                         wait=[ex_f, hs["ctx_free"]] if qc2 == 0 else [],
                         inc=s_av if qc2 == 1 else None,
                         fence=(qc2 == 1 and kt == ktmax))
                av_f = r or av_f
            hs["pt_free"][kt % 2] = av_f
            return av_f

        exv = {}
        qk0 = emit_qk(0)
        z0_f = emit_z(0, qk0)
        exv[0] = emit_tanh_exp(0, z0_f)
        if DEBUG_L1A and seq == 0 and h == 0 and qcp == 0:
            sd = B.sem("dbga")
            P.dma(lambda e: e.dma_start(out=B.dbg_z, in_=zb[0][:, :]),
                  wait=[z0_f], inc=sd)
            P.dma(lambda e: e.dma_start(out=B.dbg_p, in_=ptsb[0][:, :]),
                  wait=[exv[0]], inc=sd)
            P.dma(lambda e: e.dma_start(out=B.dbg_sh,
                                        in_=striph[:, 2047:2047 + 1024]),
                  wait=[sp_f], inc=sd)
            stc = P.on("vector", lambda e: e.tensor_copy(out=tb[1][:, :],
                                                         in_=pA[:, :]),
                       wait=[qk0], inc=B.sem("ce"), drain=True)
            hs["st_free"][0] = stc
            P.dma(lambda e: e.dma_start(out=B.dbg_st, in_=tb[1][:, :]),
                  wait=[stc], inc=sd)
            hs["tb_free"][1] = (B.sems["dbga"], B.sems["dbga"].n)
            hs["zb_free"][0] = (B.sems["dbga"], B.sems["dbga"].n)
        for kt in range(1, ktmax + 1):
            qkk = emit_qk(kt)
            emit_av(kt - 1, exv[kt - 1])
            exv[kt] = emit_tanh_exp(kt, emit_z(kt, qkk))
        av_last = emit_av(ktmax, exv[ktmax])

        P.on("vector", lambda e: e.tensor_copy(out=bcsb[0:1, :],
                                               in_=pC[64:65, :]),
             wait=[av_last, hs["bc_free"]])
        rc_f = P.on("vector", lambda e: e.reciprocal(
            out=bcsb[0:1, :], in_=bcsb[0:1, :]), inc=s_ce, drain=True)
        bc_f = None
        for qc2 in range(2):
            r = P.on("tensor", lambda e, qc2=qc2: e.matmul(
                pA[0:64, qc2 * 512:qc2 * 512 + 512], ones_row[0:1, 0:64],
                bcsb[0:1, qc2 * 512:qc2 * 512 + 512], start=True, stop=True),
                wait=[rc_f, hs["st_free"][0]] if qc2 == 0 else [],
                inc=s_qk if qc2 == 1 else None, fence=(qc2 == 1))
            bc_f = r or bc_f
        cp_f = P.on("vector", lambda e: e.tensor_copy(
            out=bcsb[0:64, :], in_=pA[0:64, :]),
            wait=[bc_f], inc=s_ce)
        hs["st_free"][0] = cp_f
        hs["bc_free"] = (B.sems["qk2"], B.sems["qk2"].n)
        sc_f = P.on("vector", lambda e: e.tensor_tensor(
            out=ctxs[:, :], in0=pC[0:64, :], in1=bcsb[0:64, :], op=ALU.mult),
            wait=[hs["ctxs_free"]], inc=s_ce, drain=True)
        hs["ctx_free"] = sc_f
        hs["ctxs_free"] = P.dma(
            lambda e, c=c, pbase=pbase, q0=q0: e.dma_start(
                out=ctx_d[c, pbase:pbase + 64, q0:q0 + 1024], in_=ctxs[:, :]),
            wait=[sc_f], inc=s_cd)
    hs["strip_free"] = (B.sems["zz"], B.sems["zz"].n)


def run_l1a(x2, Wq, Wk, Wv, Wo, bo):
    cc = host_l1_consts()
    nc = build_l1a()
    x2 = np.ascontiguousarray(np.asarray(x2, np.float32).reshape(NCORE, BPC, S, D))
    boT = np.ascontiguousarray(np.asarray(bo, np.float32).reshape(NCH, 128).T)
    base = dict(
        wqT=np.ascontiguousarray(np.asarray(Wq, np.float32).T),
        wkT=np.ascontiguousarray(np.asarray(Wk, np.float32).T),
        wvT=np.ascontiguousarray(np.asarray(Wv, np.float32).T),
        woT=np.ascontiguousarray(np.asarray(Wo, np.float32).T),
        boT=boT, identf=cc["identf"], stripb=cc["stripb"], cinf=cc["cinf"])
    in_maps = [dict(base, x2=x2[i]) for i in range(NCORE)]
    res = run_bass_kernel_spmd(nc, in_maps, core_ids=list(range(NCORE)))
    out = np.concatenate([r["x3"] for r in res.results], axis=0)
    return out.reshape(16, S, D)



AUXN = 4096


def build_l1f(B=None):
    standalone = B is None
    if standalone:
        B = Builder()
    nc, P = B.nc, B.P
    x3 = nc.dram_tensor("x3", [BPC, S, D], F32, kind="ExternalInput").ap()
    qpT_d = nc.dram_tensor("qpT", [D, 128], F32, kind="ExternalInput").ap()
    rwT_d = nc.dram_tensor("rwT", [128, 48], F32, kind="ExternalInput").ap()
    akT_d = nc.dram_tensor("akT", [128, AUXN], F32, kind="ExternalInput").ap()
    av_d = nc.dram_tensor("av", [AUXN, D], F32, kind="ExternalInput").ap()
    rmask_d = nc.dram_tensor("rmask", [1, AUXN], F32, kind="ExternalInput").ap()
    uw1T_d = nc.dram_tensor("uw1T", [D, 256], F32, kind="ExternalInput").ap()
    b1uT_d = nc.dram_tensor("b1uT", [128, 2], F32, kind="ExternalInput").ap()
    uw2T_d = nc.dram_tensor("uw2T", [128, 2], F32, kind="ExternalInput").ap()
    scal_d = nc.dram_tensor("scal", [1, 4], F32, kind="ExternalInput").ap()
    identf_d = nc.dram_tensor("identf", [128, 128], F32, kind="ExternalInput").ap()
    identb_d = nc.dram_tensor("identb", [128, 128], BF16, kind="ExternalInput").ap()
    out_d = nc.dram_tensor("out", [BPC, S, D], F32, kind="ExternalOutput").ap()
    ccin_t = nc.dram_tensor("ccin", [1, 128], F32)
    ccout_t = nc.dram_tensor("ccout", [1, 128], F32)
    ccin_d = ccin_t.ap()
    ccout_d = ccout_t.ap()

    s_ld = B.sem("ld")
    s_mm = B.sem("mm")
    s_ev = B.sem("ev")
    s_vv = B.sem("vv")
    s_cc = B.sem("cc")

    identf = B.sb("sidf", [128, 128], F32)
    identb = B.sb("sidb", [128, 128], BF16)
    ones_col = B.sb("ones_col", [128, 1], F32)
    ones_row = B.sb("ones_row", [1, 128], F32)
    avb = B.sb("avb", [128, (AUXN // 128) * D], BF16)
    rkm = B.sb("rkm", [49, AUXN], F32)
    qpw = B.sb("qpw", [128, NCH * 128], F32)
    rwb = B.sb("rwb", [128, 48], F32)
    uw1b = B.sb("uw1b", [128, NCH * 256], BF16)
    uw2b = B.sb("uw2b", [128, 2], BF16)
    b1u = B.sb("b1u", [128, 2], F32)
    scal = B.sb("scalb", [1, 8], F32)
    lvall = B.sb("lvall", [1, BPC * S], F32)
    x3t = B.sb("x3t", [128, NCH * 128], F32)
    x3b = B.sb("x3b", [128, NCH * 128], BF16)
    qb = B.sb("qb", [128, 128], F32)
    rqa = B.sb("rqa", [49, 128], F32)
    ssb = B.sb("ssb", [128, AUXN], F32)
    m8 = B.sb("m8", [128, 8], F32)
    e8 = B.sb("e8", [128, 16], F32)
    mneg = B.sb("mneg", [128, 2], F32)
    eb = B.sb("ebb", [128, AUXN], BF16)
    msk = B.sb("msk", [128, AUXN], BF16)
    ptb = B.sb("ptb", [128, AUXN], BF16)
    h1b = B.sb("h1b", [128, 256], BF16)
    rws = B.sb("rws", [1, 1024 + 1536], F32)
    gbs = B.sb("gbs", [128, 128], F32)
    tmp2 = B.sb("tmp2", [128, 512], F32)
    sepb = B.sb("sepb", [128, 512], F32)
    sq1 = B.sb("sq1", [128, 512], F32)
    stage = [B.sb(f"stg{i}", [128, D], F32) for i in range(2)]
    ostage = [B.sb(f"ost{i}", [128, D], F32) for i in range(2)]

    pA = B.ps("pA", [128, 1024], F32)
    pB = B.ps("pB", [128, 1024], F32)
    pC = B.ps("pC", [128, 1024], F32)
    pT0 = B.ps("pT0", [128, 1024], BF16)
    pF = B.ps("pF", [128, 512], F32)
    B.make_fence(pF, ones_col, identf)

    P.dma(lambda e: e.dma_start(out=identf[:, :], in_=identf_d), inc=s_ld)
    P.dma(lambda e: e.dma_start(out=identb[:, :], in_=identb_d), inc=s_ld)
    P.dma(lambda e: e.dma_start(out=rwb[:, :], in_=rwT_d), inc=s_ld)
    P.dma(lambda e: e.dma_start(out=b1u[:, :], in_=b1uT_d), inc=s_ld)
    P.dma(lambda e: e.dma_start(out=scal[0:1, 0:4], in_=scal_d), inc=s_ld)
    P.dma(lambda e: e.dma_start(out=rkm[48:49, :], in_=rmask_d), inc=s_ld)
    P.dma(lambda e: e.dma_start(
        out=qpw[:, :].rearrange("p (c j) -> p c j", j=128),
        in_=qpT_d.rearrange("(c p) j -> p c j", p=128)), inc=s_ld)
    P.dma(lambda e: e.dma_start(out=stage[1][:, 0:2], in_=uw2T_d), inc=s_ld)
    ak_f = P.dma(lambda e: e.dma_start(out=ssb[:, :], in_=akT_d), inc=s_ld)
    ld0 = (s_ld, s_ld.n)
    P.on("vector", lambda e: e.memset(ones_col[:, :], 1.0))
    P.on("vector", lambda e: e.memset(ones_row[:, :], 1.0))
    P.on("vector", lambda e: e.memset(rqa[:, :], 1.0))
    P.on("vector", lambda e: e.tensor_copy(out=uw2b[:, :], in_=stage[1][:, 0:2]),
         wait=[ld0], inc=s_vv, drain=True)
    B.make_bias_consts([LN_EPS, 1.0, 1e-6])
    st0f = (s_vv, s_vv.n)
    B.state[("stage", id(stage[0]))] = [st0f, st0f]
    emit_wstream(B, uw1T_d, uw1b, 0, NCH, 256, stage, FREE0)
    emit_wstream(B, av_d, avb, 0, AUXN // 128, D, stage, FREE0)
    rk_ev = FREE0
    for half in range(NCH):
        prev = B.state.get(("rk_ev", half % 2), FREE0)
        r = P.on("tensor", lambda e, half=half: e.matmul(
            pA[0:48, (half % 2) * 512:(half % 2) * 512 + 512],
            rwb[:, :], ssb[:, half * 512:(half + 1) * 512],
            start=True, stop=True),
            wait=[ld0, prev], inc=s_mm, fence=True)
        rk_ev = P.on("scalar", lambda e, half=half: e.activation(
            out=rkm[0:48, half * 512:(half + 1) * 512],
            in_=pA[0:48, (half % 2) * 512:(half % 2) * 512 + 512],
            func=AF.Copy),
            wait=[r], inc=s_ev, drain=(half == NCH - 1))
        B.state[("rk_ev", half % 2)] = rk_ev
    P.block("f_consts")

    x3tv = x3t[:, :].rearrange("p (c j) -> p c j", j=128)
    sqt = tmp2[:, 0:128]
    s_rw = B.sem("rw")
    for gt in range(BPC * NTT):
        seq, tt = gt // NTT, gt % NTT
        x_f = emit_xtiles(B, x3[seq][tt * 128:(tt + 1) * 128, :], 1,
                          lambda _: x3tv, stage, pA, pB, identf,
                          dst_free=(s_mm, B.state.get("lv_mm", 0)))
        for c in range(NCH):
            P.on("tensor", lambda e, c=c: e.matmul(
                pC[0:1, 0:128], ones_col[:, :], x3tv[:, c, :],
                start=(c == 0), stop=(c == NCH - 1)),
                wait=[x_f, (s_rw, B.state.get("lv_rw", 0))] if c == 0 else [])
        sq_f = P.on("scalar", lambda e: e.activation(
            out=sq1[:, 0:512].rearrange("p (c j) -> p c j", j=128),
            in_=x3tv[:, 0:4, :], func=AF.Square),
            wait=[x_f, (s_mm, B.state.get("lv_mm", 0))])
        sq_f = P.on("scalar", lambda e: e.activation(
            out=sepb[:, 0:512].rearrange("p (c j) -> p c j", j=128),
            in_=x3tv[:, 4:8, :], func=AF.Square),
            inc=s_rw, drain=True)
        for c in range(NCH):
            src = sq1 if c < 4 else sepb
            P.on("tensor", lambda e, c=c, src=src: e.matmul(
                pC[0:1, 512:640], ones_col[:, :],
                src[:, (c % 4) * 128:(c % 4) * 128 + 128],
                start=(c == 0), stop=(c == NCH - 1)),
                wait=[sq_f] if c == 0 else [],
                inc=s_mm if c == NCH - 1 else None,
                fence=(c == NCH - 1))
        B.state["lv_mm"] = s_mm.n
        mm_f = (s_mm, s_mm.n)
        P.on("scalar", lambda e: e.activation(
            out=rws[0:1, 0:128], in_=pC[0:1, 0:128], func=AF.Copy, scale=1.0 / D),
            wait=[mm_f])
        r2 = P.on("scalar", lambda e: e.activation(
            out=rws[0:1, 128:256], in_=pC[0:1, 512:640], func=AF.Copy,
            scale=1.0 / D), inc=s_rw, drain=True)
        B.state["lv_rw"] = s_rw.n
        P.on("vector", lambda e: e.tensor_tensor(
            out=rws[0:1, 256:384], in0=rws[0:1, 0:128], in1=rws[0:1, 0:128],
            op=ALU.mult), wait=[r2])
        v_f = P.on("vector", lambda e: e.tensor_tensor(
            out=rws[0:1, 128:256], in0=rws[0:1, 128:256], in1=rws[0:1, 256:384],
            op=ALU.subtract), inc=s_vv, drain=True)
        P.on("scalar", lambda e, gt=gt: e.activation(
            out=lvall[0:1, gt * 128:(gt + 1) * 128], in_=rws[0:1, 128:256],
            func=AF.Ln, bias=B.bias(1.0)[0:1, :]),
            wait=[v_f])
    P.block("f_lv")
    ps_f = P.on("vector", lambda e: e.tensor_reduce(
        rws[0:1, 512:513], lvall[0:1, :], axis=mybir.AxisListType.X, op=ALU.add),
        inc=s_vv)
    P.on("vector", lambda e: e.memset(rws[0:1, 513:640], 0.0),
         inc=s_vv, drain=True)
    ps2 = (s_vv, s_vv.n)
    dma_f = P.dma(lambda e: e.dma_start(out=ccin_d, in_=rws[0:1, 512:640]),
                  wait=[ps2], inc=s_ld)
    cc_f = P.on("gpsimd", lambda e: e.collective_compute(
        "AllReduce", ALU.add, replica_groups=[list(range(NCORE))],
        ins=[ccin_t.ap().opt()], outs=[ccout_t.ap().opt()]),
        wait=[dma_f], inc=s_cc, inc_n=1)
    gm_f = P.dma(lambda e: e.dma_start(out=rws[0:1, 640:768], in_=ccout_d),
                 wait=[cc_f], inc=s_ld)
    gm2 = P.on("scalar", lambda e: e.activation(
        out=scal[0:1, 4:5], in_=rws[0:1, 640:641], func=AF.Copy,
        scale=1.0 / 32768.0), wait=[gm_f], inc=s_ev, drain=True)
    P.on("vector", lambda e: e.tensor_scalar_add(scal[0:1, 4:5],
                                                 scal[0:1, 4:5], 1e-6),
         wait=[gm2])
    P.on("vector", lambda e: e.reciprocal(out=scal[0:1, 4:5],
                                          in_=scal[0:1, 4:5]))
    P.on("vector", lambda e: e.tensor_scalar_mul(scal[0:1, 5:6],
                                                 scal[0:1, 4:5], 0.5),
         inc=s_vv, drain=True)
    P.block("f_gm")

    for gt in range(BPC * NTT):
        _l1f_tile(B, gt, x3, out_d, identf, identb, ones_col, ones_row,
                  avb, rkm, qpw, rwb, uw1b, uw2b, b1u, scal, lvall,
                  x3t, x3b, qb, rqa, ssb, m8, e8, mneg, eb, msk, ptb, h1b,
                  rws, gbs, tmp2, sq1, stage, ostage,
                  pA, pB, pC, pT0, pF)
    s_st = B.sem("st")
    P.on("sync", lambda e, v=s_st.n: e.wait_ge(s_st.h, v))
    P.block("f_done")
    return B.finish() if standalone else None


def _l1f_tile(B, gt, x3, out_d, identf, identb, ones_col, ones_row,
              avb, rkm, qpw, rwb, uw1b, uw2b, b1u, scal, lvall,
              x3t, x3b, qb, rqa, ssb, m8, e8, mneg, eb, msk, ptb, h1b,
              rws, gbs, tmp2, sq1, stage, ostage,
              pA, pB, pC, pT0, pF):
    nc, P = B.nc, B.P
    seq, tt = gt // NTT, gt % NTT
    s_mm, s_ev, s_vv = B.sem("mm2"), B.sem("ev2"), B.sem("vv2")
    s_st = B.sem("st")
    st8 = B.state

    x3tv = x3t[:, :].rearrange("p (c j) -> p c j", j=128)
    x_f = emit_xtiles(B, x3[seq][tt * 128:(tt + 1) * 128, :], 1,
                      lambda _: x3tv, stage, pA, pB, identf,
                      dst_free=st8.get("f_x3t", FREE0))
    xb_f = P.on("vector", lambda e: e.tensor_copy(out=x3b[:, :], in_=x3t[:, :]),
                wait=[x_f, st8.get("f_x3b", FREE0)], inc=s_vv, drain=True)

    for kc in range(NCH):
        P.on("tensor", lambda e, kc=kc: e.matmul(
            pC[:, 0:128], qpw[:, kc * 128:(kc + 1) * 128],
            x3tv[:, kc, :], start=(kc == 0), stop=(kc == NCH - 1)),
            wait=[x_f, st8.get("f_pCq", FREE0)] if kc == 0 else [],
            inc=s_mm if kc == NCH - 1 else None, fence=(kc == NCH - 1))
    q_mm = (s_mm, s_mm.n)
    qb_f = P.on("scalar", lambda e: e.activation(
        out=qb[:, :], in_=pC[:, 0:128], func=AF.Copy),
        wait=[q_mm], inc=s_ev, drain=True)
    rq_mm = P.on("tensor", lambda e: e.matmul(
        pC[0:48, 128:256], rwb[:, :], qb[:, :], start=True, stop=True),
        wait=[qb_f], inc=s_mm, fence=True)
    rq_f = P.on("scalar", lambda e: e.activation(
        out=rqa[0:48, :], in_=pC[0:48, 128:256], func=AF.Copy,
        scale=1.0 / (48.0 ** 0.5)),
        wait=[rq_mm], inc=s_ev, drain=True)
    sc_last = None
    for i in range(NCH):
        r = P.on("tensor", lambda e, i=i: e.matmul(
            pA[:, (i % 2) * 512:(i % 2) * 512 + 512],
            rqa[:, :], rkm[:, i * 512:(i + 1) * 512],
            start=True, stop=True),
            wait=[rq_f, st8.get(("f_pAs", i % 2), FREE0)],
            inc=s_mm, fence=True)
        sc_last = P.on("scalar", lambda e, i=i: e.activation(
            out=ssb[:, i * 512:(i + 1) * 512],
            in_=pA[:, (i % 2) * 512:(i % 2) * 512 + 512], func=AF.Copy),
            wait=[r, st8.get("f_ssb", FREE0) if i == 0 else FREE0],
            inc=s_ev, drain=(i == NCH - 1))
        st8[("f_pAs", i % 2)] = sc_last
    mx_f = P.on("vector", lambda e: e.max(out=m8[:, :], in_=ssb[:, :]),
                wait=[sc_last, st8.get("f_m8", FREE0)], inc=s_vv, drain=True)
    mn_f = P.on("vector", lambda e: e.tensor_scalar_mul(
        mneg[:, 0:1], m8[:, 0:1], -1.0), wait=[mx_f], inc=s_vv, drain=True)
    mk_f = P.on("vector", lambda e: e.tensor_scalar(
        out=msk[:, :], in0=ssb[:, :], scalar1=m8[:, 7:8], scalar2=None,
        op0=ALU.is_ge),
        wait=[st8.get("f_msk", FREE0)], inc=s_vv)
    eb_f = P.on("scalar", lambda e: e.activation(
        out=eb[:, :], in_=ssb[:, :], func=AF.Exp, bias=mneg[:, 0:1]),
        wait=[mn_f, st8.get("f_eb", FREE0)], inc=s_ev, drain=True)
    p_f = P.on("vector", lambda e: e.tensor_tensor(
        out=msk[:, :], in0=eb[:, :], in1=msk[:, :], op=ALU.mult),
        wait=[eb_f], inc=s_vv, drain=True)
    st8["f_ssb"] = p_f
    st8["f_eb"] = p_f
    e8_f = P.on("scalar", lambda e: e.activation(
        out=e8[:, 0:8], in_=m8[:, 0:8], func=AF.Exp, bias=mneg[:, 0:1]),
        inc=s_ev, drain=True)
    st8["f_m8"] = e8_f
    den_f = P.on("vector", lambda e: e.tensor_reduce(
        e8[:, 8:9], e8[:, 0:8], axis=mybir.AxisListType.X, op=ALU.add),
        wait=[e8_f], inc=s_vv, drain=True)
    ev_prev = st8.get("f_ptb", FREE0)
    for rnd in range(4):
        tp_f = None
        for j in range(8):
            r = P.on("tensor", lambda e, rnd=rnd, j=j: e.transpose(
                pT0[:, j * 128:(j + 1) * 128],
                msk[:, (rnd * 8 + j) * 128:(rnd * 8 + j) * 128 + 128],
                identb[:, :]),
                wait=[p_f, ev_prev] if j == 0 else [],
                inc=s_mm if j == 7 else None, fence=(j == 7))
            tp_f = r or tp_f
        ev_prev = P.on("vector", lambda e, rnd=rnd: e.tensor_copy(
            out=ptb[:, rnd * 1024:(rnd + 1) * 1024], in_=pT0[:, :]),
            wait=[tp_f], inc=s_vv, drain=True)
    ptb_f = ev_prev
    st8["f_msk"] = ptb_f
    for cm in range(NCH):
        for ic in range(AUXN // 128):
            P.on("tensor", lambda e, cm=cm, ic=ic: e.matmul(
                pB[:, cm * 128:(cm + 1) * 128],
                avb[:, ic * D + cm * 128:ic * D + cm * 128 + 128],
                ptb[:, ic * 128:(ic + 1) * 128],
                start=(ic == 0), stop=(ic == AUXN // 128 - 1)),
                wait=[ptb_f, st8.get("f_pB", FREE0)]
                if ic == 0 and cm == 0 else [],
                inc=s_mm if (ic == AUXN // 128 - 1 and cm == NCH - 1) else None,
                fence=(ic == AUXN // 128 - 1 and cm == NCH - 1))
    agg_mm = (s_mm, s_mm.n)
    st8["f_ptb"] = agg_mm
    for fc in range(2):
        for kc in range(NCH):
            P.on("tensor", lambda e, fc=fc, kc=kc: e.matmul(
                pC[:, 256 + fc * 128:256 + fc * 128 + 128],
                uw1b[:, kc * 256 + fc * 128:kc * 256 + fc * 128 + 128],
                x3b[:, kc * 128:(kc + 1) * 128],
                start=(kc == 0), stop=(kc == NCH - 1)),
                wait=[xb_f, st8.get("f_pCh", FREE0)]
                if kc == 0 and fc == 0 else [],
                inc=s_mm if (kc == NCH - 1 and fc == 1) else None,
                fence=(kc == NCH - 1 and fc == 1))
    h_mm = (s_mm, s_mm.n)
    h_f = None
    for fc in range(2):
        h_f = P.on("scalar", lambda e, fc=fc: e.activation(
            out=h1b[:, fc * 128:(fc + 1) * 128],
            in_=pC[:, 256 + fc * 128:256 + fc * 128 + 128],
            func=AF.Gelu, bias=b1u[:, fc:fc + 1]),
            wait=[h_mm, st8.get("f_h1b", FREE0)] if fc == 0 else [],
            inc=s_ev, drain=(fc == 1))
    st8["f_pCh"] = h_f
    l_mm = None
    for fc in range(2):
        l_mm = P.on("tensor", lambda e, fc=fc: e.matmul(
            pC[0:1, 640:768], uw2b[:, fc:fc + 1],
            h1b[:, fc * 128:(fc + 1) * 128],
            start=(fc == 0), stop=(fc == 1)),
            wait=[h_f, st8.get("f_pCl", FREE0)] if fc == 0 else [],
            inc=s_mm if fc == 1 else None, fence=(fc == 1))
    st8["f_h1b"] = l_mm
    sg_f = P.on("scalar", lambda e: e.activation(
        out=rws[0:1, 640:768], in_=pC[0:1, 640:768], func=AF.Sigmoid,
        bias=scal[0:1, 0:1]),
        wait=[l_mm], inc=s_ev, drain=True)
    st8["f_pCl"] = sg_f
    t1_f = P.on("vector", lambda e: e.tensor_scalar_mul(
        rws[0:1, 768:896], rws[0:1, 640:768], 2.5), wait=[sg_f])
    nv_f = P.on("vector", lambda e, gt=gt: e.tensor_scalar(
        out=rws[0:1, 256:384], in0=lvall[0:1, gt * 128:(gt + 1) * 128],
        scalar1=scal[0:1, 5:6], scalar2=None, op0=ALU.mult))
    u_f = P.on("vector", lambda e: e.tensor_tensor(
        out=rws[0:1, 768:896], in0=rws[0:1, 768:896], in1=rws[0:1, 256:384],
        op=ALU.add))
    cl_f = P.on("vector", lambda e: e.tensor_scalar(
        out=rws[0:1, 768:896], in0=rws[0:1, 768:896], scalar1=5.0, scalar2=0.0,
        op0=ALU.min, op1=ALU.max), inc=s_vv, drain=True)
    g_f = P.on("scalar", lambda e: e.activation(
        out=rws[0:1, 896:1024], in_=rws[0:1, 768:896], func=AF.Sigmoid,
        scale=scal[0:1, 1:2], bias=scal[0:1, 2:3]),
        wait=[cl_f], inc=s_ev, drain=True)
    dt_f = P.on("tensor", lambda e: e.transpose(
        pC[0:1, 768:896], e8[:, 8:9], identf[:, :]),
        wait=[den_f, sg_f, st8.get("f_pCd", FREE0)], inc=s_mm, fence=True)
    dr_f = P.on("vector", lambda e: e.tensor_copy(
        out=rws[0:1, 384:512], in_=pC[0:1, 768:896]),
        wait=[dt_f], inc=s_vv)
    P.on("vector", lambda e: e.reciprocal(
        out=rws[0:1, 384:512], in_=rws[0:1, 384:512]))
    gr_f = P.on("vector", lambda e: e.tensor_tensor(
        out=rws[0:1, 384:512], in0=rws[0:1, 384:512], in1=rws[0:1, 896:1024],
        op=ALU.mult), wait=[g_f], inc=s_vv, drain=True)
    gb_mm = P.on("tensor", lambda e: e.matmul(
        pC[:, 896:1024], ones_row[0:1, :], rws[0:1, 384:512],
        start=True, stop=True),
        wait=[gr_f], inc=s_mm, fence=True)
    gb_f = P.on("vector", lambda e: e.tensor_copy(
        out=gbs[:, :], in_=pC[:, 896:1024]),
        wait=[gb_mm], inc=s_vv, drain=True)
    st8["f_pCd"] = gb_f
    cb_f = None
    for c in range(NCH):
        P.on("vector", lambda e, c=c: e.tensor_tensor(
            out=tmp2[:, 0:128], in0=pB[:, c * 128:(c + 1) * 128], in1=gbs[:, :],
            op=ALU.mult),
            wait=[agg_mm, gb_f] if c == 0 else [])
        cb_f = P.on("vector", lambda e, c=c: e.scalar_tensor_tensor(
            out=x3t[:, c * 128:(c + 1) * 128], in0=x3t[:, c * 128:(c + 1) * 128],
            scalar=2.0, in1=tmp2[:, 0:128], op0=ALU.mult, op1=ALU.add),
            inc=s_vv if c == NCH - 1 else None, drain=(c == NCH - 1))
    st8["f_pB"] = cb_f
    emit_postln_inplace(B, x3t, pA, pC, [sq1[:, 0:128], sq1[:, 128:256]],
                        tmp2[:, 256:384], rws[0:1, 1024:2560],
                        ones_col, ones_row, width=128,
                        xfm_ready=cb_f, final_drain=True)
    s_ap = B.sem("ap")
    ln_f = (s_ap, s_ap.n)
    st8["f_pCq"] = (B.sem("rw"), B.sem("rw").n)
    ot = ostage[gt % 2]
    tp_f = None
    for c in range(NCH):
        r = P.on("tensor", lambda e, c=c: e.transpose(
            pA[:, c * 128:(c + 1) * 128],
            x3t[:, c * 128:(c + 1) * 128], identf[:, :]),
            wait=[ln_f, st8.get("f_pAo", FREE0)] if c == 0 else [],
            inc=s_mm if c == NCH - 1 else None, fence=(c == NCH - 1))
        tp_f = r or tp_f
    st8["f_x3t"] = tp_f
    ev_f = P.on("scalar", lambda e, ot=ot: e.activation(
        out=ot[:, :].rearrange("p (c j) -> p c j", c=NCH),
        in_=pA[:, :].rearrange("p (c j) -> p c j", c=NCH),
        func=AF.Copy),
        wait=[tp_f, st8.get(("f_ost", gt % 2), FREE0)], inc=s_ev, drain=True)
    st8["f_pAo"] = ev_f
    st8[("f_pAs", 0)] = ev_f
    st8[("f_pAs", 1)] = ev_f
    B.state.setdefault(("xpsum", id(pA)), [FREE0, FREE0])[0] = ev_f
    st8[("f_ost", gt % 2)] = P.dma(
        lambda e, ot=ot, seq=seq, tt=tt: e.dma_start(
            out=out_d[seq][tt * 128:(tt + 1) * 128, :], in_=ot[:, :]),
        wait=[ev_f], inc=s_st)
    st8["f_x3b"] = l_mm


def run_l1f(x3, q_proj_w, router_w, aux_keys, aux_values, reliability_mask,
            unc_w1, unc_b1, unc_w2, unc_b2, gate_w1, gate_bias):
    cc = host_constants()
    nc = build_l1f()
    x3 = np.ascontiguousarray(np.asarray(x3, np.float32).reshape(NCORE, BPC, S, D))
    scal = np.zeros((1, 4), np.float32)
    scal[0, 0] = np.float32(np.asarray(unc_b2).reshape(-1)[0])
    scal[0, 1] = np.float32(gate_w1)
    scal[0, 2] = np.float32(gate_bias)
    base = dict(
        qpT=np.ascontiguousarray(np.asarray(q_proj_w, np.float32).T),
        rwT=np.ascontiguousarray(np.asarray(router_w, np.float32).T),
        akT=np.ascontiguousarray(np.asarray(aux_keys, np.float32).T),
        av=np.ascontiguousarray(np.asarray(aux_values, np.float32)),
        rmask=np.ascontiguousarray(np.asarray(reliability_mask,
                                              np.float32).reshape(1, AUXN)),
        uw1T=np.ascontiguousarray(np.asarray(unc_w1, np.float32).T),
        b1uT=np.ascontiguousarray(np.asarray(unc_b1, np.float32)
                                  .reshape(2, 128).T),
        uw2T=np.ascontiguousarray(np.asarray(unc_w2, np.float32)
                                  .reshape(2, 128).T),
        scal=scal, identf=cc["identf"], identb=cc["identb"])
    in_maps = [dict(base, x3=x3[i]) for i in range(NCORE)]
    res = run_bass_kernel_spmd(nc, in_maps, core_ids=list(range(NCORE)))
    out = np.concatenate([r["out"] for r in res.results], axis=0)
    return out.reshape(16, S, D)


# version 03
# speedup vs baseline: 1.0080x; 1.0080x over previous
import numpy as np
import ml_dtypes
from contextlib import ExitStack

import concourse.bass as bass
import concourse.mybir as mybir
from concourse.bass_utils import run_bass_kernel_spmd

F32 = mybir.dt.float32
BF16 = mybir.dt.bfloat16
AF = mybir.ActivationFunctionType
ALU = mybir.AluOpType

NCORE = 8
BPC = 2
S = 2048
D = 1024
NCH = D // 128
NTT = S // 128
H = 16
DH = 64
L0_SHIFT = 16.0
LN_EPS = 1e-5

NOW = None

_CC = {}


def host_constants():
    if _CC:
        return _CC
    identf = np.eye(128, dtype=np.float32)
    identb = np.eye(128).astype(ml_dtypes.bfloat16)
    inv_freq = 1.0 / (10000.0 ** (np.arange(0, DH, 2, dtype=np.float64) / DH))
    ang = np.arange(S, dtype=np.float64)[:, None] * inv_freq[None, :]
    cos64 = np.concatenate([np.cos(ang), np.cos(ang)], axis=1)
    sin64 = np.concatenate([-np.sin(ang), np.sin(ang)], axis=1)
    _CC.update(identf=identf, identb=identb,
               cos64=np.ascontiguousarray(cos64.astype(np.float32)),
               sin64=np.ascontiguousarray(sin64.astype(np.float32)))
    return _CC


class Sem:
    __slots__ = ("h", "n", "name")

    def __init__(self, ctx, nc, name):
        self.h = ctx.enter_context(nc.semaphore(name))
        self.n = 0
        self.name = name


FREE0 = ()


class Prog:
    ENGINES = ("sync", "tensor", "vector", "scalar", "gpsimd")

    def __init__(self, nc):
        self.nc = nc
        self.q = {e: [] for e in self.ENGINES}

    def on(self, eng, fn, wait=(), inc=None, inc_n=1, drain=False,
           fence=False):
        w = []
        for rec in wait:
            if not rec:
                continue
            s, v = rec
            if v is NOW:
                v = s.n
            if v > 0:
                w.append((s, v))
        if fence and inc is not None:
            fb = self.fence
            self.q[eng].append((tuple(w), fn, None, 0))
            self.q[eng].append(((), lambda e: e.matmul(
                fb[0], fb[1], fb[2], start=True, stop=True), None, 0))
            self.q[eng].append(((), lambda e: e.matmul(
                fb[0], fb[1], fb[2], start=True, stop=True), inc, inc_n))
        elif drain and inc is not None:
            self.q[eng].append((tuple(w), fn, None, 0))
            self.q[eng].append(((), lambda e: e.drain(), inc, inc_n))
        else:
            self.q[eng].append((tuple(w), fn, inc, inc_n))
        if inc is not None:
            inc.n += inc_n
            return (inc, inc.n)
        return None

    def dma(self, fn, wait=(), inc=None):
        return self.on("sync", fn, wait=wait, inc=inc, inc_n=16)

    def block(self, name):
        qs = self.q
        self.q = {e: [] for e in self.ENGINES}
        with self.nc.Block(name) as blk:
            for eng_name in self.ENGINES:
                entries = qs[eng_name]
                if not entries:
                    continue

                def body(eng, entries=entries):
                    for waits, fn, inc, inc_n in entries:
                        for sem, val in waits:
                            eng.wait_ge(sem.h, val)
                        instr = fn(eng)
                        if inc is not None:
                            instr.then_inc(inc.h, inc_n)

                getattr(blk, eng_name)(body)


def fmc(c, t0=0, n=S):
    return slice(c * S + t0, c * S + t0 + n)


class NCProxy:

    def __init__(self, nc, B):
        self._nc = nc
        self._B = B

    def __getattr__(self, k):
        return getattr(self._nc, k)

    def dram_tensor(self, name, shape, dtype, kind="Internal"):
        dm = self._B.dram_map
        if name not in dm:
            dm[name] = self._nc.dram_tensor(name, shape, dtype, kind=kind)
        return dm[name]

    def sbuf_tensor(self, name, shape, dtype):
        return self._nc.sbuf_tensor(self._B.prefix + name, shape, dtype)

    def psum_tensor(self, name, shape, dtype):
        return self._nc.psum_tensor(self._B.prefix + name, shape, dtype)

    def Block(self, name=None):
        return self._nc.Block(self._B.prefix + name if name else name)


class Builder:
    def __init__(self):
        self._stack = ExitStack()
        self.nc = bass.Bass("TRN2", target_bir_lowering=False)
        self._nc_real = self.nc
        self.ctx = self._stack
        self.alloc_ctx = self._stack
        self.P = Prog(self.nc)
        self.sems = {}
        self.state = {}
        self.prefix = ""
        self.dram_map = {}
        self._phase_stack = None

    def begin_phase(self, prefix):
        self.prefix = prefix
        self._phase_stack = ExitStack()
        self.alloc_ctx = self._phase_stack
        self.nc = NCProxy(self._nc_real, self)
        self.P.nc = self.nc

    def end_phase(self):
        self._phase_stack.close()
        self._phase_stack = None
        self.alloc_ctx = self._stack
        self.prefix = ""
        self.nc = self._nc_real
        self.P.nc = self.nc

    def sem(self, name):
        if name not in self.sems:
            self.sems[name] = Sem(self.ctx, self._nc_real, name)
        return self.sems[name]

    def dsem(self, role, n=2):
        key = ("dsem", role)
        if key not in self.state:
            self.state[key] = [self.sem(f"dq_{role}{j}") for j in range(n)]
        return self.state[key]

    def sb(self, name, shape, dtype):
        return self.alloc_ctx.enter_context(
            self._nc_real.sbuf_tensor(self.prefix + name, shape, dtype))

    def ps(self, name, shape, dtype):
        return self.alloc_ctx.enter_context(
            self._nc_real.psum_tensor(self.prefix + name, shape, dtype))

    def make_fence(self, pF, ones_col, identf):
        self.P.fence = (pF[0:1, 0:128], ones_col[0:1, 0:1], identf[0:1, 0:128])

    def make_bias_consts(self, values):
        self.bias_tile = self.sb("biasconsts", [128, max(1, len(values))], F32)
        self.bias_map = {}
        for i, v in enumerate(values):
            self.bias_map[float(v)] = i
            self.P.on("vector", lambda e, i=i, v=v: e.memset(
                self.bias_tile[:, i:i + 1], float(v)))

    def bias(self, v):
        return self.bias_tile[:, self.bias_map[float(v)]:self.bias_map[float(v)] + 1]

    def finish(self):
        self._stack.close()
        return self.nc



def emit_load_x_fm(B, x_dram_seq, xfm, xstage, pA, pB, identf):
    P = B.P
    s_tp, s_ev = B.sem("tp"), B.sem("ev")
    sp = B.dsem("x")
    st_free = [FREE0, FREE0]
    pp_free = [FREE0, FREE0]
    for tt in range(NTT):
        st = xstage[tt % 2]
        ld_f = P.dma(lambda e, st=st, tt=tt: e.dma_start(
            out=st[:, :], in_=x_dram_seq[tt * 128:(tt + 1) * 128, :]),
            wait=[st_free[tt % 2]], inc=sp[tt % 2])
        pp = pA if tt % 2 == 0 else pB
        tp_f = None
        for c in range(NCH):
            r = P.on("tensor", lambda e, st=st, pp=pp, c=c:
                     e.transpose(pp[:, c * 128:(c + 1) * 128],
                                 st[:, c * 128:(c + 1) * 128], identf[:, :]),
                     wait=[ld_f, pp_free[tt % 2]] if c == 0 else [],
                     inc=s_tp if c == NCH - 1 else None,
                     fence=(c == NCH - 1))
            tp_f = r or tp_f
        st_free[tt % 2] = tp_f
        pp_free[tt % 2] = P.on("scalar", lambda e, pp=pp, tt=tt: e.activation(
            out=xfm[:, :].rearrange("p (c s) -> p c s", c=NCH)[
                :, :, tt * 128:tt * 128 + 128],
            in_=pp[:, :].rearrange("p (c j) -> p c j", c=NCH),
            func=AF.Copy),
            wait=[tp_f], inc=s_ev)


def emit_store_fm_to_tm(B, src_fm, out_dram_seq, ostage, pA, pB, identf,
                        width=S):
    P = B.P
    s_tp, s_ev = B.sem("tp"), B.sem("ev")
    sp = B.dsem("o")
    st_free = [FREE0, FREE0]
    pp_free = [FREE0, FREE0]
    for tt in range(width // 128):
        pp = pA if tt % 2 == 0 else pB
        tp_f = None
        for c in range(NCH):
            r = P.on("tensor", lambda e, pp=pp, c=c, tt=tt:
                     e.transpose(pp[:, c * 128:(c + 1) * 128],
                                 src_fm[:, c * width + tt * 128:
                                        c * width + tt * 128 + 128],
                                 identf[:, :]),
                     wait=[pp_free[tt % 2]] if c == 0 else [],
                     inc=s_tp if c == NCH - 1 else None,
                     fence=(c == NCH - 1))
            tp_f = r or tp_f
        st = ostage[tt % 2]
        ev_f = P.on("scalar", lambda e, pp=pp, st=st: e.activation(
            out=st[:, :], in_=pp[:, :], func=AF.Copy),
            wait=[tp_f, st_free[tt % 2]], inc=s_ev, drain=True)
        pp_free[tt % 2] = ev_f
        st_free[tt % 2] = P.dma(lambda e, st=st, tt=tt: e.dma_start(
            out=out_dram_seq[tt * 128:(tt + 1) * 128, :], in_=st[:, :]),
            wait=[ev_f], inc=sp[tt % 2])
    return list(st_free)


def emit_weight_cast(B, w_dram, wsb, wstage, eng="vector"):
    P = B.P
    s_wc = B.sem("wc")
    sp = B.dsem("w")
    nkc = w_dram.shape[0] // 128
    M = w_dram.shape[1]
    key = ("stage", id(wstage[0]))
    st_free = B.state.setdefault(key, [FREE0, FREE0])
    for kc in range(nkc):
        st = wstage[kc % 2]
        ld_f = P.dma(lambda e, st=st, kc=kc: e.dma_start(
            out=st[:, 0:M], in_=w_dram[kc * 128:(kc + 1) * 128, :]),
            wait=[st_free[kc % 2]], inc=sp[kc % 2])
        st_free[kc % 2] = P.on(eng, lambda e, st=st, kc=kc: e.tensor_copy(
            out=wsb[:, kc * M:(kc + 1) * M], in_=st[:, 0:M]),
            wait=[ld_f], inc=s_wc)


def emit_postln_inplace(B, xfm, pA, pC, sqb, tmp, rows, ones_col, ones_row,
                        width=S, xfm_ready=FREE0, final_drain=False):
    P = B.P
    def col(c, t0, n):
        return slice(c * width + t0, c * width + t0 + n)
    s_mm, s_rw, s_ap = B.sem("mm"), B.sem("rw"), B.sem("ap")
    pc_free = FREE0
    pa_free = FREE0
    sq_free = [FREE0, FREE0]
    N = min(512, width)
    for tg in range(max(1, width // 512)):
        t0 = tg * 512
        for c in range(NCH):
            P.on("tensor", lambda e, c=c, t0=t0: e.matmul(
                pC[0:1, 0:N], ones_col[:, :], xfm[:, col(c, t0, N)],
                start=(c == 0), stop=(c == NCH - 1)),
                wait=[pc_free, xfm_ready] if c == 0 else [])
        mm_f = None
        for c in range(NCH):
            sq = sqb[c % 2]
            rw_f = P.on("scalar", lambda e, c=c, t0=t0, sq=sq: e.activation(
                out=sq[:, 0:N], in_=xfm[:, col(c, t0, N)], func=AF.Square),
                wait=[sq_free[c % 2], xfm_ready if c == 0 else FREE0],
                inc=s_rw, drain=True)
            mm_f = P.on("tensor", lambda e, c=c, sq=sq: e.matmul(
                pC[0:1, 512:512 + N], ones_col[:, :], sq[:, 0:N],
                start=(c == 0), stop=(c == NCH - 1)),
                wait=[rw_f], inc=s_mm, fence=(c == NCH - 1))
            sq_free[c % 2] = mm_f
        P.on("scalar", lambda e: e.activation(
            out=rows[:, 0:N], in_=pC[0:1, 0:N], func=AF.Copy, scale=1.0 / D),
            wait=[mm_f])
        pc_free = P.on("scalar", lambda e: e.activation(
            out=rows[:, 512:512 + N], in_=pC[0:1, 512:512 + N], func=AF.Copy,
            scale=1.0 / D), inc=s_rw, drain=True)
        P.on("vector", lambda e: e.tensor_tensor(
            out=rows[:, 1024:1024 + N], in0=rows[:, 0:N], in1=rows[:, 0:N],
            op=ALU.mult), wait=[pc_free])
        var_f = P.on("vector", lambda e: e.tensor_tensor(
            out=rows[:, 512:512 + N], in0=rows[:, 512:512 + N],
            in1=rows[:, 1024:1024 + N],
            op=ALU.subtract), inc=s_rw, drain=True)
        P.on("scalar", lambda e: e.activation(
            out=rows[:, 512:512 + N], in_=rows[:, 512:512 + N], func=AF.Ln,
            bias=B.bias(LN_EPS)[0:1, :]), wait=[var_f])
        rstd_f = P.on("scalar", lambda e: e.activation(
            out=rows[:, 512:512 + N], in_=rows[:, 512:512 + N], func=AF.Exp,
            scale=-0.5), inc=s_rw, drain=True)
        bm_f = P.on("vector", lambda e: e.tensor_tensor(
            out=rows[:, 0:N], in0=rows[:, 0:N], in1=rows[:, 512:512 + N],
            op=ALU.mult), wait=[rstd_f], inc=s_rw, drain=True)
        P.on("tensor", lambda e: e.matmul(
            pA[:, 0:N], ones_row[:, :], rows[:, 512:512 + N],
            start=True, stop=True),
            wait=[bm_f, pa_free])
        bc_f = P.on("tensor", lambda e: e.matmul(
            pA[:, 512:512 + N], ones_row[:, :], rows[:, 0:N],
            start=True, stop=True), inc=s_mm, fence=True)
        ap_f = None
        for c in range(NCH):
            P.on("vector", lambda e, c=c, t0=t0: e.tensor_tensor(
                out=tmp[:, 0:N], in0=xfm[:, col(c, t0, N)], in1=pA[:, 0:N],
                op=ALU.mult),
                wait=[bc_f] if c == 0 else [])
            ap_f = P.on("vector", lambda e, c=c, t0=t0: e.tensor_tensor(
                out=xfm[:, col(c, t0, N)], in0=tmp[:, 0:N],
                in1=pA[:, 512:512 + N],
                op=ALU.subtract), inc=s_ap,
                drain=(final_drain and c == NCH - 1
                       and tg == max(1, width // 512) - 1))
        pa_free = ap_f


def emit_xtiles(B, x_dram_rows, ntiles, dst_fn, xstage, pA, pB, identf,
                evac_eng="scalar", dst_free=FREE0):
    P = B.P
    s_tp, s_ev = B.sem("tp"), B.sem("ev")
    sp = B.dsem("x")
    key = ("stage", id(xstage[0]))
    st_free = B.state.setdefault(key, [FREE0, FREE0])
    keyp = ("xpsum", id(pA))
    pp_free = B.state.setdefault(keyp, [FREE0, FREE0])
    ev_f = None
    for tt in range(ntiles):
        st = xstage[tt % 2]
        ld_f = P.dma(lambda e, st=st, tt=tt: e.dma_start(
            out=st[:, 0:D], in_=x_dram_rows[tt * 128:(tt + 1) * 128, :]),
            wait=[st_free[tt % 2]], inc=sp[tt % 2])
        pp = pA if tt % 2 == 0 else pB
        tp_f = None
        for c in range(NCH):
            r = P.on("tensor", lambda e, st=st, pp=pp, c=c:
                     e.transpose(pp[:, c * 128:(c + 1) * 128],
                                 st[:, c * 128:(c + 1) * 128], identf[:, :]),
                     wait=[ld_f, pp_free[tt % 2]] if c == 0 else [],
                     inc=s_tp if c == NCH - 1 else None,
                     fence=(c == NCH - 1))
            tp_f = r or tp_f
        st_free[tt % 2] = tp_f
        if evac_eng == "scalar":
            ev_f = P.on("scalar", lambda e, pp=pp, tt=tt: e.activation(
                out=dst_fn(tt), in_=pp[:, :].rearrange("p (c j) -> p c j", c=NCH),
                func=AF.Copy),
                wait=[tp_f, dst_free], inc=s_ev, drain=True)
        else:
            ev_f = P.on("vector", lambda e, pp=pp, tt=tt: e.tensor_copy(
                out=dst_fn(tt), in_=pp[:, :].rearrange("p (c j) -> p c j", c=NCH)),
                wait=[tp_f, dst_free], inc=s_ev, drain=True)
        pp_free[tt % 2] = ev_f
    return ev_f


def emit_wstream(B, w_dram_2d, dst_sb, dst_off, nkc, cols, wstage, dst_free):
    P = B.P
    s_wcp = B.sem("wcp")
    sp = B.dsem("w")
    key = ("stage", id(wstage[0]))
    st_free = B.state.setdefault(key, [FREE0, FREE0])
    cp_f = None
    for kc in range(nkc):
        st = wstage[kc % 2]
        ld_f = P.dma(lambda e, st=st, kc=kc: e.dma_start(
            out=st[:, 0:cols], in_=w_dram_2d[kc * 128:(kc + 1) * 128, :]),
            wait=[st_free[kc % 2]], inc=sp[kc % 2])
        cp_f = P.on("gpsimd", lambda e, st=st, kc=kc: e.tensor_copy(
            out=dst_sb[:, dst_off + kc * cols:dst_off + (kc + 1) * cols],
            in_=st[:, 0:cols]),
            wait=[ld_f] + ([dst_free] if kc == 0 else []),
            inc=s_wcp, drain=(kc == nkc - 1))
        st_free[kc % 2] = cp_f
    return cp_f



DEBUG_L0A = False
B2_BARRIERS = False


def build_l0a(B=None):
    standalone = B is None
    if standalone:
        B = Builder()
    nc, P = B.nc, B.P
    xs = nc.dram_tensor("xs", [BPC, S, D], F32, kind="ExternalInput").ap()
    wq_d = nc.dram_tensor("wq", [D, D], F32, kind="ExternalInput").ap()
    wkv_d = nc.dram_tensor("wkv", [D, D], F32, kind="ExternalInput").ap()
    wout_d = nc.dram_tensor("wout", [D, D], F32, kind="ExternalInput").ap()
    boutT = nc.dram_tensor("boutT", [128, NCH], F32, kind="ExternalInput").ap()
    identf_d = nc.dram_tensor("identf", [128, 128], F32, kind="ExternalInput").ap()
    identb_d = nc.dram_tensor("identb", [128, 128], BF16, kind="ExternalInput").ap()
    cos_d = nc.dram_tensor("cos64", [S, 64], F32, kind="ExternalInput").ap()
    sin_d = nc.dram_tensor("sin64", [S, 64], F32, kind="ExternalInput").ap()
    x1_d = nc.dram_tensor("x1", [BPC, S, D], F32, kind="ExternalOutput").ap()
    ctx_d = nc.dram_tensor("ctxd", [NCH, 128, S], BF16,
                           kind="ExternalOutput" if DEBUG_L0A else "Internal").ap()
    if DEBUG_L0A:
        B.dbg_qtm = nc.dram_tensor("dbg_qtm", [128, NTT * D], BF16,
                                   kind="ExternalOutput").ap()
        B.dbg_kvtm = nc.dram_tensor("dbg_kvtm", [128, NTT * D], BF16,
                                    kind="ExternalOutput").ap()
        B.dbg_pre = nc.dram_tensor("dbg_pre", [128, NCH * S], F32,
                                   kind="ExternalOutput").ap()
        B.dbg_qraw = nc.dram_tensor("dbg_qraw", [2, 128, D], F32,
                                    kind="ExternalOutput").ap()
        B.dbg_xfm = nc.dram_tensor("dbg_xfm", [128, NCH * S], F32,
                                   kind="ExternalOutput").ap()
        B.dbg_w = nc.dram_tensor("dbg_w", [2, 128, NCH * D], BF16,
                                 kind="ExternalOutput").ap()

    s_ld = B.sem("ld")

    xfm = B.sb("xfm", [128, NCH * S], F32)
    identf = B.sb("sidf", [128, 128], F32)
    identb = B.sb("sidb", [128, 128], BF16)
    ones_col = B.sb("ones_col", [128, 1], F32)
    ones_row = B.sb("ones_row", [1, 128], F32)
    boutsb = B.sb("boutsb", [128, NCH], F32)
    qtm = B.sb("qtm", [128, NTT * D], BF16)
    kvtm = B.sb("kvtm", [128, NTT * D], BF16)
    cossb = B.sb("cossb", [128, NTT * 64], F32)
    sinsb = B.sb("sinsb", [128, NTT * 64], F32)
    wq = B.sb("wqsb", [128, NCH * D], BF16)
    wkv = B.sb("wkvsb", [128, NCH * D], BF16)
    wout = B.sb("wosb", [128, NCH * D], BF16)

    pA = B.ps("pA", [128, 1024], F32)
    pB = B.ps("pB", [128, 1024], F32)
    pC = B.ps("pC", [128, 1024], F32)
    pT0 = B.ps("pT0", [128, 1024], BF16)
    pF = B.ps("pF", [128, 512], F32)
    pT1 = pT0
    B.make_fence(pF, ones_col, identf)

    P.dma(lambda e: e.dma_start(out=identf[:, :], in_=identf_d), inc=s_ld)
    P.dma(lambda e: e.dma_start(out=identb[:, :], in_=identb_d), inc=s_ld)
    P.dma(lambda e: e.dma_start(out=boutsb[:, :], in_=boutT), inc=s_ld)
    P.dma(lambda e: e.dma_start(
        out=cossb[:, :].rearrange("p (t d) -> p t d", t=NTT),
        in_=cos_d.rearrange("(t p) d -> p t d", p=128)), inc=s_ld)
    P.dma(lambda e: e.dma_start(
        out=sinsb[:, :].rearrange("p (t d) -> p t d", t=NTT),
        in_=sin_d.rearrange("(t p) d -> p t d", p=128)), inc=s_ld)
    P.on("vector", lambda e: e.memset(ones_col[:, :], 1.0))
    P.on("vector", lambda e: e.memset(ones_row[:, :], 1.0))
    B.make_bias_consts([-L0_SHIFT, LN_EPS])
    P.on("sync", lambda e, v=s_ld.n: e.wait_ge(s_ld.h, v))
    P.block("consts")

    with ExitStack() as cw:
        wstage = [cw.enter_context(nc.sbuf_tensor(f"wst{i}", [128, D], F32))
                  for i in range(2)]
        emit_weight_cast(B, wq_d, wq, wstage)
        emit_weight_cast(B, wkv_d, wkv, wstage)
        emit_weight_cast(B, wout_d, wout, wstage)
        P.block("wcast")
        if DEBUG_L0A:
            s_dbg4 = B.sem("dbg4")
            P.dma(lambda e: e.dma_start(out=B.dbg_w[0], in_=wq[:, :]), inc=s_dbg4)
            P.dma(lambda e: e.dma_start(out=B.dbg_w[1], in_=wkv[:, :]), inc=s_dbg4)
            P.on("sync", lambda e, v=s_dbg4.n: e.wait_ge(s_dbg4.h, v))
            P.block("dbgdump4")

    for seq in range(BPC):
        _l0a_seq(B, seq, xs, x1_d, ctx_d, xfm, qtm, kvtm, cossb, sinsb,
                 identf, identb, ones_col, ones_row, boutsb, wq, wkv, wout,
                 pA, pB, pC, pT0, pT1)

    return B.finish() if standalone else None


def _l0a_seq(B, seq, xs, x1_d, ctx_d, xfm, qtm, kvtm, cossb, sinsb,
             identf, identb, ones_col, ones_row, boutsb, wq, wkv, wout,
             pA, pB, pC, pT0, pT1):
    nc, P = B.nc, B.P
    s_xb, s_qk, s_pj, s_rot = (B.sem(n) for n in ("xb", "qk", "pj", "rot"))

    with ExitStack() as c1:
        xstage = [c1.enter_context(nc.sbuf_tensor(f"xst{seq}_{i}", [128, D], F32))
                  for i in range(2)]
        emit_load_x_fm(B, xs[seq], xfm, xstage, pA, pB, identf)
        P.block(f"s{seq}_loadx")
        if DEBUG_L0A and seq == 0:
            s_dbg3 = B.sem("dbg3")
            P.dma(lambda e: e.dma_start(out=B.dbg_xfm, in_=xfm[:, :]), inc=s_dbg3)
            P.on("sync", lambda e, v=s_dbg3.n: e.wait_ge(s_dbg3.h, v))
            P.block("dbgdump3")

    with ExitStack() as c2:
        en2 = c2.enter_context
        xbt = en2(nc.sbuf_tensor(f"xbt{seq}", [128, NCH * 128], BF16))
        qraw = en2(nc.sbuf_tensor(f"qraw{seq}", [128, D], F32))
        t1 = en2(nc.sbuf_tensor(f"t1_{seq}", [128, D], F32))
        t2 = en2(nc.sbuf_tensor(f"t2_{seq}", [128, D], F32))

        xbt_free = FREE0
        pp_free = [FREE0, FREE0]
        qraw_free = FREE0
        for tt in range(NTT):
            xb_f = P.on("vector", lambda e, tt=tt: e.tensor_copy(
                out=xbt[:, :].rearrange("p (c j) -> p c j", c=NCH),
                in_=xfm[:, :].rearrange("p (c s) -> p c s", c=NCH)[
                    :, :, tt * 128:tt * 128 + 128]),
                wait=[xbt_free], inc=s_xb, drain=True)
            for wi, (wt, dst, scale) in enumerate(
                    ((wq, qtm, 0.125), (wkv, kvtm, 1.0))):
                pp = pB if wi == 0 else pA
                qk_f = None
                for half in range(2):
                    for kc in range(NCH):
                        r = P.on("tensor",
                                 lambda e, pp=pp, kc=kc, half=half, wt=wt:
                                 e.matmul(pp[:, half * 512:half * 512 + 512],
                                          xbt[:, kc * 128:(kc + 1) * 128],
                                          wt[:, kc * D + half * 512:
                                             kc * D + half * 512 + 512],
                                          start=(kc == 0), stop=(kc == NCH - 1)),
                                 wait=([xb_f, pp_free[wi]]
                                       if kc == 0 and half == 0 else []),
                                 inc=s_qk if (kc == NCH - 1 and half == 1) else None,
                                 fence=(kc == NCH - 1 and half == 1))
                        qk_f = r or qk_f
                xbt_free = qk_f
                pj_f = P.on("vector", lambda e, pp=pp, scale=scale:
                    e.tensor_scalar_mul(qraw[:, :], pp[:, :], scale),
                    wait=[qk_f, qraw_free], inc=s_pj)
                pp_free[wi] = pj_f
                if DEBUG_L0A and seq == 0 and tt == 0:
                    s_dbg2 = B.sem("dbg2")
                    db_f = P.dma(lambda e, wi=wi: e.dma_start(
                        out=B.dbg_qraw[wi], in_=qraw[:, :]),
                        wait=[pj_f], inc=s_dbg2)
                    qraw_free = db_f
                cs = cossb[:, tt * 64:(tt + 1) * 64]
                sn = sinsb[:, tt * 64:(tt + 1) * 64]
                qv = qraw[:, :].rearrange("p (h sj) -> p h sj", sj=64)
                t1v = t1[:, :].rearrange("p (h sj) -> p h sj", sj=64)
                t2v = t2[:, :].rearrange("p (h sj) -> p h sj", sj=64)
                rot_f = None
                for hh in range(H):
                    P.on("vector", lambda e, hh=hh, cs=cs, qv=qv, t1v=t1v:
                         e.tensor_tensor(out=t1v[:, hh, :], in0=qv[:, hh, :],
                                         in1=cs, op=ALU.mult),
                         wait=[pj_f] if hh == 0 else [])
                    P.on("vector", lambda e, hh=hh, sn=sn, qv=qv, t2v=t2v:
                         e.tensor_tensor(out=t2v[:, hh, 0:32],
                                         in0=qv[:, hh, 32:64],
                                         in1=sn[:, 0:32], op=ALU.mult))
                    r = P.on("vector", lambda e, hh=hh, sn=sn, qv=qv, t2v=t2v:
                             e.tensor_tensor(out=t2v[:, hh, 32:64],
                                             in0=qv[:, hh, 0:32],
                                             in1=sn[:, 32:64], op=ALU.mult),
                             inc=s_rot if hh == H - 1 else None)
                    rot_f = r or rot_f
                qraw_free = rot_f
                P.on("vector", lambda e, dst=dst, tt=tt: e.tensor_tensor(
                    out=dst[:, tt * D:(tt + 1) * D], in0=t1[:, :], in1=t2[:, :],
                    op=ALU.add))
            if B2_BARRIERS:
                P.block(f"s{seq}_proj_t{tt}")
        P.block(f"s{seq}_proj")
        if DEBUG_L0A and seq == 0:
            s_dbg = B.sem("dbg")
            P.dma(lambda e: e.dma_start(out=B.dbg_qtm, in_=qtm[:, :]), inc=s_dbg)
            P.dma(lambda e: e.dma_start(out=B.dbg_kvtm, in_=kvtm[:, :]), inc=s_dbg)
            P.on("sync", lambda e, v=s_dbg.n: e.wait_ge(s_dbg.h, v))
            P.block("dbgdump")

    with ExitStack() as c3:
        en3 = c3.enter_context
        qfh = en3(nc.sbuf_tensor(f"qfh{seq}", [64, S], BF16))
        kfh = en3(nc.sbuf_tensor(f"kfh{seq}", [64, S], BF16))
        ktm65 = en3(nc.sbuf_tensor(f"ktm65{seq}", [128, NTT * 65], BF16))
        ptsb = [en3(nc.sbuf_tensor(f"pt{seq}_{i}", [128, 1024], BF16))
                for i in range(2)]
        ctxs = en3(nc.sbuf_tensor(f"ctxs{seq}", [64, 1024], BF16))
        bcsb = en3(nc.sbuf_tensor(f"bcsb{seq}", [65, 1024], F32))
        drow = bcsb[64:65, :]
        hs = {
            "pt_free": [FREE0, FREE0],
            "st_free": [FREE0, FREE0],
            "ctx_free": FREE0,
            "fm_free": FREE0,
            "k65_free": FREE0,
            "ptbank_free": [FREE0, FREE0],
            "ctxs_free": FREE0,
            "bc_free": FREE0,
        }
        for h in range(H):
            _l0a_head(B, seq, h, hs, qtm, kvtm, qfh, kfh, ktm65, ptsb, ctxs,
                      bcsb, drow, ctx_d, identb, ones_row, pA, pB, pC, pT0, pT1)
        P.block(f"s{seq}_attn")

    with ExitStack() as c4:
        en4 = c4.enter_context
        ctxsb = [en4(nc.sbuf_tensor(f"cxsb{seq}_0", [128, NCH * 512], BF16))]
        sqb = [en4(nc.sbuf_tensor(f"sq{seq}_{i}", [128, 512], F32))
               for i in range(2)]
        tmp = en4(nc.sbuf_tensor(f"tmp{seq}", [128, 512], F32))
        rows = en4(nc.sbuf_tensor(f"rows{seq}", [1, 1536], F32))
        s_ld = B.sem("ld")

        slots = [(pA, 0), (pA, 512), (pB, 0), (pB, 512)]
        slot_free = [FREE0] * 4
        cx_free = [FREE0]
        s_cd = B.sem("cd")
        idx = 0
        for tg in range(4):
            cx = ctxsb[0]
            ld_f = P.dma(lambda e, cx=cx, tg=tg: e.dma_start(
                out=cx[:, :].rearrange("p (c s) -> p c s", c=NCH),
                in_=ctx_d[:, :, tg * 512:(tg + 1) * 512].rearrange(
                    "c p s -> p c s")),
                wait=[cx_free[0], (s_cd, s_cd.n)], inc=s_ld)
            for mc in range(NCH):
                pp, off = slots[idx % 4]
                qk_f = None
                for kc in range(NCH):
                    r = P.on("tensor",
                             lambda e, pp=pp, off=off, kc=kc, mc=mc, cx=cx:
                             e.matmul(pp[:, off:off + 512],
                                      wout[:, kc * D + mc * 128:
                                           kc * D + mc * 128 + 128],
                                      cx[:, kc * 512:(kc + 1) * 512],
                                      start=(kc == 0), stop=(kc == NCH - 1)),
                             wait=[ld_f, slot_free[idx % 4]] if kc == 0 else [],
                             inc=s_qk if kc == NCH - 1 else None,
                             fence=(kc == NCH - 1))
                    qk_f = r or qk_f
                if mc == NCH - 1:
                    cx_free[0] = qk_f
                P.on("vector", lambda e, pp=pp, off=off, mc=mc:
                     e.tensor_scalar(out=tmp[:, :], in0=pp[:, off:off + 512],
                                     scalar1=boutsb[:, mc:mc + 1], scalar2=None,
                                     op0=ALU.add),
                     wait=[qk_f])
                slot_free[idx % 4] = P.on(
                    "vector", lambda e, mc=mc, tg=tg:
                    e.tensor_tensor(out=xfm[:, fmc(mc, tg * 512, 512)],
                                    in0=tmp[:, :],
                                    in1=xfm[:, fmc(mc, tg * 512, 512)],
                                    op=ALU.add),
                    inc=s_pj)
                idx += 1
        P.block(f"s{seq}_wout")
        if DEBUG_L0A and seq == 0:
            s_dbg = B.sem("dbg")
            P.dma(lambda e: e.dma_start(out=B.dbg_pre, in_=xfm[:, :]), inc=s_dbg)
            P.on("sync", lambda e, v=s_dbg.n: e.wait_ge(s_dbg.h, v))
            P.block("dbgdump2")

        emit_postln_inplace(B, xfm, pA, pC, sqb, tmp, rows, ones_col, ones_row)
        P.block(f"s{seq}_ln")

    with ExitStack() as c5:
        ostage = [c5.enter_context(nc.sbuf_tensor(f"ost{seq}_{i}", [128, D], F32))
                  for i in range(2)]
        recs = emit_store_fm_to_tm(B, xfm, x1_d[seq], ostage, pA, pB, identf)
        for sm, v in recs:
            P.on("sync", lambda e, sm=sm, v=v: e.wait_ge(sm.h, v))
        P.block(f"s{seq}_store")


def _l0a_head(B, seq, h, hs, qtm, kvtm, qfh, kfh, ktm65, ptsb, ctxs, bcsb,
              drow, ctx_d, identb, ones_row, pA, pB, pC, pT0, pT1):
    nc, P = B.nc, B.P
    s_tp, s_ev = B.sem("tp"), B.sem("ev")
    s_qk, s_ex, s_av, s_ce = (B.sem(n) for n in ("qk2", "ex", "av", "ce"))
    s_cd = B.sem("cd")

    for src, dst in ((qtm, qfh), (kvtm, kfh)):
        for half in range(2):
            rb = 64 * half
            tp_f = None
            for i in range(8):
                tt = half * 8 + i
                r = P.on("tensor", lambda e, rb=rb, i=i, tt=tt, src=src:
                         e.transpose(pT0[rb:rb + 64, i * 128:(i + 1) * 128],
                                     src[:, tt * D + h * 64:tt * D + h * 64 + 64],
                                     identb[:, :]),
                         wait=[hs["ptbank_free"][half]] if i == 0 else [],
                         inc=s_tp if i == 7 else None)
                tp_f = r or tp_f
            hs["ptbank_free"][half] = P.on(
                "vector", lambda e, rb=rb, half=half, dst=dst:
                e.tensor_copy(out=dst[:, half * 1024:(half + 1) * 1024],
                              in_=pT0[rb:rb + 64, :]),
                wait=[tp_f, hs["fm_free"]], inc=s_ev, drain=True)
    P.on("vector", lambda e: e.memset(ktm65[:, :], 1.0),
         wait=[hs["k65_free"]])
    k65_f = P.on("vector", lambda e: e.tensor_copy(
        out=ktm65[:, :].rearrange("p (t c) -> p t c", c=65)[:, :, 0:64],
        in_=kvtm[:, :].rearrange("p (t d) -> p t d", d=D)[
            :, :, h * 64:h * 64 + 64]),
        inc=s_ev, drain=True)

    for qcp in range(2):
        q0 = qcp * 1024

        def emit_qk(kt):
            stp = pA if kt % 2 == 0 else pB
            qk_f = None
            for qc in range(2):
                r = P.on("tensor", lambda e, stp=stp, kt=kt, qc=qc, q0=q0:
                         e.matmul(stp[:, qc * 512:qc * 512 + 512],
                                  kfh[:, kt * 128:(kt + 1) * 128],
                                  qfh[:, q0 + qc * 512:q0 + qc * 512 + 512],
                                  start=True, stop=True),
                         wait=([k65_f, hs["st_free"][kt % 2]]
                               if qc == 0 else []),
                         inc=s_qk if qc == 1 else None)
                qk_f = r or qk_f
            return qk_f

        def emit_exp(kt, qk_f):
            stp = pA if kt % 2 == 0 else pB
            pt = ptsb[kt % 2]
            ex_f = P.on("scalar", lambda e, stp=stp, pt=pt: e.activation(
                out=pt[:, :], in_=stp[:, :], func=AF.Exp, bias=B.bias(-L0_SHIFT)),
                wait=[qk_f, hs["pt_free"][kt % 2]], inc=s_ex, drain=True)
            hs["st_free"][kt % 2] = ex_f
            return ex_f

        def emit_av(kt, ex_f):
            av_f = None
            pt = ptsb[kt % 2]
            for qc in range(2):
                r = P.on("tensor", lambda e, pt=pt, kt=kt, qc=qc:
                         e.matmul(pC[0:65, qc * 512:qc * 512 + 512],
                                  ktm65[:, kt * 65:kt * 65 + 65],
                                  pt[:, qc * 512:qc * 512 + 512],
                                  start=(kt == 0), stop=(kt == NTT - 1)),
                         wait=[ex_f, hs["ctx_free"]] if qc == 0 else [],
                         inc=s_av if qc == 1 else None,
                         fence=(qc == 1 and kt == NTT - 1))
                av_f = r or av_f
            hs["pt_free"][kt % 2] = av_f
            return av_f

        exv = {}
        qk0 = emit_qk(0)
        exv[0] = emit_exp(0, qk0)
        for kt in range(1, NTT):
            qkk = emit_qk(kt)
            emit_av(kt - 1, exv[kt - 1])
            exv[kt] = emit_exp(kt, qkk)
        av_last = emit_av(NTT - 1, exv[NTT - 1])
        hs["k65_free"] = av_last

        P.on("vector", lambda e: e.tensor_copy(out=bcsb[0:1, :], in_=pC[64:65, :]),
             wait=[av_last, hs["bc_free"]])
        rc_f = P.on("vector", lambda e: e.reciprocal(
            out=bcsb[0:1, :], in_=bcsb[0:1, :]), inc=s_ce, drain=True)
        bc_f = None
        for qc in range(2):
            r = P.on("tensor", lambda e, qc=qc: e.matmul(
                pA[0:64, qc * 512:qc * 512 + 512], ones_row[0:1, 0:64],
                bcsb[0:1, qc * 512:qc * 512 + 512], start=True, stop=True),
                wait=[rc_f, hs["st_free"][0]] if qc == 0 else [],
                inc=s_qk if qc == 1 else None, fence=(qc == 1))
            bc_f = r or bc_f
        hs["bc_free"] = (B.sem("qk2"), B.sem("qk2").n)
        cp_f = P.on("vector", lambda e: e.tensor_copy(
            out=bcsb[0:64, :], in_=pA[0:64, :]),
            wait=[bc_f], inc=s_ce)
        hs["st_free"][0] = cp_f
        sc_f = P.on("vector", lambda e: e.tensor_tensor(
            out=ctxs[:, :], in0=pC[0:64, :], in1=bcsb[0:64, :], op=ALU.mult),
            wait=[hs["ctxs_free"]], inc=s_ce, drain=True)
        hs["ctx_free"] = sc_f
        c = h // 2
        pbase = 64 * (h % 2)
        hs["ctxs_free"] = P.dma(
            lambda e, c=c, pbase=pbase, q0=q0: e.dma_start(
                out=ctx_d[c, pbase:pbase + 64, q0:q0 + 1024], in_=ctxs[:, :]),
            wait=[sc_f], inc=s_cd)
    hs["fm_free"] = (s_qk, s_qk.n)



def _common_consts():
    cc = host_constants()
    return dict(identf=cc["identf"], identb=cc["identb"],
                cos64=cc["cos64"], sin64=cc["sin64"])


def run_l0a(L0_attn_x, L0_attn_Wq, L0_attn_Wkv, L0_attn_Wout, L0_attn_bout):
    nc = build_l0a()
    x = np.ascontiguousarray(np.asarray(L0_attn_x, dtype=np.float32))
    shards = x.reshape(NCORE, BPC, S, D)
    boutT = np.ascontiguousarray(
        np.asarray(L0_attn_bout, np.float32).reshape(NCH, 128).T)
    base = dict(
        wq=np.ascontiguousarray(np.asarray(L0_attn_Wq, np.float32)),
        wkv=np.ascontiguousarray(np.asarray(L0_attn_Wkv, np.float32)),
        wout=np.ascontiguousarray(np.asarray(L0_attn_Wout, np.float32)),
        boutT=boutT, **_common_consts())
    in_maps = [dict(base, xs=np.ascontiguousarray(shards[i]))
               for i in range(NCORE)]
    res = run_bass_kernel_spmd(nc, in_maps, core_ids=list(range(NCORE)))
    out = np.concatenate([r["x1"] for r in res.results], axis=0)
    return out.reshape(16, S, D)


def build_mega():
    B = Builder()
    for nm in ("x1", "x2", "x3"):
        B.dram_map[nm] = B._nc_real.dram_tensor(nm, [BPC, S, D], F32,
                                                kind="Internal")
    B.begin_phase("A_")
    build_l0a(B)
    B.end_phase()
    B.begin_phase("F_")
    build_l0f(B)
    B.end_phase()
    B.begin_phase("C_")
    build_l1a(B)
    B.end_phase()
    B.begin_phase("G_")
    build_l1f(B)
    B.end_phase()
    return B.finish()


def kernel(**inputs):
    cc = host_l1_consts()
    nc = build_mega()
    x = np.ascontiguousarray(
        np.asarray(inputs["L0_attn_x"], np.float32).reshape(NCORE, BPC, S, D))
    scal = np.zeros((1, 4), np.float32)
    scal[0, 0] = np.float32(np.asarray(inputs["L1_ffn_unc_b2"]).reshape(-1)[0])
    scal[0, 1] = np.float32(inputs["L1_ffn_gate_w1"])
    scal[0, 2] = np.float32(inputs["L1_ffn_gate_bias"])
    f32c = lambda a: np.ascontiguousarray(np.asarray(a, np.float32))
    base = dict(
        wq=f32c(inputs["L0_attn_Wq"]), wkv=f32c(inputs["L0_attn_Wkv"]),
        wout=f32c(inputs["L0_attn_Wout"]),
        boutT=f32c(np.asarray(inputs["L0_attn_bout"],
                              np.float32).reshape(NCH, 128).T),
        identf=cc["identf"], identb=cc["identb"],
        cos64=cc["cos64"], sin64=cc["sin64"],
        wred=f32c(inputs["L0_ffn_w_red"]),
        wgT=f32c(np.asarray(inputs["L0_ffn_wg"], np.float32).T),
        W1=f32c(inputs["L0_ffn_W1"]),
        b1T=f32c(np.asarray(inputs["L0_ffn_b1"], np.float32)
                 .reshape(NEXP, FFH // 128, 128).transpose(0, 2, 1)),
        W2=f32c(inputs["L0_ffn_W2"]), b2=f32c(inputs["L0_ffn_b2"]),
        wqT=f32c(np.asarray(inputs["L1_attn_Wq"], np.float32).T),
        wkT=f32c(np.asarray(inputs["L1_attn_Wk"], np.float32).T),
        wvT=f32c(np.asarray(inputs["L1_attn_Wv"], np.float32).T),
        woT=f32c(np.asarray(inputs["L1_attn_Wo"], np.float32).T),
        boT=f32c(np.asarray(inputs["L1_attn_bo"],
                            np.float32).reshape(NCH, 128).T),
        stripb=cc["stripb"], cinf=cc["cinf"],
        qpT=f32c(np.asarray(inputs["L1_ffn_q_proj_w"], np.float32).T),
        rwT=f32c(np.asarray(inputs["L1_ffn_router_w"], np.float32).T),
        akT=f32c(np.asarray(inputs["L1_ffn_aux_keys"], np.float32).T),
        av=f32c(inputs["L1_ffn_aux_values"]),
        rmask=f32c(np.asarray(inputs["L1_ffn_reliability_mask"],
                              np.float32).reshape(1, AUXN)),
        uw1T=f32c(np.asarray(inputs["L1_ffn_unc_w1"], np.float32).T),
        b1uT=f32c(np.asarray(inputs["L1_ffn_unc_b1"],
                             np.float32).reshape(2, 128).T),
        uw2T=f32c(np.asarray(inputs["L1_ffn_unc_w2"],
                             np.float32).reshape(2, 128).T),
        scal=scal)
    in_maps = [dict(base, xs=np.ascontiguousarray(x[i])) for i in range(NCORE)]
    res = run_bass_kernel_spmd(nc, in_maps, core_ids=list(range(NCORE)))
    out = np.concatenate([r["out"] for r in res.results], axis=0)
    return np.ascontiguousarray(out.reshape(16, S, D).astype(np.float32))



FFH = 2048
NEXP = 8
GRP = 1024
DEBUG_L0F = False


def build_l0f(B=None):
    standalone = B is None
    if standalone:
        B = Builder()
    nc, P = B.nc, B.P
    x1 = nc.dram_tensor("x1", [BPC, S, D], F32, kind="ExternalInput").ap()
    wred_d = nc.dram_tensor("wred", [D, 16], F32, kind="ExternalInput").ap()
    wgT_d = nc.dram_tensor("wgT", [16, NEXP], F32, kind="ExternalInput").ap()
    W1_d = nc.dram_tensor("W1", [NEXP, D, FFH], F32, kind="ExternalInput").ap()
    b1T_d = nc.dram_tensor("b1T", [NEXP, 128, FFH // 128], F32, kind="ExternalInput").ap()
    W2_d = nc.dram_tensor("W2", [NEXP, FFH, D], F32, kind="ExternalInput").ap()
    b2_d = nc.dram_tensor("b2", [NEXP, D], F32, kind="ExternalInput").ap()
    identf_d = nc.dram_tensor("identf", [128, 128], F32, kind="ExternalInput").ap()
    x2_d = nc.dram_tensor("x2", [BPC, S, D], F32, kind="ExternalOutput").ap()
    if DEBUG_L0F:
        B.dbg_xb = nc.dram_tensor("dbg_xb", [128, NCH * GRP], BF16,
                                  kind="ExternalOutput").ap()
        B.dbg_gat = nc.dram_tensor("dbg_gat", [9, GRP], F32,
                                   kind="ExternalOutput").ap()
        B.dbg_oacc = nc.dram_tensor("dbg_oacc", [128, NCH * GRP], F32,
                                    kind="ExternalOutput").ap()
        B.dbg_hb = nc.dram_tensor("dbg_hb", [128, 16 * 512], BF16,
                                  kind="ExternalOutput").ap()
        B.dbg_comb = nc.dram_tensor("dbg_comb", [128, NCH * GRP], F32,
                                    kind="ExternalOutput").ap()
        B.dbg_gs = nc.dram_tensor("dbg_gs", [128, 80], F32,
                                  kind="ExternalOutput").ap()

    s_ld = B.sem("ld")
    s_mm2 = B.sem("mm2")
    s_ge = B.sem("ge")
    s_mk = B.sem("mk")
    s_oa = B.sem("oa")
    s_gt = B.sem("gt")

    identf = B.sb("sidf", [128, 128], F32)
    ones_col = B.sb("ones_col", [128, 1], F32)
    ones_row = B.sb("ones_row", [1, 128], F32)
    xb = B.sb("xb", [128, NCH * GRP], BF16)
    o_acc = B.sb("o_acc", [128, NCH * GRP], F32)
    w1b = [B.sb(f"w1b{i}", [128, NCH * FFH], BF16) for i in range(2)]
    w2b = B.sb("w2b", [128, (FFH // 128) * D], BF16)
    hb = B.sb("hb", [128, 16 * 512], BF16)
    stage = [B.sb(f"stg{i}", [128, FFH], F32) for i in range(2)]
    wrsb = B.sb("wrsb", [128, NCH * 16], BF16)
    wgT = B.sb("wgTsb", [16, NEXP], F32)
    wgnb = B.sb("wgnb", [16, NEXP], BF16)
    b2sb = B.sb("b2sb", [NEXP, D], BF16)
    b1sb = B.sb("b1sb", [128, FFH // 128], F32)
    xrTb = B.sb("xrTb", [16, GRP], BF16)
    mx = B.sb("mx", [128, 8], F32)
    mxn = B.sb("mxn", [128, 8], F32)
    gt = B.sb("gtb", [128, 64], F32)
    gs = B.sb("gs", [128, 8], F32)
    oh_tm = B.sb("oh_tm", [128, 9 * 8], F32)
    gat = B.sb("gat", [9, GRP], F32)
    ohb = B.sb("ohb", [NEXP, GRP], BF16)
    ssr = B.sb("ssr", [1, GRP], F32)
    maske = B.sb("maske", [128, GRP], BF16)
    B.orow = B.sb("orow", [1, GRP], F32)
    x1t = B.sb("x1t", [128, NCH * 128], F32)
    sqb = [stage[0][:, 0:512], stage[1][:, 0:512]]
    tmp = stage[0][:, 512:1024]
    tmp2 = B.sb("tmp2", [128, 128], F32)
    rows = stage[1][0:1, 512:2048]

    pA = B.ps("pA", [128, 1024], F32)
    pB = B.ps("pB", [128, 1024], F32)
    pC = B.ps("pC", [128, 1024], F32)
    pF = B.ps("pF", [128, 512], F32)
    B.make_fence(pF, ones_col, identf)

    P.dma(lambda e: e.dma_start(out=identf[:, :], in_=identf_d), inc=s_ld)
    P.dma(lambda e: e.dma_start(out=wgT[:, :], in_=wgT_d), inc=s_ld)
    P.dma(lambda e: e.dma_start(out=stage[0][0:NEXP, 0:D], in_=b2_d), inc=s_ld)
    P.on("vector", lambda e: e.memset(ones_col[:, :], 1.0))
    P.on("vector", lambda e: e.memset(ones_row[:, :], 1.0))
    B.make_bias_consts([LN_EPS, 1e-8])
    ld0 = (s_ld, s_ld.n)
    P.on("vector", lambda e: e.tensor_copy(out=b2sb[:, :],
                                           in_=stage[0][0:NEXP, 0:D]),
         wait=[ld0])
    emit_wstream(B, wred_d, wrsb, 0, NCH, 16, stage, FREE0)
    P.on("scalar", lambda e: e.activation(out=gt[0:16, 0:8], in_=wgT[:, :],
                                          func=AF.Square), wait=[ld0],
         inc=s_gt, drain=True)
    g1 = (B.sems["gt"], B.sems["gt"].n)
    P.on("tensor", lambda e: e.matmul(pC[0:1, 0:8], ones_col[0:16, :],
                                      gt[0:16, 0:8], start=True, stop=True),
         wait=[g1], inc=s_mm2, fence=True)
    g2 = (s_mm2, s_mm2.n)
    P.on("vector", lambda e: e.tensor_scalar_max(rows[:, 0:8], pC[0:1, 0:8],
                                                 1e-8), wait=[g2],
         inc=s_gt, drain=True)
    g2b = (s_gt, s_gt.n)
    P.on("scalar", lambda e: e.activation(out=rows[:, 0:8], in_=rows[:, 0:8],
                                          func=AF.Ln), wait=[g2b])
    P.on("scalar", lambda e: e.activation(out=rows[:, 0:8], in_=rows[:, 0:8],
                                          func=AF.Exp, scale=-0.5),
         inc=s_gt, drain=True)
    g3 = (B.sems["gt"], B.sems["gt"].n)
    P.on("tensor", lambda e: e.matmul(pC[0:16, 8:16], ones_row[0:1, 0:16],
                                      rows[:, 0:8], start=True, stop=True),
         wait=[g3], inc=s_mm2, fence=True)
    g4 = (s_mm2, s_mm2.n)
    P.on("vector", lambda e: e.tensor_tensor(out=wgnb[:, :], in0=wgT[:, :],
                                             in1=pC[0:16, 8:16], op=ALU.mult),
         wait=[g4], inc=s_gt, drain=True)
    P.block("l0f_consts")

    for g in range(4):
        _l0f_group(B, g, x1, W1_d, b1T_d, W2_d, x2_d,
                   xb, o_acc, w1b, w2b, hb, stage, wrsb, wgnb, b2sb, b1sb,
                   xrTb, mx, mxn, gt, gs, oh_tm, gat, ohb, ssr, maske,
                   x1t, sqb, tmp, tmp2, rows, identf, ones_col, ones_row,
                   pA, pB, pC)
    return B.finish() if standalone else None


def _l0f_group(B, g, x1, W1_d, b1T_d, W2_d, x2_d,
               xb, o_acc, w1b, w2b, hb, stage, wrsb, wgnb, b2sb, b1sb,
               xrTb, mx, mxn, gt, gs, oh_tm, gat, ohb, ssr, maske,
               x1t, sqb, tmp, tmp2, rows, identf, ones_col, ones_row,
               pA, pB, pC):
    nc, P = B.nc, B.P
    seq, t0 = g // 2, (g % 2) * GRP
    s_mm2, s_ge, s_mk, s_oa, s_gt = (B.sems[k] for k in
                                     ("mm2", "ge", "mk", "oa", "gt"))
    s_wcp = B.sem("wcp")

    xbv = xb[:, :].rearrange("p (c s) -> p c s", s=GRP)
    emit_xtiles(B, x1[seq][t0:t0 + GRP, :], GRP // 128,
                lambda tt: xbv[:, :, tt * 128:tt * 128 + 128],
                stage, pA, pB, identf)
    P.block(f"g{g}_xb")
    if DEBUG_L0F and g == 0:
        sd = B.sem("dbgf")
        P.dma(lambda e: e.dma_start(out=B.dbg_xb, in_=xb[:, :]), inc=sd)
        P.on("sync", lambda e, v=sd.n: e.wait_ge(sd.h, v))
        P.block("dbgf_xb")

    for half in range(GRP // 512):
        for kc in range(NCH):
            P.on("tensor", lambda e, half=half, kc=kc: e.matmul(
                pC[0:16, half * 512:half * 512 + 512],
                wrsb[:, kc * 16:(kc + 1) * 16],
                xb[:, kc * GRP + half * 512:kc * GRP + half * 512 + 512],
                start=(kc == 0), stop=(kc == NCH - 1)),
                inc=s_mm2 if kc == NCH - 1 else None,
                fence=(kc == NCH - 1 and half == GRP // 512 - 1))
    xr_mm = (s_mm2, s_mm2.n)
    P.on("vector", lambda e: e.tensor_copy(out=xrTb[:, :], in_=pC[0:16, 0:GRP]),
         wait=[xr_mm], inc=s_gt, drain=True)
    xr_f = (s_gt, s_gt.n)
    for i in range(8):
        P.on("tensor", lambda e, i=i: e.matmul(
            pA[:, i * 8:(i + 1) * 8], xrTb[:, i * 128:(i + 1) * 128],
            wgnb[:, :], start=True, stop=True),
            wait=[xr_f] if i == 0 else [],
            inc=s_mm2 if i == 7 else None, fence=(i == 7))
    lg_f = (s_mm2, s_mm2.n)
    for i in range(8):
        P.on("vector", lambda e, i=i: e.tensor_reduce(
            mx[:, i:i + 1], pA[:, i * 8:(i + 1) * 8],
            axis=mybir.AxisListType.X, op=ALU.max),
            wait=[lg_f] if i == 0 else [])
    for i in range(8):
        P.on("scalar", lambda e, i=i: e.activation(
            out=gt[:, i * 8:(i + 1) * 8], in_=pA[:, i * 8:(i + 1) * 8],
            func=AF.Exp),
            wait=[lg_f] if i == 0 else [],
            inc=s_ge if i == 7 else None, drain=(i == 7))
    gs_f = (s_ge, s_ge.n)
    for i in range(8):
        P.on("vector", lambda e, i=i: e.tensor_reduce(
            gs[:, i:i + 1], gt[:, i * 8:(i + 1) * 8],
            axis=mybir.AxisListType.X, op=ALU.add),
            wait=[gs_f] if i == 0 else [])
    for i in range(8):
        P.on("vector", lambda e, i=i: e.tensor_reduce(
            mxn[:, i:i + 1], gt[:, i * 8:(i + 1) * 8],
            axis=mybir.AxisListType.X, op=ALU.max))
    for i in range(8):
        P.on("vector", lambda e, i=i: e.reciprocal(
            out=gs[:, i:i + 1], in_=gs[:, i:i + 1]))
    for i in range(8):
        P.on("vector", lambda e, i=i: e.tensor_scalar(
            out=oh_tm[:, i * 9:i * 9 + 8], in0=pA[:, i * 8:(i + 1) * 8],
            scalar1=mx[:, i:i + 1], scalar2=None, op0=ALU.is_equal))
        P.on("vector", lambda e, i=i: e.tensor_tensor(
            out=oh_tm[:, i * 9 + 8:i * 9 + 9], in0=mxn[:, i:i + 1],
            in1=gs[:, i:i + 1], op=ALU.mult),
            inc=s_gt if i == 7 else None, drain=(i == 7))
    oh_f = (s_gt, s_gt.n)
    for i in range(8):
        P.on("tensor", lambda e, i=i: e.transpose(
            pC[0:9, i * 128:(i + 1) * 128], oh_tm[:, i * 9:(i + 1) * 9],
            identf[:, :]),
            wait=[oh_f] if i == 0 else [],
            inc=s_mm2 if i == 7 else None, fence=(i == 7))
    tr_f = (s_mm2, s_mm2.n)
    P.on("vector", lambda e: e.tensor_copy(out=gat[:, :], in_=pC[0:9, 0:GRP]),
         wait=[tr_f])
    P.on("vector", lambda e: e.tensor_copy(out=ohb[:, :], in_=gat[0:8, :]),
         inc=s_gt, drain=True)
    gat_f = (s_gt, s_gt.n)
    ssr_f = P.dma(lambda e: e.dma_start(out=ssr[:, :], in_=gat[8:9, :]),
                  wait=[gat_f, B.state.get("ssr_free", FREE0)],
                  inc=B.dsem("sr", 1)[0])
    B.state["ssr_f"] = ssr_f
    if DEBUG_L0F and g == 0:
        sd = B.sem("dbgf")
        P.dma(lambda e: e.dma_start(out=B.dbg_gat, in_=gat[:, :]),
              wait=[gat_f], inc=sd)
        P.dma(lambda e: e.dma_start(out=B.dbg_gs[:, 0:64], in_=gt[:, :]),
              wait=[gat_f], inc=sd)
        P.dma(lambda e: e.dma_start(out=B.dbg_gs[:, 64:72], in_=gs[:, :]),
              wait=[gat_f], inc=sd)
        P.dma(lambda e: e.dma_start(out=B.dbg_gs[:, 72:80], in_=mx[:, :]),
              wait=[gat_f], inc=sd)
        P.on("sync", lambda e, v=sd.n: e.wait_ge(sd.h, v))
    P.block(f"g{g}_gate")

    st_w1 = B.state.setdefault("w1_free", [FREE0, FREE0])
    st_w2 = B.state.setdefault("w2_free", [FREE0])
    st_b1 = B.state.setdefault("b1_free", [FREE0])
    st_hb = B.state.setdefault("hb_free", [FREE0])
    st_me = B.state.setdefault("maske_free", [FREE0])
    s_ld = B.sem("ld")

    w1cast = {}
    w1cast[0] = emit_wstream(B, W1_d[0], w1b[0], 0, NCH, FFH, stage, st_w1[0])
    for e_ in range(NEXP):
        wb = w1b[e_ % 2]
        b1_f = P.dma(lambda e, e_=e_: e.dma_start(out=b1sb[:, :], in_=b1T_d[e_]),
                     wait=[st_b1[0]], inc=B.dsem("b1", 1)[0])
        orow = B.orow
        or_f = P.dma(lambda e, e_=e_, orow=orow: e.dma_start(
            out=orow[:, :], in_=gat[e_:e_ + 1, :]),
            wait=[gat_f, B.state.get("orow_free", FREE0)],
            inc=B.dsem("or", 1)[0])
        for half in range(GRP // 512):
            P.on("tensor", lambda e, e_=e_, half=half: e.matmul(
                pC[:, half * 512:half * 512 + 512], ones_row[0:1, :],
                orow[:, half * 512:half * 512 + 512],
                start=True, stop=True),
                wait=[or_f, st_me[0]] if half == 0 else [],
                inc=s_mm2 if half == 1 else None, fence=(half == 1))
        B.state["orow_free"] = (s_mm2, s_mm2.n)
        mk_mm = (s_mm2, s_mm2.n)
        mk_f = P.on("vector", lambda e: e.tensor_copy(out=maske[:, :],
                                                      in_=pC[:, 0:GRP]),
                    wait=[mk_mm], inc=s_mk, drain=True)
        w2cast = emit_wstream(B, W2_d[e_], w2b, 0, FFH // 128, D, stage,
                              st_w2[0])
        if e_ + 1 < NEXP:
            w1cast[e_ + 1] = emit_wstream(B, W1_d[e_ + 1], w1b[(e_ + 1) % 2],
                                          0, NCH, FFH, stage, st_w1[(e_ + 1) % 2])
        for nt in range(GRP // 512):
            n0 = nt * 512
            for hm in range(16):
                pp, off = (pA, 0) if hm % 2 == 0 else (pA, 512)
                for kc in range(NCH):
                    P.on("tensor", lambda e, wb=wb, pp=pp, off=off, hm=hm,
                         kc=kc, n0=n0: e.matmul(
                             pp[:, off:off + 512],
                             wb[:, kc * FFH + hm * 128:kc * FFH + hm * 128 + 128],
                             xb[:, kc * GRP + n0:kc * GRP + n0 + 512],
                             start=(kc == 0), stop=(kc == NCH - 1)),
                         wait=([w1cast[e_], (s_ge, B.state.get(("hps", hm % 2), 0))]
                               if kc == 0 else []),
                         inc=s_mm2 if kc == NCH - 1 else None,
                         fence=(kc == NCH - 1 and hm == 15))
                h_mm = (s_mm2, s_mm2.n)
                ge_f = P.on("scalar", lambda e, pp=pp, off=off, hm=hm, e_=e_:
                            e.activation(out=hb[:, hm * 512:(hm + 1) * 512],
                                         in_=pp[:, off:off + 512], func=AF.Gelu,
                                         bias=b1sb[:, hm:hm + 1]),
                            wait=[h_mm, b1_f, st_hb[0]], inc=s_ge,
                            drain=(hm == 15))
                B.state[("hps", hm % 2)] = s_ge.n
            ge_all = (s_ge, s_ge.n)
            for hm in range(16):
                mk2 = P.on("vector", lambda e, hm=hm, n0=n0: e.tensor_tensor(
                    out=hb[:, hm * 512:(hm + 1) * 512],
                    in0=hb[:, hm * 512:(hm + 1) * 512],
                    in1=maske[:, n0:n0 + 512], op=ALU.mult),
                    wait=[ge_all, mk_f] if hm == 0 else [],
                    inc=s_mk if hm == 15 else None, drain=(hm == 15))
            mk_all = (s_mk, s_mk.n)
            for om in range(NCH):
                pp, off = (pB, 0) if om % 2 == 0 else (pB, 512)
                for kh in range(16):
                    P.on("tensor", lambda e, pp=pp, off=off, om=om, kh=kh:
                         e.matmul(pp[:, off:off + 512],
                                  w2b[:, kh * D + om * 128:kh * D + om * 128 + 128],
                                  hb[:, kh * 512:(kh + 1) * 512],
                                  start=(kh == 0), stop=(kh == 15)),
                         wait=([mk_all, w2cast,
                                (s_oa, B.state.get(("ops", om % 2), 0))]
                               if kh == 0 else []),
                         inc=s_mm2 if kh == 15 else None,
                         fence=(kh == 15 and om == NCH - 1))
                o_mm = (s_mm2, s_mm2.n)
                if e_ == 0:
                    oa_f = P.on("vector", lambda e, pp=pp, off=off, om=om, n0=n0:
                                e.tensor_copy(
                                    out=o_acc[:, om * GRP + n0:om * GRP + n0 + 512],
                                    in_=pp[:, off:off + 512]),
                                wait=[o_mm], inc=s_oa)
                else:
                    oa_f = P.on("vector", lambda e, pp=pp, off=off, om=om, n0=n0:
                                e.tensor_tensor(
                                    out=o_acc[:, om * GRP + n0:om * GRP + n0 + 512],
                                    in0=pp[:, off:off + 512],
                                    in1=o_acc[:, om * GRP + n0:om * GRP + n0 + 512],
                                    op=ALU.add),
                                wait=[o_mm], inc=s_oa)
                B.state[("ops", om % 2)] = s_oa.n
            st_hb[0] = (s_mm2, s_mm2.n)
        st_w1[e_ % 2] = (s_mm2, s_mm2.n)
        st_w2[0] = (s_mm2, s_mm2.n)
        st_b1[0] = (s_ge, s_ge.n)
        st_me[0] = (s_mk, s_mk.n)
        if DEBUG_L0F and g == 0 and e_ == 0:
            sd = B.sem("dbgf")
            P.dma(lambda e: e.dma_start(out=B.dbg_hb, in_=hb[:, :]),
                  wait=[(B.sems["mk"], B.sems["mk"].n)], inc=sd)
            P.on("sync", lambda e, v=sd.n: e.wait_ge(sd.h, v))
    P.block(f"g{g}_exp")
    if DEBUG_L0F and g == 0:
        sd = B.sem("dbgf")
        P.dma(lambda e: e.dma_start(out=B.dbg_oacc, in_=o_acc[:, :]), inc=sd)
        P.on("sync", lambda e, v=sd.n: e.wait_ge(sd.h, v))
        P.block("dbgf_oacc")

    s_cb = B.sem("cb")
    x1tv = x1t[:, :].rearrange("p (c j) -> p c j", j=128)
    for tt in range(GRP // 128):
        x_f = emit_xtiles(B, x1[seq][t0 + tt * 128:t0 + (tt + 1) * 128, :], 1,
                          lambda _: x1tv, stage, pA, pB, identf,
                          dst_free=(s_cb, s_cb.n))
        for c in range(NCH):
            P.on("tensor", lambda e, c=c, tt=tt: e.matmul(
                pC[:, c * 128:(c + 1) * 128], b2sb[:, c * 128:(c + 1) * 128],
                ohb[:, tt * 128:(tt + 1) * 128], start=True, stop=True),
                wait=[(B.sems["cb"], B.state.get("pc_free", 0))] if c == 0 else [],
                inc=s_mm2 if c == NCH - 1 else None)
        P.on("tensor", lambda e, tt=tt: e.matmul(
            pB[:, 0:128], ones_row[0:1, :], ssr[:, tt * 128:(tt + 1) * 128],
            start=True, stop=True),
            wait=[B.state["ssr_f"]] if tt == 0 else [], inc=s_mm2, fence=True)
        cb_mm = (s_mm2, s_mm2.n)
        for c in range(NCH):
            sl = slice(c * GRP + tt * 128, c * GRP + tt * 128 + 128)
            P.on("vector", lambda e, sl=sl: e.tensor_tensor(
                out=tmp2[:, :], in0=o_acc[:, sl], in1=pB[:, 0:128],
                op=ALU.mult),
                wait=[cb_mm, x_f] if c == 0 else [])
            P.on("vector", lambda e, c=c: e.tensor_tensor(
                out=tmp2[:, :], in0=tmp2[:, :], in1=pC[:, c * 128:(c + 1) * 128],
                op=ALU.add))
            P.on("vector", lambda e, sl=sl, c=c: e.tensor_tensor(
                out=o_acc[:, sl], in0=tmp2[:, :], in1=x1tv[:, c, :],
                op=ALU.add),
                inc=s_cb if c == NCH - 1 else None)
        B.state["pc_free"] = B.sems["cb"].n
    P.block(f"g{g}_comb")
    if DEBUG_L0F and g == 0:
        sd = B.sem("dbgf")
        P.dma(lambda e: e.dma_start(out=B.dbg_comb, in_=o_acc[:, :]), inc=sd)
        P.on("sync", lambda e, v=sd.n: e.wait_ge(sd.h, v))
        P.block("dbgf_comb")

    emit_postln_inplace(B, o_acc, pA, pC, sqb, tmp, rows, ones_col, ones_row,
                        width=GRP)
    P.block(f"g{g}_ln")

    ostage = [stage[0][:, 0:D], stage[1][:, 0:D]]
    recs = emit_store_fm_to_tm(B, o_acc, x2_d[seq][t0:t0 + GRP, :], ostage,
                               pA, pB, identf, width=GRP)
    for sm, v in recs:
        P.on("sync", lambda e, sm=sm, v=v: e.wait_ge(sm.h, v))
    P.block(f"g{g}_store")


def run_l0f(x1, w_red, wg, W1, b1, W2, b2):
    cc = host_constants()
    nc = build_l0f()
    x1 = np.ascontiguousarray(np.asarray(x1, np.float32).reshape(NCORE, BPC, S, D))
    b1T = np.ascontiguousarray(
        np.asarray(b1, np.float32).reshape(NEXP, FFH // 128, 128).transpose(0, 2, 1))
    base = dict(
        wred=np.ascontiguousarray(np.asarray(w_red, np.float32)),
        wgT=np.ascontiguousarray(np.asarray(wg, np.float32).T),
        W1=np.ascontiguousarray(np.asarray(W1, np.float32)),
        b1T=b1T,
        W2=np.ascontiguousarray(np.asarray(W2, np.float32)),
        b2=np.ascontiguousarray(np.asarray(b2, np.float32)),
        identf=cc["identf"])
    in_maps = [dict(base, x1=x1[i]) for i in range(NCORE)]
    res = run_bass_kernel_spmd(nc, in_maps, core_ids=list(range(NCORE)))
    out = np.concatenate([r["x2"] for r in res.results], axis=0)
    return out.reshape(16, S, D)



STRIPW = 4608


def host_l1_consts():
    cc = host_constants()
    if "stripb" not in cc:
        p = np.arange(128, dtype=np.float64)[:, None]
        c = np.arange(STRIPW, dtype=np.float64)[None, :]
        cc["stripb"] = np.ascontiguousarray(
            (-np.abs(c - p - 2047.0)).astype(np.float32))
        cc["cinf"] = np.ascontiguousarray(
            np.where(c < p + 2047.0, -4.2e6, 0.0).astype(np.float32))
    return cc


DEBUG_L1A = False


def build_l1a(B=None):
    standalone = B is None
    if standalone:
        B = Builder()
    nc, P = B.nc, B.P
    x2 = nc.dram_tensor("x2", [BPC, S, D], F32, kind="ExternalInput").ap()
    wqT_d = nc.dram_tensor("wqT", [D, D], F32, kind="ExternalInput").ap()
    wkT_d = nc.dram_tensor("wkT", [D, D], F32, kind="ExternalInput").ap()
    wvT_d = nc.dram_tensor("wvT", [D, D], F32, kind="ExternalInput").ap()
    woT_d = nc.dram_tensor("woT", [D, D], F32, kind="ExternalInput").ap()
    boT = nc.dram_tensor("boT", [128, NCH], F32, kind="ExternalInput").ap()
    identf_d = nc.dram_tensor("identf", [128, 128], F32, kind="ExternalInput").ap()
    strip_d = nc.dram_tensor("stripb", [128, STRIPW], F32, kind="ExternalInput").ap()
    cinf_d = nc.dram_tensor("cinf", [128, STRIPW], F32, kind="ExternalInput").ap()
    x3_d = nc.dram_tensor("x3", [BPC, S, D], F32, kind="ExternalOutput").ap()
    ctx_d = nc.dram_tensor("ctxd", [NCH, 128, S], BF16,
                           kind="ExternalOutput" if DEBUG_L1A else "Internal").ap()
    if DEBUG_L1A:
        B.dbg_qf = nc.dram_tensor("dbg_qf", [128, NCH * S], BF16,
                                  kind="ExternalOutput").ap()
        B.dbg_kf = nc.dram_tensor("dbg_kf", [128, NCH * S], BF16,
                                  kind="ExternalOutput").ap()
        B.dbg_v65 = nc.dram_tensor("dbg_v65", [128, NTT * H * 65], BF16,
                                   kind="ExternalOutput").ap()
        B.dbg_z = nc.dram_tensor("dbg_z", [128, 1024], F32,
                                 kind="ExternalOutput").ap()
        B.dbg_p = nc.dram_tensor("dbg_p", [128, 1024], BF16,
                                 kind="ExternalOutput").ap()
        B.dbg_st = nc.dram_tensor("dbg_st", [128, 1024], F32,
                                  kind="ExternalOutput").ap()
        B.dbg_pre = nc.dram_tensor("dbg_pre", [128, NCH * S], F32,
                                   kind="ExternalOutput").ap()
        B.dbg_cx = nc.dram_tensor("dbg_cx", [128, NCH * 512], BF16,
                                  kind="ExternalOutput").ap()
        B.dbg_sh = nc.dram_tensor("dbg_sh", [128, 1024], F32,
                                  kind="ExternalOutput").ap()

    s_ld = B.sem("ld")
    identf = B.sb("sidf", [128, 128], F32)
    ones_col = B.sb("ones_col", [128, 1], F32)
    ones_row = B.sb("ones_row", [1, 128], F32)
    bosb = B.sb("bosb", [128, NCH], F32)

    qf = B.sb("qf", [128, NCH * S], BF16)
    kf = B.sb("kf", [128, NCH * S], BF16)
    v65 = B.sb("v65", [128, NTT * H * 65], BF16)

    pA = B.ps("pA", [128, 1024], F32)
    pB = B.ps("pB", [128, 1024], F32)
    pC = B.ps("pC", [128, 1024], F32)
    pF = B.ps("pF", [128, 512], F32)
    B.make_fence(pF, ones_col, identf)

    P.dma(lambda e: e.dma_start(out=identf[:, :], in_=identf_d), inc=s_ld)
    P.dma(lambda e: e.dma_start(out=bosb[:, :], in_=boT), inc=s_ld)

    P.on("vector", lambda e: e.memset(ones_col[:, :], 1.0))
    P.on("vector", lambda e: e.memset(ones_row[:, :], 1.0))
    B.make_bias_consts([LN_EPS, -30.0])
    P.on("sync", lambda e, v=s_ld.n: e.wait_ge(s_ld.h, v))
    P.block("l1_consts")

    for seq in range(BPC):
        _l1a_seq(B, seq, x2, x3_d, ctx_d, wqT_d, wkT_d, wvT_d, woT_d,
                 identf, ones_col, ones_row, bosb, strip_d, cinf_d, qf, kf, v65,
                 pA, pB, pC)
    return B.finish() if standalone else None


def _l1a_seq(B, seq, x2, x3_d, ctx_d, wqT_d, wkT_d, wvT_d, woT_d,
             identf, ones_col, ones_row, bosb, strip_d, cinf_d, qf, kf, v65,
             pA, pB, pC):
    nc, P = B.nc, B.P
    s_qk, s_pj, s_wc = B.sem("qk"), B.sem("pj"), B.sem("wc")
    s_ld = B.sem("ld")

    with ExitStack() as c2:
        en2 = c2.enter_context
        xb = en2(nc.sbuf_tensor(f"xb{seq}", [128, NCH * S], BF16))
        wsb = en2(nc.sbuf_tensor(f"wsb{seq}", [128, NCH * D], BF16))
        stage = [en2(nc.sbuf_tensor(f"stg{seq}_{i}", [128, D], F32))
                 for i in range(2)]

        xbv = xb[:, :].rearrange("p (c s) -> p c s", s=S)
        emit_xtiles(B, x2[seq], NTT, lambda tt: xbv[:, :, tt * 128:tt * 128 + 128],
                    stage, pA, pB, identf)
        P.block(f"q{seq}_xb")

        for wi, (wd, dst) in enumerate(((wqT_d, qf), (wkT_d, kf))):
            w_f = emit_wstream(B, wd, wsb, 0, NCH, D, stage,
                               B.state.get("l1w_free", FREE0))
            slot_free = [FREE0] * 2
            for tg in range(4):
                for mc in range(NCH):
                    pp, off = (pA, 0) if mc % 2 == 0 else (pA, 512)
                    qk_f = None
                    for kc in range(NCH):
                        r = P.on("tensor",
                                 lambda e, pp=pp, off=off, kc=kc, mc=mc, tg=tg:
                                 e.matmul(pp[:, off:off + 512],
                                          wsb[:, kc * D + mc * 128:
                                              kc * D + mc * 128 + 128],
                                          xb[:, kc * S + tg * 512:
                                             kc * S + tg * 512 + 512],
                                          start=(kc == 0), stop=(kc == NCH - 1)),
                                 wait=[w_f, slot_free[mc % 2]] if kc == 0 else [],
                                 inc=s_qk if kc == NCH - 1 else None,
                                 fence=(kc == NCH - 1))
                        qk_f = r or qk_f
                    slot_free[mc % 2] = P.on(
                        "scalar", lambda e, pp=pp, off=off, mc=mc, tg=tg, dst=dst:
                        e.activation(out=dst[:, mc * S + tg * 512:
                                             mc * S + tg * 512 + 512],
                                     in_=pp[:, off:off + 512], func=AF.Copy),
                        wait=[qk_f], inc=s_pj, drain=True)
            B.state["l1w_free"] = (s_qk, s_qk.n)
        w_f = emit_wstream(B, wvT_d, wsb, 0, NCH, D, stage,
                           B.state.get("l1w_free", FREE0))
        P.on("vector", lambda e: e.memset(v65[:, :], 1.0))
        v65v = v65[:, :].rearrange("p (t h c) -> p t h c", h=H, c=65)
        for tt in range(NTT):
            for half in range(2):
                pp = pB if half == 0 else pC
                qk_f = None
                for kc in range(NCH):
                    r = P.on("tensor", lambda e, pp=pp, kc=kc, half=half, tt=tt:
                             e.matmul(pp[:, 0:512],
                                      xb[:, kc * S + tt * 128:kc * S + tt * 128 + 128],
                                      wsb[:, kc * D + half * 512:
                                          kc * D + half * 512 + 512],
                                      start=(kc == 0), stop=(kc == NCH - 1))
                             if False else
                             e.matmul(pp[:, half * 0:512][:, 0:512] if False else pp[:, 0:512],
                                      xb[:, kc * S + tt * 128:kc * S + tt * 128 + 128],
                                      wsb[:, kc * D + half * 512:
                                          kc * D + half * 512 + 512],
                                      start=(kc == 0), stop=(kc == NCH - 1)),
                             wait=[w_f, B.state.get(("vps", half), FREE0)]
                             if kc == 0 else [],
                             inc=s_qk if kc == NCH - 1 else None,
                             fence=(kc == NCH - 1))
                    qk_f = r or qk_f
                ev = P.on("scalar", lambda e, pp=pp, tt=tt, half=half:
                          e.activation(
                              out=v65v[:, tt, half * 8:(half + 1) * 8, 0:64],
                              in_=pp[:, 0:512].rearrange("p (h d) -> p h d", d=64),
                              func=AF.Copy),
                          wait=[qk_f], inc=s_pj, drain=True)
                B.state[("vps", half)] = ev
        P.block(f"q{seq}_proj")
        if DEBUG_L1A and seq == 0:
            sd = B.sem("dbga")
            P.dma(lambda e: e.dma_start(out=B.dbg_qf, in_=qf[:, :]), inc=sd)
            P.dma(lambda e: e.dma_start(out=B.dbg_kf, in_=kf[:, :]), inc=sd)
            P.dma(lambda e: e.dma_start(out=B.dbg_v65, in_=v65[:, :]), inc=sd)
            P.on("sync", lambda e, v=sd.n: e.wait_ge(sd.h, v))
            P.block("dbga_proj")

    with ExitStack() as c3:
        en3 = c3.enter_context
        striph = en3(nc.sbuf_tensor(f"sh{seq}", [128, STRIPW], F32))
        stripb = en3(nc.sbuf_tensor(f"sb{seq}", [128, STRIPW], F32))
        cinf = en3(nc.sbuf_tensor(f"ci{seq}", [128, STRIPW], F32))
        s_ld2 = B.sem("ld")
        st_f = P.dma(lambda e: e.dma_start(out=stripb[:, :], in_=strip_d),
                     wait=[B.state.get("strip_ld_free", FREE0)], inc=s_ld2)
        ci_f = P.dma(lambda e: e.dma_start(out=cinf[:, :], in_=cinf_d),
                     inc=s_ld2)
        B.state["strips_ld"] = (s_ld2, s_ld2.n)
        zb = [en3(nc.sbuf_tensor(f"zb{seq}_{i}", [128, 1024], F32)) for i in range(2)]
        tb = [en3(nc.sbuf_tensor(f"tb{seq}_{i}", [128, 1024], F32)) for i in range(2)]
        ptsb = [en3(nc.sbuf_tensor(f"pt{seq}_{i}", [128, 1024], BF16))
                for i in range(2)]
        ctxs = en3(nc.sbuf_tensor(f"cs{seq}", [64, 1024], BF16))
        bcsb = en3(nc.sbuf_tensor(f"bc{seq}", [65, 1024], F32))
        hs = {
            "pt_free": [FREE0, FREE0],
            "st_free": [FREE0, FREE0],
            "zb_free": [FREE0, FREE0],
            "tb_free": [FREE0, FREE0],
            "ctx_free": FREE0,
            "ctxs_free": FREE0,
            "strip_free": FREE0,
            "bc_free": FREE0,
        }
        for h in range(H):
            _l1a_head(B, seq, h, hs, qf, kf, v65, striph, stripb, cinf, zb, tb,
                      ptsb, ctxs, bcsb, ctx_d, ones_row, pA, pB, pC)
        P.block(f"q{seq}_attn")

    with ExitStack() as c4:
        en4 = c4.enter_context
        xfm = en4(nc.sbuf_tensor(f"xf{seq}", [128, NCH * S], F32))
        wsb = en4(nc.sbuf_tensor(f"wo{seq}", [128, NCH * D], BF16))
        stage = [en4(nc.sbuf_tensor(f"st2{seq}_{i}", [128, D], F32))
                 for i in range(2)]
        ctxsb = en4(nc.sbuf_tensor(f"cx{seq}", [128, NCH * 512], BF16))
        tmp = en4(nc.sbuf_tensor(f"tmp{seq}", [128, 512], F32))
        sqb = [stage[0][:, 0:512], stage[1][:, 0:512]]
        rows = en4(nc.sbuf_tensor(f"rw{seq}", [1, 1536], F32))

        xfv = xfm[:, :].rearrange("p (c s) -> p c s", s=S)
        emit_xtiles(B, x2[seq], NTT, lambda tt: xfv[:, :, tt * 128:tt * 128 + 128],
                    stage, pA, pB, identf)
        w_f = emit_wstream(B, woT_d, wsb, 0, NCH, D, stage,
                           B.state.get("l1w_free", FREE0))
        B.state["l1w_free"] = FREE0
        P.block(f"q{seq}_epi0")

        s_cd = B.sem("cd")
        slots = [(pA, 0), (pA, 512), (pB, 0), (pB, 512)]
        slot_free = [FREE0] * 4
        cx_free = [FREE0]
        idx = 0
        for tg in range(4):
            cx = ctxsb
            ld_f = P.dma(lambda e, cx=cx, tg=tg: e.dma_start(
                out=cx[:, :].rearrange("p (c s) -> p c s", c=NCH),
                in_=ctx_d[:, :, tg * 512:(tg + 1) * 512].rearrange(
                    "c p s -> p c s")),
                wait=[cx_free[0], (s_cd, s_cd.n)], inc=s_ld)
            for mc in range(NCH):
                pp, off = slots[idx % 4]
                qk_f = None
                for kc in range(NCH):
                    r = P.on("tensor",
                             lambda e, pp=pp, off=off, kc=kc, mc=mc, cx=cx:
                             e.matmul(pp[:, off:off + 512],
                                      wsb[:, kc * D + mc * 128:
                                          kc * D + mc * 128 + 128],
                                      cx[:, kc * 512:(kc + 1) * 512],
                                      start=(kc == 0), stop=(kc == NCH - 1)),
                             wait=[ld_f, w_f, slot_free[idx % 4]]
                             if kc == 0 else [],
                             inc=s_qk if kc == NCH - 1 else None,
                             fence=(kc == NCH - 1))
                    qk_f = r or qk_f
                if mc == NCH - 1:
                    cx_free[0] = qk_f
                P.on("vector", lambda e, pp=pp, off=off, mc=mc:
                     e.tensor_scalar(out=tmp[:, :], in0=pp[:, off:off + 512],
                                     scalar1=bosb[:, mc:mc + 1], scalar2=None,
                                     op0=ALU.add),
                     wait=[qk_f])
                slot_free[idx % 4] = P.on(
                    "vector", lambda e, mc=mc, tg=tg:
                    e.tensor_tensor(out=xfm[:, fmc(mc, tg * 512, 512)],
                                    in0=tmp[:, :],
                                    in1=xfm[:, fmc(mc, tg * 512, 512)],
                                    op=ALU.add),
                    inc=s_pj)
                idx += 1
            if DEBUG_L1A and seq == 1 and tg == 3:
                sd = B.sem("dbga")
                P.dma(lambda e: e.dma_start(out=B.dbg_cx, in_=ctxsb[:, :]),
                      wait=[(B.sems["qk"], B.sems["qk"].n)], inc=sd)
                cx_free[0] = (sd, sd.n)
        P.block(f"q{seq}_wo")
        if DEBUG_L1A and seq == 1:
            sd = B.sem("dbga")
            P.dma(lambda e: e.dma_start(out=B.dbg_pre, in_=xfm[:, :]), inc=sd)
            P.on("sync", lambda e, v=sd.n: e.wait_ge(sd.h, v))
            P.block("dbga_pre")

        emit_postln_inplace(B, xfm, pA, pC, sqb, tmp, rows, ones_col, ones_row)
        P.block(f"q{seq}_ln")

        ostage = [stage[0][:, 0:D], stage[1][:, 0:D]]
        recs = emit_store_fm_to_tm(B, xfm, x3_d[seq], ostage, pA, pB, identf)
        for sm, v in recs:
            P.on("sync", lambda e, sm=sm, v=v: e.wait_ge(sm.h, v))
        P.block(f"q{seq}_store")


def _l1a_head(B, seq, h, hs, qf, kf, v65, striph, stripb, cinf, zb, tb, ptsb,
              ctxs, bcsb, ctx_d, ones_row, pA, pB, pC):
    nc, P = B.nc, B.P
    s_qk, s_ex, s_av, s_ce = (B.sem(n) for n in ("qk2", "ex", "av", "ce"))
    s_zz, s_aa, s_tt = (B.sem(n) for n in ("zz", "aa", "tt2"))
    s_cd = B.sem("cd")
    slope = 2.0 ** (-(h + 1) * 8.0 / H)

    c = h // 2
    pbase = 64 * (h % 2)
    qsl = lambda t0, n: qf[pbase:pbase + 64, c * S + t0:c * S + t0 + n]
    ksl = lambda t0, n: kf[pbase:pbase + 64, c * S + t0:c * S + t0 + n]

    P.on("vector", lambda e: e.tensor_scalar_mul(
        striph[:, :], stripb[:, :], slope / 240.0),
        wait=[hs["strip_free"], B.state["strips_ld"]])
    sp_f = P.on("vector", lambda e: e.tensor_tensor(
        out=striph[:, :], in0=striph[:, :], in1=cinf[:, :], op=ALU.add),
        inc=s_zz, drain=True)

    for qcp in range(2):
        qlo = 2 * qcp
        q0 = qlo * 512
        ktmax = 4 * (qlo + 1) + 3

        def emit_qk(kt):
            stp = pA if kt % 2 == 0 else pB
            qk_f = None
            for qc2 in range(2):
                r = P.on("tensor", lambda e, stp=stp, kt=kt, qc2=qc2, q0=q0:
                         e.matmul(stp[:, qc2 * 512:qc2 * 512 + 512],
                                  ksl(kt * 128, 128),
                                  qsl(q0 + qc2 * 512, 512),
                                  start=True, stop=True),
                         wait=[hs["st_free"][kt % 2]] if qc2 == 0 else [],
                         inc=s_qk if qc2 == 1 else None)
                qk_f = r or qk_f
            return qk_f

        def emit_z(kt, qk_f):
            stp = pA if kt % 2 == 0 else pB
            z = zb[kt % 2]
            toff = 2047 - kt * 128 + q0
            zz_f = P.on("vector", lambda e, stp=stp, z=z, toff=toff:
                        e.scalar_tensor_tensor(
                            out=z[:, :], in0=stp[:, :], scalar=1.0 / 240.0,
                            in1=striph[:, toff:toff + 1024],
                            op0=ALU.mult, op1=ALU.add),
                        wait=[qk_f, sp_f, hs["zb_free"][kt % 2]],
                        inc=s_zz, drain=True)
            hs["st_free"][kt % 2] = zz_f
            return zz_f

        def emit_tanh_exp(kt, aa_f):
            z = zb[kt % 2]
            t = tb[kt % 2]
            pt = ptsb[kt % 2]
            P.on("scalar", lambda e, z=z, t=t: e.activation(
                out=t[:, :], in_=z[:, :], func=AF.Tanh),
                wait=[aa_f, hs["tb_free"][kt % 2]])
            ex_f = P.on("scalar", lambda e, t=t, pt=pt: e.activation(
                out=pt[:, :], in_=t[:, :], func=AF.Exp, scale=30.0,
                bias=B.bias(-30.0)),
                wait=[hs["pt_free"][kt % 2]], inc=s_ex, drain=True)
            hs["zb_free"][kt % 2] = ex_f
            hs["tb_free"][kt % 2] = ex_f
            return ex_f

        def emit_av(kt, ex_f):
            pt = ptsb[kt % 2]
            av_f = None
            for qc2 in range(2):
                r = P.on("tensor",
                         lambda e, pt=pt, kt=kt, qc2=qc2, ktmax=ktmax:
                         e.matmul(pC[0:65, qc2 * 512:qc2 * 512 + 512],
                                  v65[:, (kt * H + h) * 65:(kt * H + h) * 65 + 65],
                                  pt[:, qc2 * 512:qc2 * 512 + 512],
                                  start=(kt == 0), stop=(kt == ktmax)),
                         wait=[ex_f, hs["ctx_free"]] if qc2 == 0 else [],
                         inc=s_av if qc2 == 1 else None,
                         fence=(qc2 == 1 and kt == ktmax))
                av_f = r or av_f
            hs["pt_free"][kt % 2] = av_f
            return av_f

        exv = {}
        qk0 = emit_qk(0)
        z0_f = emit_z(0, qk0)
        exv[0] = emit_tanh_exp(0, z0_f)
        if DEBUG_L1A and seq == 0 and h == 0 and qcp == 0:
            sd = B.sem("dbga")
            P.dma(lambda e: e.dma_start(out=B.dbg_z, in_=zb[0][:, :]),
                  wait=[z0_f], inc=sd)
            P.dma(lambda e: e.dma_start(out=B.dbg_p, in_=ptsb[0][:, :]),
                  wait=[exv[0]], inc=sd)
            P.dma(lambda e: e.dma_start(out=B.dbg_sh,
                                        in_=striph[:, 2047:2047 + 1024]),
                  wait=[sp_f], inc=sd)
            stc = P.on("vector", lambda e: e.tensor_copy(out=tb[1][:, :],
                                                         in_=pA[:, :]),
                       wait=[qk0], inc=B.sem("ce"), drain=True)
            hs["st_free"][0] = stc
            P.dma(lambda e: e.dma_start(out=B.dbg_st, in_=tb[1][:, :]),
                  wait=[stc], inc=sd)
            hs["tb_free"][1] = (B.sems["dbga"], B.sems["dbga"].n)
            hs["zb_free"][0] = (B.sems["dbga"], B.sems["dbga"].n)
        for kt in range(1, ktmax + 1):
            qkk = emit_qk(kt)
            emit_av(kt - 1, exv[kt - 1])
            exv[kt] = emit_tanh_exp(kt, emit_z(kt, qkk))
        av_last = emit_av(ktmax, exv[ktmax])

        P.on("vector", lambda e: e.tensor_copy(out=bcsb[0:1, :],
                                               in_=pC[64:65, :]),
             wait=[av_last, hs["bc_free"]])
        rc_f = P.on("vector", lambda e: e.reciprocal(
            out=bcsb[0:1, :], in_=bcsb[0:1, :]), inc=s_ce, drain=True)
        bc_f = None
        for qc2 in range(2):
            r = P.on("tensor", lambda e, qc2=qc2: e.matmul(
                pA[0:64, qc2 * 512:qc2 * 512 + 512], ones_row[0:1, 0:64],
                bcsb[0:1, qc2 * 512:qc2 * 512 + 512], start=True, stop=True),
                wait=[rc_f, hs["st_free"][0]] if qc2 == 0 else [],
                inc=s_qk if qc2 == 1 else None, fence=(qc2 == 1))
            bc_f = r or bc_f
        cp_f = P.on("vector", lambda e: e.tensor_copy(
            out=bcsb[0:64, :], in_=pA[0:64, :]),
            wait=[bc_f], inc=s_ce)
        hs["st_free"][0] = cp_f
        hs["bc_free"] = (B.sems["qk2"], B.sems["qk2"].n)
        sc_f = P.on("vector", lambda e: e.tensor_tensor(
            out=ctxs[:, :], in0=pC[0:64, :], in1=bcsb[0:64, :], op=ALU.mult),
            wait=[hs["ctxs_free"]], inc=s_ce, drain=True)
        hs["ctx_free"] = sc_f
        hs["ctxs_free"] = P.dma(
            lambda e, c=c, pbase=pbase, q0=q0: e.dma_start(
                out=ctx_d[c, pbase:pbase + 64, q0:q0 + 1024], in_=ctxs[:, :]),
            wait=[sc_f], inc=s_cd)
    hs["strip_free"] = (B.sems["zz"], B.sems["zz"].n)


def run_l1a(x2, Wq, Wk, Wv, Wo, bo):
    cc = host_l1_consts()
    nc = build_l1a()
    x2 = np.ascontiguousarray(np.asarray(x2, np.float32).reshape(NCORE, BPC, S, D))
    boT = np.ascontiguousarray(np.asarray(bo, np.float32).reshape(NCH, 128).T)
    base = dict(
        wqT=np.ascontiguousarray(np.asarray(Wq, np.float32).T),
        wkT=np.ascontiguousarray(np.asarray(Wk, np.float32).T),
        wvT=np.ascontiguousarray(np.asarray(Wv, np.float32).T),
        woT=np.ascontiguousarray(np.asarray(Wo, np.float32).T),
        boT=boT, identf=cc["identf"], stripb=cc["stripb"], cinf=cc["cinf"])
    in_maps = [dict(base, x2=x2[i]) for i in range(NCORE)]
    res = run_bass_kernel_spmd(nc, in_maps, core_ids=list(range(NCORE)))
    out = np.concatenate([r["x3"] for r in res.results], axis=0)
    return out.reshape(16, S, D)



AUXN = 4096


def build_l1f(B=None):
    standalone = B is None
    if standalone:
        B = Builder()
    nc, P = B.nc, B.P
    x3 = nc.dram_tensor("x3", [BPC, S, D], F32, kind="ExternalInput").ap()
    qpT_d = nc.dram_tensor("qpT", [D, 128], F32, kind="ExternalInput").ap()
    rwT_d = nc.dram_tensor("rwT", [128, 48], F32, kind="ExternalInput").ap()
    akT_d = nc.dram_tensor("akT", [128, AUXN], F32, kind="ExternalInput").ap()
    av_d = nc.dram_tensor("av", [AUXN, D], F32, kind="ExternalInput").ap()
    rmask_d = nc.dram_tensor("rmask", [1, AUXN], F32, kind="ExternalInput").ap()
    uw1T_d = nc.dram_tensor("uw1T", [D, 256], F32, kind="ExternalInput").ap()
    b1uT_d = nc.dram_tensor("b1uT", [128, 2], F32, kind="ExternalInput").ap()
    uw2T_d = nc.dram_tensor("uw2T", [128, 2], F32, kind="ExternalInput").ap()
    scal_d = nc.dram_tensor("scal", [1, 4], F32, kind="ExternalInput").ap()
    identf_d = nc.dram_tensor("identf", [128, 128], F32, kind="ExternalInput").ap()
    identb_d = nc.dram_tensor("identb", [128, 128], BF16, kind="ExternalInput").ap()
    out_d = nc.dram_tensor("out", [BPC, S, D], F32, kind="ExternalOutput").ap()
    ccin_t = nc.dram_tensor("ccin", [1, 128], F32)
    ccout_t = nc.dram_tensor("ccout", [1, 128], F32)
    ccin_d = ccin_t.ap()
    ccout_d = ccout_t.ap()

    s_ld = B.sem("ld")
    s_mm = B.sem("mm")
    s_ev = B.sem("ev")
    s_vv = B.sem("vv")
    s_cc = B.sem("cc")

    identf = B.sb("sidf", [128, 128], F32)
    identb = B.sb("sidb", [128, 128], BF16)
    ones_col = B.sb("ones_col", [128, 1], F32)
    ones_row = B.sb("ones_row", [1, 128], F32)
    avb = B.sb("avb", [128, (AUXN // 128) * D], BF16)
    rkm = B.sb("rkm", [49, AUXN], F32)
    qpw = B.sb("qpw", [128, NCH * 128], F32)
    rwb = B.sb("rwb", [128, 48], F32)
    uw1b = B.sb("uw1b", [128, NCH * 256], BF16)
    uw2b = B.sb("uw2b", [128, 2], BF16)
    b1u = B.sb("b1u", [128, 2], F32)
    scal = B.sb("scalb", [1, 8], F32)
    lvall = B.sb("lvall", [1, BPC * S], F32)
    x3t = B.sb("x3t", [128, NCH * 128], F32)
    x3b = B.sb("x3b", [128, NCH * 128], BF16)
    qb = B.sb("qb", [128, 128], F32)
    rqa = B.sb("rqa", [49, 128], F32)
    ssb = B.sb("ssb", [128, AUXN], F32)
    m8 = B.sb("m8", [128, 8], F32)
    e8 = B.sb("e8", [128, 16], F32)
    mneg = B.sb("mneg", [128, 2], F32)
    eb = B.sb("ebb", [128, AUXN], BF16)
    msk = B.sb("msk", [128, AUXN], BF16)
    ptb = B.sb("ptb", [128, AUXN], BF16)
    h1b = B.sb("h1b", [128, 256], BF16)
    rws = B.sb("rws", [1, 1024 + 1536], F32)
    gbs = B.sb("gbs", [128, 128], F32)
    tmp2 = B.sb("tmp2", [128, 512], F32)
    sepb = B.sb("sepb", [128, 512], F32)
    sq1 = B.sb("sq1", [128, 512], F32)
    stage = [B.sb(f"stg{i}", [128, D], F32) for i in range(2)]
    ostage = [B.sb(f"ost{i}", [128, D], F32) for i in range(2)]

    pA = B.ps("pA", [128, 1024], F32)
    pB = B.ps("pB", [128, 1024], F32)
    pC = B.ps("pC", [128, 1024], F32)
    pT0 = B.ps("pT0", [128, 1024], BF16)
    pF = B.ps("pF", [128, 512], F32)
    B.make_fence(pF, ones_col, identf)

    P.dma(lambda e: e.dma_start(out=identf[:, :], in_=identf_d), inc=s_ld)
    P.dma(lambda e: e.dma_start(out=identb[:, :], in_=identb_d), inc=s_ld)
    P.dma(lambda e: e.dma_start(out=rwb[:, :], in_=rwT_d), inc=s_ld)
    P.dma(lambda e: e.dma_start(out=b1u[:, :], in_=b1uT_d), inc=s_ld)
    P.dma(lambda e: e.dma_start(out=scal[0:1, 0:4], in_=scal_d), inc=s_ld)
    P.dma(lambda e: e.dma_start(out=rkm[48:49, :], in_=rmask_d), inc=s_ld)
    P.dma(lambda e: e.dma_start(
        out=qpw[:, :].rearrange("p (c j) -> p c j", j=128),
        in_=qpT_d.rearrange("(c p) j -> p c j", p=128)), inc=s_ld)
    P.dma(lambda e: e.dma_start(out=stage[1][:, 0:2], in_=uw2T_d), inc=s_ld)
    ak_f = P.dma(lambda e: e.dma_start(out=ssb[:, :], in_=akT_d), inc=s_ld)
    ld0 = (s_ld, s_ld.n)
    P.on("vector", lambda e: e.memset(ones_col[:, :], 1.0))
    P.on("vector", lambda e: e.memset(ones_row[:, :], 1.0))
    P.on("vector", lambda e: e.memset(rqa[:, :], 1.0))
    P.on("vector", lambda e: e.tensor_copy(out=uw2b[:, :], in_=stage[1][:, 0:2]),
         wait=[ld0], inc=s_vv, drain=True)
    B.make_bias_consts([LN_EPS, 1.0, 1e-6])
    st0f = (s_vv, s_vv.n)
    B.state[("stage", id(stage[0]))] = [st0f, st0f]
    emit_wstream(B, uw1T_d, uw1b, 0, NCH, 256, stage, FREE0)
    emit_wstream(B, av_d, avb, 0, AUXN // 128, D, stage, FREE0)
    rk_ev = FREE0
    for half in range(NCH):
        prev = B.state.get(("rk_ev", half % 2), FREE0)
        r = P.on("tensor", lambda e, half=half: e.matmul(
            pA[0:48, (half % 2) * 512:(half % 2) * 512 + 512],
            rwb[:, :], ssb[:, half * 512:(half + 1) * 512],
            start=True, stop=True),
            wait=[ld0, prev], inc=s_mm, fence=True)
        rk_ev = P.on("scalar", lambda e, half=half: e.activation(
            out=rkm[0:48, half * 512:(half + 1) * 512],
            in_=pA[0:48, (half % 2) * 512:(half % 2) * 512 + 512],
            func=AF.Copy),
            wait=[r], inc=s_ev, drain=(half == NCH - 1))
        B.state[("rk_ev", half % 2)] = rk_ev
    P.block("f_consts")

    x3tv = x3t[:, :].rearrange("p (c j) -> p c j", j=128)
    sqt = tmp2[:, 0:128]
    s_rw = B.sem("rw")
    for gt in range(BPC * NTT):
        seq, tt = gt // NTT, gt % NTT
        x_f = emit_xtiles(B, x3[seq][tt * 128:(tt + 1) * 128, :], 1,
                          lambda _: x3tv, stage, pA, pB, identf,
                          dst_free=(s_mm, B.state.get("lv_mm", 0)))
        for c in range(NCH):
            P.on("tensor", lambda e, c=c: e.matmul(
                pC[0:1, 0:128], ones_col[:, :], x3tv[:, c, :],
                start=(c == 0), stop=(c == NCH - 1)),
                wait=[x_f, (s_rw, B.state.get("lv_rw", 0))] if c == 0 else [])
        sq_f = P.on("scalar", lambda e: e.activation(
            out=sq1[:, 0:512].rearrange("p (c j) -> p c j", j=128),
            in_=x3tv[:, 0:4, :], func=AF.Square),
            wait=[x_f, (s_mm, B.state.get("lv_mm", 0))])
        sq_f = P.on("scalar", lambda e: e.activation(
            out=sepb[:, 0:512].rearrange("p (c j) -> p c j", j=128),
            in_=x3tv[:, 4:8, :], func=AF.Square),
            inc=s_rw, drain=True)
        for c in range(NCH):
            src = sq1 if c < 4 else sepb
            P.on("tensor", lambda e, c=c, src=src: e.matmul(
                pC[0:1, 512:640], ones_col[:, :],
                src[:, (c % 4) * 128:(c % 4) * 128 + 128],
                start=(c == 0), stop=(c == NCH - 1)),
                wait=[sq_f] if c == 0 else [],
                inc=s_mm if c == NCH - 1 else None,
                fence=(c == NCH - 1))
        B.state["lv_mm"] = s_mm.n
        mm_f = (s_mm, s_mm.n)
        P.on("scalar", lambda e: e.activation(
            out=rws[0:1, 0:128], in_=pC[0:1, 0:128], func=AF.Copy, scale=1.0 / D),
            wait=[mm_f])
        r2 = P.on("scalar", lambda e: e.activation(
            out=rws[0:1, 128:256], in_=pC[0:1, 512:640], func=AF.Copy,
            scale=1.0 / D), inc=s_rw, drain=True)
        B.state["lv_rw"] = s_rw.n
        P.on("vector", lambda e: e.tensor_tensor(
            out=rws[0:1, 256:384], in0=rws[0:1, 0:128], in1=rws[0:1, 0:128],
            op=ALU.mult), wait=[r2])
        v_f = P.on("vector", lambda e: e.tensor_tensor(
            out=rws[0:1, 128:256], in0=rws[0:1, 128:256], in1=rws[0:1, 256:384],
            op=ALU.subtract), inc=s_vv, drain=True)
        P.on("scalar", lambda e, gt=gt: e.activation(
            out=lvall[0:1, gt * 128:(gt + 1) * 128], in_=rws[0:1, 128:256],
            func=AF.Ln, bias=B.bias(1.0)[0:1, :]),
            wait=[v_f])
    P.block("f_lv")
    ps_f = P.on("vector", lambda e: e.tensor_reduce(
        rws[0:1, 512:513], lvall[0:1, :], axis=mybir.AxisListType.X, op=ALU.add),
        inc=s_vv)
    P.on("vector", lambda e: e.memset(rws[0:1, 513:640], 0.0),
         inc=s_vv, drain=True)
    ps2 = (s_vv, s_vv.n)
    dma_f = P.dma(lambda e: e.dma_start(out=ccin_d, in_=rws[0:1, 512:640]),
                  wait=[ps2], inc=s_ld)
    cc_f = P.on("gpsimd", lambda e: e.collective_compute(
        "AllReduce", ALU.add, replica_groups=[list(range(NCORE))],
        ins=[ccin_t.ap().opt()], outs=[ccout_t.ap().opt()]),
        wait=[dma_f], inc=s_cc, inc_n=1)
    gm_f = P.dma(lambda e: e.dma_start(out=rws[0:1, 640:768], in_=ccout_d),
                 wait=[cc_f], inc=s_ld)
    gm2 = P.on("scalar", lambda e: e.activation(
        out=scal[0:1, 4:5], in_=rws[0:1, 640:641], func=AF.Copy,
        scale=1.0 / 32768.0), wait=[gm_f], inc=s_ev, drain=True)
    P.on("vector", lambda e: e.tensor_scalar_add(scal[0:1, 4:5],
                                                 scal[0:1, 4:5], 1e-6),
         wait=[gm2])
    P.on("vector", lambda e: e.reciprocal(out=scal[0:1, 4:5],
                                          in_=scal[0:1, 4:5]))
    P.on("vector", lambda e: e.tensor_scalar_mul(scal[0:1, 5:6],
                                                 scal[0:1, 4:5], 0.5),
         inc=s_vv, drain=True)
    P.block("f_gm")

    for gt in range(BPC * NTT):
        _l1f_tile(B, gt, x3, out_d, identf, identb, ones_col, ones_row,
                  avb, rkm, qpw, rwb, uw1b, uw2b, b1u, scal, lvall,
                  x3t, x3b, qb, rqa, ssb, m8, e8, mneg, eb, msk, ptb, h1b,
                  rws, gbs, tmp2, sq1, stage, ostage,
                  pA, pB, pC, pT0, pF)
    stp = B.dsem("o")
    for sm in stp:
        P.on("sync", lambda e, sm=sm, v=sm.n: e.wait_ge(sm.h, v))
    P.block("f_done")
    return B.finish() if standalone else None


def _l1f_tile(B, gt, x3, out_d, identf, identb, ones_col, ones_row,
              avb, rkm, qpw, rwb, uw1b, uw2b, b1u, scal, lvall,
              x3t, x3b, qb, rqa, ssb, m8, e8, mneg, eb, msk, ptb, h1b,
              rws, gbs, tmp2, sq1, stage, ostage,
              pA, pB, pC, pT0, pF):
    nc, P = B.nc, B.P
    seq, tt = gt // NTT, gt % NTT
    s_mm, s_ev, s_vv = B.sem("mm2"), B.sem("ev2"), B.sem("vv2")
    stp = B.dsem("o")
    st8 = B.state

    x3tv = x3t[:, :].rearrange("p (c j) -> p c j", j=128)
    x_f = emit_xtiles(B, x3[seq][tt * 128:(tt + 1) * 128, :], 1,
                      lambda _: x3tv, stage, pA, pB, identf,
                      dst_free=st8.get("f_x3t", FREE0))
    xb_f = P.on("vector", lambda e: e.tensor_copy(out=x3b[:, :], in_=x3t[:, :]),
                wait=[x_f, st8.get("f_x3b", FREE0)], inc=s_vv, drain=True)

    for kc in range(NCH):
        P.on("tensor", lambda e, kc=kc: e.matmul(
            pC[:, 0:128], qpw[:, kc * 128:(kc + 1) * 128],
            x3tv[:, kc, :], start=(kc == 0), stop=(kc == NCH - 1)),
            wait=[x_f, st8.get("f_pCq", FREE0)] if kc == 0 else [],
            inc=s_mm if kc == NCH - 1 else None, fence=(kc == NCH - 1))
    q_mm = (s_mm, s_mm.n)
    qb_f = P.on("scalar", lambda e: e.activation(
        out=qb[:, :], in_=pC[:, 0:128], func=AF.Copy),
        wait=[q_mm], inc=s_ev, drain=True)
    rq_mm = P.on("tensor", lambda e: e.matmul(
        pC[0:48, 128:256], rwb[:, :], qb[:, :], start=True, stop=True),
        wait=[qb_f], inc=s_mm, fence=True)
    rq_f = P.on("scalar", lambda e: e.activation(
        out=rqa[0:48, :], in_=pC[0:48, 128:256], func=AF.Copy,
        scale=1.0 / (48.0 ** 0.5)),
        wait=[rq_mm], inc=s_ev, drain=True)
    sc_last = None
    for i in range(NCH):
        r = P.on("tensor", lambda e, i=i: e.matmul(
            pA[:, (i % 2) * 512:(i % 2) * 512 + 512],
            rqa[:, :], rkm[:, i * 512:(i + 1) * 512],
            start=True, stop=True),
            wait=[rq_f, st8.get(("f_pAs", i % 2), FREE0)],
            inc=s_mm, fence=True)
        sc_last = P.on("scalar", lambda e, i=i: e.activation(
            out=ssb[:, i * 512:(i + 1) * 512],
            in_=pA[:, (i % 2) * 512:(i % 2) * 512 + 512], func=AF.Copy),
            wait=[r, st8.get("f_ssb", FREE0) if i == 0 else FREE0],
            inc=s_ev, drain=(i == NCH - 1))
        st8[("f_pAs", i % 2)] = sc_last
    mx_f = P.on("vector", lambda e: e.max(out=m8[:, :], in_=ssb[:, :]),
                wait=[sc_last, st8.get("f_m8", FREE0)], inc=s_vv, drain=True)
    mn_f = P.on("vector", lambda e: e.tensor_scalar_mul(
        mneg[:, 0:1], m8[:, 0:1], -1.0), wait=[mx_f], inc=s_vv, drain=True)
    mk_f = P.on("vector", lambda e: e.tensor_scalar(
        out=msk[:, :], in0=ssb[:, :], scalar1=m8[:, 7:8], scalar2=None,
        op0=ALU.is_ge),
        wait=[st8.get("f_msk", FREE0)], inc=s_vv)
    eb_f = P.on("scalar", lambda e: e.activation(
        out=eb[:, :], in_=ssb[:, :], func=AF.Exp, bias=mneg[:, 0:1]),
        wait=[mn_f, st8.get("f_eb", FREE0)], inc=s_ev, drain=True)
    p_f = P.on("vector", lambda e: e.tensor_tensor(
        out=msk[:, :], in0=eb[:, :], in1=msk[:, :], op=ALU.mult),
        wait=[eb_f], inc=s_vv, drain=True)
    st8["f_ssb"] = p_f
    st8["f_eb"] = p_f
    e8_f = P.on("scalar", lambda e: e.activation(
        out=e8[:, 0:8], in_=m8[:, 0:8], func=AF.Exp, bias=mneg[:, 0:1]),
        inc=s_ev, drain=True)
    st8["f_m8"] = e8_f
    den_f = P.on("vector", lambda e: e.tensor_reduce(
        e8[:, 8:9], e8[:, 0:8], axis=mybir.AxisListType.X, op=ALU.add),
        wait=[e8_f], inc=s_vv, drain=True)
    ev_prev = st8.get("f_ptb", FREE0)
    for rnd in range(4):
        tp_f = None
        for j in range(8):
            r = P.on("tensor", lambda e, rnd=rnd, j=j: e.transpose(
                pT0[:, j * 128:(j + 1) * 128],
                msk[:, (rnd * 8 + j) * 128:(rnd * 8 + j) * 128 + 128],
                identb[:, :]),
                wait=[p_f, ev_prev] if j == 0 else [],
                inc=s_mm if j == 7 else None, fence=(j == 7))
            tp_f = r or tp_f
        ev_prev = P.on("vector", lambda e, rnd=rnd: e.tensor_copy(
            out=ptb[:, rnd * 1024:(rnd + 1) * 1024], in_=pT0[:, :]),
            wait=[tp_f], inc=s_vv, drain=True)
    ptb_f = ev_prev
    st8["f_msk"] = ptb_f
    for cm in range(NCH):
        for ic in range(AUXN // 128):
            P.on("tensor", lambda e, cm=cm, ic=ic: e.matmul(
                pB[:, cm * 128:(cm + 1) * 128],
                avb[:, ic * D + cm * 128:ic * D + cm * 128 + 128],
                ptb[:, ic * 128:(ic + 1) * 128],
                start=(ic == 0), stop=(ic == AUXN // 128 - 1)),
                wait=[ptb_f, st8.get("f_pB", FREE0)]
                if ic == 0 and cm == 0 else [],
                inc=s_mm if (ic == AUXN // 128 - 1 and cm == NCH - 1) else None,
                fence=(ic == AUXN // 128 - 1 and cm == NCH - 1))
    agg_mm = (s_mm, s_mm.n)
    st8["f_ptb"] = agg_mm
    for fc in range(2):
        for kc in range(NCH):
            P.on("tensor", lambda e, fc=fc, kc=kc: e.matmul(
                pC[:, 256 + fc * 128:256 + fc * 128 + 128],
                uw1b[:, kc * 256 + fc * 128:kc * 256 + fc * 128 + 128],
                x3b[:, kc * 128:(kc + 1) * 128],
                start=(kc == 0), stop=(kc == NCH - 1)),
                wait=[xb_f, st8.get("f_pCh", FREE0)]
                if kc == 0 and fc == 0 else [],
                inc=s_mm if (kc == NCH - 1 and fc == 1) else None,
                fence=(kc == NCH - 1 and fc == 1))
    h_mm = (s_mm, s_mm.n)
    h_f = None
    for fc in range(2):
        h_f = P.on("scalar", lambda e, fc=fc: e.activation(
            out=h1b[:, fc * 128:(fc + 1) * 128],
            in_=pC[:, 256 + fc * 128:256 + fc * 128 + 128],
            func=AF.Gelu, bias=b1u[:, fc:fc + 1]),
            wait=[h_mm, st8.get("f_h1b", FREE0)] if fc == 0 else [],
            inc=s_ev, drain=(fc == 1))
    st8["f_pCh"] = h_f
    l_mm = None
    for fc in range(2):
        l_mm = P.on("tensor", lambda e, fc=fc: e.matmul(
            pC[0:1, 640:768], uw2b[:, fc:fc + 1],
            h1b[:, fc * 128:(fc + 1) * 128],
            start=(fc == 0), stop=(fc == 1)),
            wait=[h_f, st8.get("f_pCl", FREE0)] if fc == 0 else [],
            inc=s_mm if fc == 1 else None, fence=(fc == 1))
    st8["f_h1b"] = l_mm
    sg_f = P.on("scalar", lambda e: e.activation(
        out=rws[0:1, 640:768], in_=pC[0:1, 640:768], func=AF.Sigmoid,
        bias=scal[0:1, 0:1]),
        wait=[l_mm], inc=s_ev, drain=True)
    st8["f_pCl"] = sg_f
    t1_f = P.on("vector", lambda e: e.tensor_scalar_mul(
        rws[0:1, 768:896], rws[0:1, 640:768], 2.5), wait=[sg_f])
    nv_f = P.on("vector", lambda e, gt=gt: e.tensor_scalar(
        out=rws[0:1, 256:384], in0=lvall[0:1, gt * 128:(gt + 1) * 128],
        scalar1=scal[0:1, 5:6], scalar2=None, op0=ALU.mult))
    u_f = P.on("vector", lambda e: e.tensor_tensor(
        out=rws[0:1, 768:896], in0=rws[0:1, 768:896], in1=rws[0:1, 256:384],
        op=ALU.add))
    cl_f = P.on("vector", lambda e: e.tensor_scalar(
        out=rws[0:1, 768:896], in0=rws[0:1, 768:896], scalar1=5.0, scalar2=0.0,
        op0=ALU.min, op1=ALU.max), inc=s_vv, drain=True)
    g_f = P.on("scalar", lambda e: e.activation(
        out=rws[0:1, 896:1024], in_=rws[0:1, 768:896], func=AF.Sigmoid,
        scale=scal[0:1, 1:2], bias=scal[0:1, 2:3]),
        wait=[cl_f], inc=s_ev, drain=True)
    dt_f = P.on("tensor", lambda e: e.transpose(
        pC[0:1, 768:896], e8[:, 8:9], identf[:, :]),
        wait=[den_f, sg_f, st8.get("f_pCd", FREE0)], inc=s_mm, fence=True)
    dr_f = P.on("vector", lambda e: e.tensor_copy(
        out=rws[0:1, 384:512], in_=pC[0:1, 768:896]),
        wait=[dt_f], inc=s_vv)
    P.on("vector", lambda e: e.reciprocal(
        out=rws[0:1, 384:512], in_=rws[0:1, 384:512]))
    gr_f = P.on("vector", lambda e: e.tensor_tensor(
        out=rws[0:1, 384:512], in0=rws[0:1, 384:512], in1=rws[0:1, 896:1024],
        op=ALU.mult), wait=[g_f], inc=s_vv, drain=True)
    gb_mm = P.on("tensor", lambda e: e.matmul(
        pC[:, 896:1024], ones_row[0:1, :], rws[0:1, 384:512],
        start=True, stop=True),
        wait=[gr_f], inc=s_mm, fence=True)
    gb_f = P.on("vector", lambda e: e.tensor_copy(
        out=gbs[:, :], in_=pC[:, 896:1024]),
        wait=[gb_mm], inc=s_vv, drain=True)
    st8["f_pCd"] = gb_f
    cb_f = None
    for c in range(NCH):
        P.on("vector", lambda e, c=c: e.tensor_tensor(
            out=tmp2[:, 0:128], in0=pB[:, c * 128:(c + 1) * 128], in1=gbs[:, :],
            op=ALU.mult),
            wait=[agg_mm, gb_f] if c == 0 else [])
        cb_f = P.on("vector", lambda e, c=c: e.scalar_tensor_tensor(
            out=x3t[:, c * 128:(c + 1) * 128], in0=x3t[:, c * 128:(c + 1) * 128],
            scalar=2.0, in1=tmp2[:, 0:128], op0=ALU.mult, op1=ALU.add),
            inc=s_vv if c == NCH - 1 else None, drain=(c == NCH - 1))
    st8["f_pB"] = cb_f
    emit_postln_inplace(B, x3t, pA, pC, [sq1[:, 0:128], sq1[:, 128:256]],
                        tmp2[:, 256:384], rws[0:1, 1024:2560],
                        ones_col, ones_row, width=128,
                        xfm_ready=cb_f, final_drain=True)
    s_ap = B.sem("ap")
    ln_f = (s_ap, s_ap.n)
    st8["f_pCq"] = (B.sem("rw"), B.sem("rw").n)
    ot = ostage[gt % 2]
    tp_f = None
    for c in range(NCH):
        r = P.on("tensor", lambda e, c=c: e.transpose(
            pA[:, c * 128:(c + 1) * 128],
            x3t[:, c * 128:(c + 1) * 128], identf[:, :]),
            wait=[ln_f, st8.get("f_pAo", FREE0)] if c == 0 else [],
            inc=s_mm if c == NCH - 1 else None, fence=(c == NCH - 1))
        tp_f = r or tp_f
    st8["f_x3t"] = tp_f
    ev_f = P.on("scalar", lambda e, ot=ot: e.activation(
        out=ot[:, :].rearrange("p (c j) -> p c j", c=NCH),
        in_=pA[:, :].rearrange("p (c j) -> p c j", c=NCH),
        func=AF.Copy),
        wait=[tp_f, st8.get(("f_ost", gt % 2), FREE0)], inc=s_ev, drain=True)
    st8["f_pAo"] = ev_f
    st8[("f_pAs", 0)] = ev_f
    st8[("f_pAs", 1)] = ev_f
    B.state.setdefault(("xpsum", id(pA)), [FREE0, FREE0])[0] = ev_f
    st8[("f_ost", gt % 2)] = P.dma(
        lambda e, ot=ot, seq=seq, tt=tt: e.dma_start(
            out=out_d[seq][tt * 128:(tt + 1) * 128, :], in_=ot[:, :]),
        wait=[ev_f], inc=stp[gt % 2])
    st8["f_x3b"] = l_mm


def run_l1f(x3, q_proj_w, router_w, aux_keys, aux_values, reliability_mask,
            unc_w1, unc_b1, unc_w2, unc_b2, gate_w1, gate_bias):
    cc = host_constants()
    nc = build_l1f()
    x3 = np.ascontiguousarray(np.asarray(x3, np.float32).reshape(NCORE, BPC, S, D))
    scal = np.zeros((1, 4), np.float32)
    scal[0, 0] = np.float32(np.asarray(unc_b2).reshape(-1)[0])
    scal[0, 1] = np.float32(gate_w1)
    scal[0, 2] = np.float32(gate_bias)
    base = dict(
        qpT=np.ascontiguousarray(np.asarray(q_proj_w, np.float32).T),
        rwT=np.ascontiguousarray(np.asarray(router_w, np.float32).T),
        akT=np.ascontiguousarray(np.asarray(aux_keys, np.float32).T),
        av=np.ascontiguousarray(np.asarray(aux_values, np.float32)),
        rmask=np.ascontiguousarray(np.asarray(reliability_mask,
                                              np.float32).reshape(1, AUXN)),
        uw1T=np.ascontiguousarray(np.asarray(unc_w1, np.float32).T),
        b1uT=np.ascontiguousarray(np.asarray(unc_b1, np.float32)
                                  .reshape(2, 128).T),
        uw2T=np.ascontiguousarray(np.asarray(unc_w2, np.float32)
                                  .reshape(2, 128).T),
        scal=scal, identf=cc["identf"], identb=cc["identb"])
    in_maps = [dict(base, x3=x3[i]) for i in range(NCORE)]
    res = run_bass_kernel_spmd(nc, in_maps, core_ids=list(range(NCORE)))
    out = np.concatenate([r["out"] for r in res.results], axis=0)
    return out.reshape(16, S, D)
